# version 08
# baseline (speedup 1.0000x reference)
import contextlib

import numpy as np
import ml_dtypes

import concourse.bass as bass
import concourse.mybir as mybir
from concourse.bass_utils import run_bass_kernel_spmd

F32 = mybir.dt.float32
BF16 = mybir.dt.bfloat16
U8 = mybir.dt.uint8
AF = mybir.ActivationFunctionType
OP = mybir.AluOpType
AXX = mybir.AxisListType.X

NCORE = 8
NSEQ = 2
N = 2048
D = 1024
H = 4
DVH = 512
DV = H * DVH
C = 128
NCH = N // C
NTOT = NSEQ * NCH
TOK = NSEQ * N

LN16 = float(np.log(16.0))
EPS = 1e-5
CLAMP_E = 1e15

DEBUG = False


class Book:
    def __init__(self, nc, ctx):
        self.nc, self.ctx = nc, ctx
        self.sems = {}
        self.counts = {}

    @staticmethod
    def _key(name):
        if name.startswith('d_'):
            return name
        if name[0] == 't':
            return 'ET'
        if name[0] == 'v':
            return 'EV'
        return 'ES'

    def get(self, name):
        key = self._key(name)
        if key not in self.sems:
            self.sems[key] = self.ctx.enter_context(self.nc.semaphore(key))
            self.counts[key] = 0
        return self.sems[key]

    def tok(self, name, k=1):
        self.get(name)
        key = self._key(name)
        self.counts[key] += k
        return (key, self.counts[key], k)


class Prog:

    def __init__(self, book):
        self.book = book
        self.stages = []

    def add(self, eng, emit, waits=(), inc=None, inc_k=1, multi_dma=False):
        tok = self.book.tok(inc, inc_k) if inc is not None else None
        ws = [w for w in waits if w is not None]
        self.stages.append((eng, emit, ws, tok, multi_dma))
        return tok

    def emit_engine(self, eng_name, eng):
        book = self.book
        for e, emit, waits, tok, multi_dma in self.stages:
            if e != eng_name:
                continue
            for w in waits:
                eng.wait_ge(book.sems[w[0]], w[1])
            if multi_dma:
                emit(eng, book.sems[tok[0]])
            else:
                last = emit(eng)
                if tok is not None:
                    assert last is not None
                    last.then_inc(book.sems[tok[0]], tok[2])

    def run_block(self, nc):
        with nc.Block() as blk:
            @blk.tensor
            def _(t):
                self.emit_engine('T', t)

            @blk.vector
            def _(v):
                self.emit_engine('V', v)

            @blk.scalar
            def _(s):
                self.emit_engine('S', s)

            @blk.gpsimd
            def _(g):
                self.emit_engine('G', g)

            @blk.sync
            def _(y):
                self.emit_engine('Y', y)


def tr8(e, pt, src, identb):
    last = None
    for j in range(8):
        last = e.transpose(pt[:, 128 * j:128 * (j + 1)],
                           src[:, 128 * j:128 * (j + 1)], identb[:])
    return last


def build_nc(debug=False):
    nc = bass.Bass("TRN2", target_bir_lowering=False)

    xin = nc.dram_tensor("xin", [TOK, D], F32, kind="ExternalInput").ap()
    out_d = nc.dram_tensor("out", [TOK, D], F32, kind="ExternalOutput").ap()
    o_scr = nc.dram_tensor("o_scr", [TOK, DV], BF16,
                           kind="ExternalOutput" if debug else "Internal").ap()
    wkT_scr = nc.dram_tensor("wkT_scr", [128, 8 * D], BF16).ap()
    wvT_scr = nc.dram_tensor("wvT_scr", [128, 8 * D], BF16).ap()

    wq_d = nc.dram_tensor("wq", [D, D], F32, kind="ExternalInput").ap()
    wk_d = nc.dram_tensor("wk", [D, D], F32, kind="ExternalInput").ap()
    wv_d = nc.dram_tensor("wv", [D, DV], F32, kind="ExternalInput").ap()
    wg_d = nc.dram_tensor("wg", [D, DV], F32, kind="ExternalInput").ap()
    wgk1_d = nc.dram_tensor("wgk1", [D, 16], F32, kind="ExternalInput").ap()
    wgk2_d = nc.dram_tensor("wgk2", [16, D], F32, kind="ExternalInput").ap()
    bgk2_d = nc.dram_tensor("bgk2", [1, D], F32, kind="ExternalInput").ap()
    wo_d = nc.dram_tensor("wo", [DV, D], F32, kind="ExternalInput").ap()
    gnw_d = nc.dram_tensor("gnw", [1, DVH], F32, kind="ExternalInput").ap()
    wr_d = nc.dram_tensor("wr", [D, D], F32, kind="ExternalInput").ap()
    wk2_d = nc.dram_tensor("wk2", [D, D], F32, kind="ExternalInput").ap()
    wv2_d = nc.dram_tensor("wv2", [D, D], F32, kind="ExternalInput").ap()
    mur_d = nc.dram_tensor("mur", [8, 128], F32, kind="ExternalInput").ap()
    muk_d = nc.dram_tensor("muk", [8, 128], F32, kind="ExternalInput").ap()
    cmod_d = nc.dram_tensor("cmod", [NSEQ, D], F32, kind="ExternalInput").ap()
    wmod_d = nc.dram_tensor("wmod", [D, 6 * D], F32, kind="ExternalInput").ap()

    identb_d = nc.dram_tensor("identb", [128, 128], BF16, kind="ExternalInput").ap()
    identf_d = nc.dram_tensor("identf", [128, 128], F32, kind="ExternalInput").ap()
    utri_d = nc.dram_tensor("utri", [128, 128], F32, kind="ExternalInput").ap()
    mtri_d = nc.dram_tensor("mtri", [128, 128], F32, kind="ExternalInput").ap()
    cmask_d = nc.dram_tensor("cmask", [128, 128], U8, kind="ExternalInput").ap()
    hones_d = nc.dram_tensor("hones", [128, 3], F32, kind="ExternalInput").ap()

    dbg = {}
    if debug:
        dbg['h'] = nc.dram_tensor("dbg_h", [TOK, D], BF16, kind="ExternalOutput").ap()
        dbg['sp'] = nc.dram_tensor("dbg_sp", [TOK, D], F32, kind="ExternalOutput").ap()
        dbg['mod'] = nc.dram_tensor("dbg_mod", [NSEQ, 6 * D], F32, kind="ExternalOutput").ap()
        for nm, w in [('eA', D), ('eK', D), ('qh', D), ('kh', D), ('qhfm', D),
                      ('qinfm', D), ('kinfm', D), ('Pm', 512)]:
            dbg[nm] = nc.dram_tensor("dbg_" + nm, [128, w], BF16, kind="ExternalOutput").ap()
        dbg['colv'] = nc.dram_tensor("dbg_colv", [128, 48], F32, kind="ExternalOutput").ap()

    with contextlib.ExitStack() as CTX:
        E = CTX.enter_context
        book = Book(nc, CTX)

        wg_s = E(nc.sbuf_tensor("wg_s", [128, 8 * DV], BF16))
        wrT_s = E(nc.sbuf_tensor("wrT_s", [128, 8 * D], BF16))
        identb = E(nc.sbuf_tensor("identb_s", [128, 128], BF16))
        identf = E(nc.sbuf_tensor("identf_s", [128, 128], F32))
        utri = E(nc.sbuf_tensor("utri_s", [128, 128], F32))
        mtri = E(nc.sbuf_tensor("mtri_s", [128, 128], F32))
        cmask = E(nc.sbuf_tensor("cmask_s", [128, 128], U8))
        zeros = E(nc.sbuf_tensor("zeros", [128, 128], F32))
        onesrow = E(nc.sbuf_tensor("onesrow", [1, 128], F32))
        onescol = E(nc.sbuf_tensor("onescol", [128, 1], F32))
        hones = E(nc.sbuf_tensor("hones_s", [128, 3], F32))
        cpool = E(nc.sbuf_tensor("cpool", [128, 10], F32))
        (C_NEG1, C_NLN16, C_NI16, C_PI16, C_NHALF, C_I512, C_EPS, C_I1024,
         C_ONE, C_SPCL) = range(10)
        CPV = [-1.0, -LN16, -1.0 / 16, 1.0 / 16, -0.5, 1.0 / 512, EPS,
               1.0 / 1024, 1.0, 80.0]

        def cp(i):
            return cpool[:, i:i + 1]

        mod_fm = E(nc.sbuf_tensor("mod_fm", [128, 96], F32))

        def modcol(v, dj, b):
            i = (v * 8 + dj) * 2 + b
            return mod_fm[:, i:i + 1]

        murcol = E(nc.sbuf_tensor("murcol", [128, 8], F32))
        mukcol = E(nc.sbuf_tensor("mukcol", [128, 8], F32))
        omrcol = E(nc.sbuf_tensor("omrcol", [128, 8], F32))
        omkcol = E(nc.sbuf_tensor("omkcol", [128, 8], F32))
        gnw_rep = E(nc.sbuf_tensor("gnw_rep", [128, DV], BF16))
        ga_rep = E(nc.sbuf_tensor("ga_rep", [128, 2 * D], BF16))
        gb_rep = E(nc.sbuf_tensor("gb_rep", [128, 2 * D], BF16))
        S_bf = E(nc.sbuf_tensor("S_bf", [128, H * 2 * DVH], BF16))

        def S_tile(h, j):
            o = (2 * h + j) * DVH
            return S_bf[:, o:o + DVH]

        def wsrc(dr, F, nch=8):
            return bass.AP(dr.tensor, 0, [[F, 128], [128 * F, nch], [1, F]])

        def wdst(sb, F, nch=8):
            return bass.AP(sb, 0, [[nch * F, 128], [F, nch], [1, F]])

        bw = {}

        with contextlib.ExitStack() as SA:
            ES = SA.enter_context
            wq_s = ES(nc.sbuf_tensor("wq_s", [128, 8 * D], BF16))
            wk_s = ES(nc.sbuf_tensor("wk_s", [128, 8 * D], BF16))
            wv_s = ES(nc.sbuf_tensor("wv_s", [128, 8 * DV], BF16))
            wgk1_s = ES(nc.sbuf_tensor("wgk1_s", [128, 8 * 16], BF16))
            wgk2b = ES(nc.sbuf_tensor("wgk2b", [17, D], F32))
            z1ones = ES(nc.sbuf_tensor("z1ones", [17, 128], F32))

            with contextlib.ExitStack() as ST:
                EE = ST.enter_context
                wmod_bf = EE(nc.sbuf_tensor("wmod_bf", [128, 8 * D], BF16))
                cmod_s = EE(nc.sbuf_tensor("cmod_s", [NSEQ, D], F32))
                sc_bf = EE(nc.sbuf_tensor("sc_bf", [NSEQ, D], BF16))
                scfm = EE(nc.sbuf_tensor("scfm", [128, 16], BF16))
                modsb = EE(nc.sbuf_tensor("modsb", [NSEQ, 6 * D], F32))
                mtmp = EE(nc.sbuf_tensor("mtmp", [NSEQ, D], F32))
                mtmp2 = EE(nc.sbuf_tensor("mtmp2", [NSEQ, D], F32))
                murow = EE(nc.sbuf_tensor("murow", [8, 128], F32))
                mukrow = EE(nc.sbuf_tensor("mukrow", [8, 128], F32))
                gnwrow = EE(nc.sbuf_tensor("gnwrow", [1, DVH], F32))
                brow = EE(nc.sbuf_tensor("brow", [1, D], F32))

                pmod = EE(nc.psum_tensor("pmod", [NSEQ, 1024], F32))
                ptr0 = EE(nc.psum_tensor("ptr0", [128, 1024], BF16))
                ptf0 = EE(nc.psum_tensor("ptf0", [128, 512], F32))

                P = Prog(book)

                def dma(dst, src):
                    return lambda e: e.dma_start(dst, src)

                t_con = None
                for dst, src in [(identb[:], identb_d), (identf[:], identf_d),
                                 (utri[:], utri_d), (mtri[:], mtri_d),
                                 (cmask[:], cmask_d),
                                 (wgk2b[0:16, :], wgk2_d),
                                 (wgk2b[16:17, :], bgk2_d),
                                 (cmod_s[:], cmod_d),
                                 (murow[:], mur_d), (mukrow[:], muk_d),
                                 (gnwrow[:], gnw_d), (hones[:], hones_d)]:
                    t_con = P.add('Y', dma(dst, src), inc='d_st', inc_k=16)

                t_wq = P.add('G', dma(wdst(wq_s, D), wsrc(wq_d, D)), inc='d_w', inc_k=16)
                t_wk = P.add('G', dma(wdst(wk_s, D), wsrc(wk_d, D)), inc='d_w', inc_k=16)
                t_wv = P.add('G', dma(wdst(wv_s, DV), wsrc(wv_d, DV)), inc='d_w', inc_k=16)
                t_wgk1 = P.add('G', dma(wdst(wgk1_s, 16), wsrc(wgk1_d, 16)), inc='d_w', inc_k=16)
                t_wg = P.add('G', dma(wdst(wg_s, DV), wsrc(wg_d, DV)), inc='d_w', inc_k=16)
                bw['wg'] = t_wg

                def ms(e):
                    e.memset(zeros[:], 0.0)
                    e.memset(onesrow[:], 1.0)
                    e.memset(onescol[:], 1.0)
                    e.memset(z1ones[:], 1.0)
                    last = e.memset(S_bf[:], 0.0)
                    for i, val in enumerate(CPV):
                        last = e.memset(cpool[:, i:i + 1], float(val))
                    return last
                t_ms = P.add('V', ms, inc='v_st')

                def silu_c(e):
                    e.activation(mtmp[:], cmod_s[:], AF.Exp, scale=cpool[0:NSEQ, C_NEG1:C_NEG1 + 1])
                    e.activation(mtmp[:], mtmp[:], AF.Ln, bias=cpool[0:NSEQ, C_ONE:C_ONE + 1])
                    return e.activation(mtmp[:], mtmp[:], AF.Exp, scale=cpool[0:NSEQ, C_NEG1:C_NEG1 + 1])
                t_sig = P.add('S', silu_c, waits=[t_con, t_ms], inc='s_st')

                t_sc = P.add('V', lambda e: e.tensor_tensor(sc_bf[:], cmod_s[:], mtmp[:], OP.mult),
                             waits=[t_sig], inc='v_st')

                def tr_sc(e):
                    last = None
                    for j in range(8):
                        last = e.transpose(ptr0[:, 2 * j:2 * j + 2],
                                           sc_bf[:, 128 * j:128 * j + 128],
                                           identb[0:NSEQ, 0:NSEQ])
                    return last
                t_trsc = P.add('T', tr_sc, waits=[t_sc], inc='t_st')
                t_evsc = P.add('V', lambda e: e.tensor_copy(scfm[:], ptr0[:, 0:16]),
                               waits=[t_trsc], inc='v_st')

                t_prev_mm = t_evsc
                t_wmld = P.add('G', lambda e: e.dma_start(
                    wdst(wmod_bf, D),
                    bass.AP(wmod_d.tensor, 0, [[6 * D, 128], [128 * 6 * D, 8], [1, D]])),
                    inc='d_w', inc_k=16)
                t_modsb = None
                for v in range(6):
                    def mm_mod(e):
                        last = None
                        for half in range(2):
                            for dj in range(8):
                                last = e.matmul(
                                    pmod[:, 512 * half:512 * (half + 1)],
                                    scfm[:, 2 * dj:2 * dj + 2],
                                    wmod_bf[:, D * dj + 512 * half: D * dj + 512 * (half + 1)],
                                    start=(dj == 0), stop=(dj == 7))
                        return last
                    t_mm = P.add('T', mm_mod, waits=[t_prev_mm, t_wmld], inc='t_st')
                    if v < 5:
                        t_wmld = P.add('G', lambda e, v=v: e.dma_start(
                            wdst(wmod_bf, D),
                            bass.AP(wmod_d.tensor, (v + 1) * D,
                                    [[6 * D, 128], [128 * 6 * D, 8], [1, D]])),
                            waits=[t_mm], inc='d_w', inc_k=16)
                    t_prev_mm = P.add('V', lambda e, v=v: e.tensor_copy(
                        modsb[:, v * D:(v + 1) * D], pmod[:]),
                        waits=[t_mm], inc='v_st')
                t_modsb = t_prev_mm

                def mod_xf(e):
                    e.tensor_single_scalar(modsb[:, 1 * D:2 * D], modsb[:, 1 * D:2 * D], 1.0, OP.add)
                    e.tensor_single_scalar(modsb[:, 4 * D:5 * D], modsb[:, 4 * D:5 * D], 1.0, OP.add)
                    last = None
                    for v in (2, 5):
                        sl = modsb[:, v * D:(v + 1) * D]
                        e.tensor_single_scalar(mtmp[:], sl, 0.0, OP.is_ge)
                        e.tensor_scalar(mtmp[:], mtmp[:], 2.0, -1.0, OP.mult, OP.add)
                        e.tensor_single_scalar(mtmp2[:], sl, -1.0, OP.mult)
                        e.tensor_tensor(mtmp2[:], mtmp2[:], sl, OP.max)
                        e.tensor_single_scalar(mtmp2[:], mtmp2[:], 0.25, OP.max)
                        last = e.tensor_tensor(sl, mtmp[:], mtmp2[:], OP.mult)
                    return last
                t_modxf = P.add('V', mod_xf, waits=[t_modsb], inc='v_st')
                if debug:
                    P.add('Y', dma(dbg['mod'], modsb[:]), waits=[t_modxf], inc='d_dbg', inc_k=16)

                t_prev = t_modxf
                for v in range(6):
                    def tr_mod(e, v=v):
                        last = None
                        for dj in range(8):
                            last = e.transpose(ptf0[:, 2 * dj:2 * dj + 2],
                                               modsb[:, v * D + 128 * dj: v * D + 128 * dj + 128],
                                               identf[0:NSEQ, 0:NSEQ])
                        return last
                    tm = P.add('T', tr_mod, waits=[t_prev], inc='t_st')
                    t_prev = P.add('V', lambda e, v=v: e.tensor_copy(
                        mod_fm[:, v * 16:v * 16 + 16], ptf0[:, 0:16]),
                        waits=[tm], inc='v_st')
                t_modfm = t_prev

                def tr_mu(e):
                    e.transpose(ptf0[:, 32:40], murow[:], identf[0:8, 0:8])
                    return e.transpose(ptf0[:, 40:48], mukrow[:], identf[0:8, 0:8])
                t_trmu = P.add('T', tr_mu, waits=[t_modfm], inc='t_st')

                def ev_mu(e):
                    e.tensor_copy(murcol[:], ptf0[:, 32:40])
                    e.tensor_copy(mukcol[:], ptf0[:, 40:48])
                    e.drain()
                    e.tensor_scalar(omrcol[:], murcol[:], -1.0, 1.0, OP.mult, OP.add)
                    return e.tensor_scalar(omkcol[:], mukcol[:], -1.0, 1.0, OP.mult, OP.add)
                t_evmu = P.add('V', ev_mu, waits=[t_trmu], inc='v_st')

                t_bg = P.add('T', lambda e: e.matmul(ptf0[:, 0:512], onesrow[:], gnwrow[:],
                                                     start=True, stop=True),
                             waits=[t_evmu], inc='t_st')

                def ev_gnw(e):
                    last = None
                    for r in range(H):
                        last = e.tensor_copy(gnw_rep[:, r * 512:(r + 1) * 512], ptf0[:, 0:512])
                    return last
                t_prevb = P.add('V', ev_gnw, waits=[t_bg], inc='v_st')

                for b in range(NSEQ):
                    for rep, v in ((ga_rep, 2), (gb_rep, 5)):
                        t_row = P.add('Y', lambda e, b=b, v=v: e.dma_start(
                            brow[:], modsb[b:b + 1, v * D:(v + 1) * D]),
                            waits=[t_prevb], inc='d_st', inc_k=16)
                        for half in range(2):
                            tb = P.add('T', lambda e, half=half: e.matmul(
                                ptf0[:, 0:512], onesrow[:],
                                brow[:, 512 * half:512 * (half + 1)],
                                start=True, stop=True),
                                waits=[t_prevb, t_row], inc='t_st')
                            t_prevb = P.add('V', lambda e, b=b, rep=rep, half=half: e.tensor_copy(
                                rep[:, b * D + 512 * half: b * D + 512 * (half + 1)],
                                ptf0[:, 0:512]),
                                waits=[tb], inc='v_st')

                P.run_block(nc)

            with contextlib.ExitStack() as AS:
                EA = AS.enter_context
                xa2 = EA(nc.sbuf_tensor("xa2", [128, 2 * D], F32))
                y_bf = EA(nc.sbuf_tensor("y_bf", [128, D], BF16))
                hraw = EA(nc.sbuf_tensor("hraw", [128, D], BF16))
                h_fm = EA(nc.sbuf_tensor("h_fm", [128, D], BF16))
                q_bf = EA(nc.sbuf_tensor("q_bf", [128, D], BF16))
                k_bf = EA(nc.sbuf_tensor("k_bf", [128, D], BF16))
                v_bf = EA(nc.sbuf_tensor("v_bf", [128, DV], BF16))
                z1_f = EA(nc.sbuf_tensor("z1_f", [128, 16], F32))
                sp_f = EA(nc.sbuf_tensor("sp_f", [128, D], F32))
                eA_bf = EA(nc.sbuf_tensor("eA_bf", [128, D], BF16))
                eK_bf = EA(nc.sbuf_tensor("eK_bf", [128, D], BF16))
                qh_tm = EA(nc.sbuf_tensor("qh_tm", [128, D], BF16))
                kh_tm = EA(nc.sbuf_tensor("kh_tm", [128, D], BF16))
                qh_fm = EA(nc.sbuf_tensor("qh_fm", [128, D], BF16))
                qin_fm = EA(nc.sbuf_tensor("qin_fm", [128, D], BF16))
                kin_fm = EA(nc.sbuf_tensor("kin_fm", [128, D], BF16))
                Pm = EA(nc.sbuf_tensor("Pm", [128, 512], BF16))
                on_sb = EA(nc.sbuf_tensor("on_sb", [128, DV], BF16))
                wtr_bf = EA(nc.sbuf_tensor("wtr_bf", [128, 8 * D], BF16))
                ln_t = EA(nc.sbuf_tensor("ln_t", [128, 8], F32))
                x2a = EA(nc.sbuf_tensor("x2a", [128, D], BF16))
                EbA = EA(nc.sbuf_tensor("EbA", [128, D], BF16))
                colv = EA(nc.sbuf_tensor("colv", [128, 48], F32))
                ss4 = EA(nc.sbuf_tensor("ss4", [128, 8], F32))

                P1 = EA(nc.psum_tensor("P1_A", [128, 1024], F32))
                P2 = EA(nc.psum_tensor("P2_A", [128, 1024], F32))
                P3 = EA(nc.psum_tensor("P3_A", [128, 1024], F32))
                P4 = EA(nc.psum_tensor("P4_A", [128, 512], F32))
                P5 = EA(nc.psum_tensor("P5_A", [128, 512], F32))
                pS = P3

                P = Prog(book)
                pv = {}

                def dma_tr8(src, dst):
                    sap = bass.AP(src, 0, [[1024, 128], [128, 8], [1, 128]])
                    dap = bass.AP(dst, 0, [[1024, 128], [128, 8], [1, 128]])
                    return lambda e: e.dma_start(dap, sap, transpose=True)

                pv['d_x'] = P.add('G', lambda e: e.dma_start(xa2[:, 0:D], xin[0:128, :]),
                                  inc='d_x', inc_k=16)
                pv['d_x1'] = P.add('G', lambda e: e.dma_start(xa2[:, D:2 * D], xin[128:256, :]),
                                   inc='d_x', inc_k=16)

                def trs_group(dst_s, cj):
                    def f(e, sem):
                        for dj in range(8):
                            e.dma_start(
                                dst_s[:, 1024 * cj + 128 * dj: 1024 * cj + 128 * dj + 128],
                                wtr_bf[:, 1024 * dj + 128 * cj: 1024 * dj + 128 * cj + 128],
                                transpose=True).then_inc(sem, 16)
                        return None
                    return f
                bjob = []
                bjob.append(('G', dma(wdst(wtr_bf, D), wsrc(wk2_d, D)), 16))
                for cj0 in range(8):
                    bjob.append(('Y', trs_group(wrT_s, cj0), 16 * 8))
                bjob.append(('Y', lambda e: e.dma_start(wkT_scr, wrT_s[:]), 16))
                bjob.append(('G', dma(wdst(wtr_bf, D), wsrc(wv2_d, D)), 16))
                for cj0 in range(8):
                    bjob.append(('Y', trs_group(wrT_s, cj0), 16 * 8))
                bjob.append(('Y', lambda e: e.dma_start(wvT_scr, wrT_s[:]), 16))
                bjob.append(('G', dma(wdst(wtr_bf, D), wsrc(wr_d, D)), 16))
                for cj0 in range(8):
                    bjob.append(('Y', trs_group(wrT_s, cj0), 16 * 8))

                bstate = {'prev': None, 'idx': 0}

                def inject_bjob():
                    if bstate['idx'] >= len(bjob):
                        return
                    eng, emit, k = bjob[bstate['idx']]
                    sem = 'd_wt' if eng in ('Y', 'S') else 'd_w'
                    bstate['prev'] = P.add(eng, emit, waits=[bstate['prev']],
                                           inc=sem, inc_k=k, multi_dma=(k > 16))
                    bstate['idx'] += 1

                def emit_frontA(i):
                    b = i // NCH
                    xa = xa2[:, (i % 2) * D:(i % 2 + 1) * D]
                    t_x2 = P.add('S', lambda e, xa=xa: e.activation(x2a[:], xa, AF.Square),
                                 waits=[pv['d_x'], pv.get('vm2r')], inc='s_x2')
                    t_m = P.add('V', lambda e, xa=xa: e.tensor_reduce(
                        ln_t[:, 0:1], xa, AXX, OP.add), waits=[pv['d_x']], inc='v_m')
                    t_m2 = P.add('V', lambda e: e.tensor_reduce(ln_t[:, 2:3], x2a[:],
                                                                AXX, OP.add),
                                 waits=[t_x2], inc='v_m2r')
                    pv['vm2r'] = t_m2

                    def s_stats(e):
                        e.activation(ln_t[:, 1:2], ln_t[:, 0:1], AF.Identity, scale=cp(C_I1024))
                        e.activation(ln_t[:, 4:5], ln_t[:, 2:3], AF.Identity,
                                     scale=cp(C_I1024))
                        e.drain()
                        return e.activation(ln_t[:, 3:4], ln_t[:, 1:2], AF.Square)
                    t_st_ = P.add('S', s_stats, waits=[t_m, t_m2], inc='s_stats')

                    t_var = P.add('V', lambda e: e.tensor_tensor(
                        ln_t[:, 5:6], ln_t[:, 4:5], ln_t[:, 3:4], OP.subtract),
                        waits=[t_st_], inc='v_var')

                    def rstd(e):
                        e.activation(ln_t[:, 6:7], ln_t[:, 5:6], AF.Ln, bias=cp(C_EPS))
                        e.drain()
                        return e.activation(ln_t[:, 6:7], ln_t[:, 6:7], AF.Exp, scale=cp(C_NHALF))
                    t_rstd = P.add('S', rstd, waits=[t_var], inc='s_rstd')

                    t_y = P.add('V', lambda e, xa=xa: e.tensor_scalar(
                        y_bf[:], xa, ln_t[:, 1:2], ln_t[:, 6:7], OP.subtract, OP.mult),
                        waits=[t_rstd, pv.get('d_ty')], inc='v_y')

                    if i + 2 < NTOT:
                        pv['xq'].append(P.add(
                            'G', lambda e, r=128 * (i + 2), p=(i % 2): e.dma_start(
                                xa2[:, p * D:(p + 1) * D], xin[r:r + 128, :]),
                            waits=[t_y], inc='d_x', inc_k=16))

                    t_ty = P.add('Y', dma_tr8(y_bf, hraw),
                                 waits=[t_y, pv.get('s_h')], inc='d_ty', inc_k=16)
                    pv['d_ty'] = t_ty

                    def hev(e, b=b):
                        last = None
                        for j in range(8):
                            last = e.activation(h_fm[:, 128 * j:128 * (j + 1)],
                                                hraw[:, 128 * j:128 * (j + 1)], AF.Identity,
                                                scale=modcol(1, j, b), bias=modcol(0, j, b))
                        return last
                    t_h = P.add('S', hev, waits=[t_ty, pv.get('tvb')], inc='s_h')
                    pv['s_h'] = t_h

                def emit_gk(i):
                    def mm_z1(e):
                        last = None
                        for dj in range(8):
                            last = e.matmul(P4[:, 0:16], h_fm[:, 128 * dj:128 * (dj + 1)],
                                            wgk1_s[:, 16 * dj:16 * (dj + 1)],
                                            start=(dj == 0), stop=(dj == 7))
                        return last
                    t_z1 = P.add('T', mm_z1,
                                 waits=[pv.get('scols'), pv['s_h']] +
                                       ([t_wgk1] if i == 0 else []),
                                 inc='t_z1')
                    t_z1e = P.add('V', lambda e: e.tensor_copy(z1_f[:], P4[:, 0:16]),
                                  waits=[t_z1], inc='v_z1')
                    t_z1t = P.add('T', lambda e: e.transpose(P4[0:16, 128:256], z1_f[:],
                                                             identf[:]),
                                  waits=[t_z1e], inc='t_z1t')
                    t_z1te = P.add('V', lambda e: e.tensor_copy(z1ones[0:16, :],
                                                                P4[0:16, 128:256]),
                                   waits=[t_z1t], inc='v_z1t')

                    def mm_gkp(e):
                        e.matmul(P4[:, 0:512], z1ones[:], wgk2b[:, 0:512], start=True, stop=True)
                        return e.matmul(P5[:, 0:512], z1ones[:], wgk2b[:, 512:1024],
                                        start=True, stop=True)
                    t_gkp = P.add('T', mm_gkp, waits=[t_z1te, pv.get('vP5')], inc='t_gkp')

                    def s_E(e):
                        e.activation(EbA[:, 0:512], P4[:, 0:512], AF.Exp, scale=cp(C_NEG1))
                        return e.activation(EbA[:, 512:1024], P5[:, 0:512], AF.Exp,
                                            scale=cp(C_NEG1))
                    t_E = P.add('S', s_E, waits=[t_gkp], inc='s_E')
                    pv['sE'] = t_E
                    t_sp0 = P.add('S', lambda e: e.activation(sp_f[:], EbA[:], AF.Ln,
                                                              bias=cp(C_ONE)),
                                  waits=[pv.get('tcols'), pv.get('dbg_sp')], inc='s_sp0')
                    pv['sp0'] = t_sp0
                    pv['vP5'] = pv['sE']

                pv['xq'] = [pv['d_x1']]
                emit_frontA(0)
                emit_gk(0)
                for i in range(NTOT):
                    b = i // NCH
                    c = i % NCH
                    row = 128 * i

                    if c == 0:
                        pv['sinit'] = P.add('V', lambda e: e.memset(S_bf[:], 0.0),
                                            inc='v_sinit')
                    t_h = pv['s_h']

                    def mm_proj(dst, wsb, stride, base):
                        def f(e):
                            last = None
                            for half in range(2):
                                for dj in range(8):
                                    last = e.matmul(
                                        dst[:, 512 * half:512 * (half + 1)],
                                        h_fm[:, 128 * dj:128 * (dj + 1)],
                                        wsb[:, stride * dj + base + 512 * half:
                                            stride * dj + base + 512 * (half + 1)],
                                        start=(dj == 0), stop=(dj == 7))
                            return last
                        return f

                    t_q = P.add('T', mm_proj(P1, wq_s, D, 0),
                                waits=[t_h, pv.get('von')] + ([t_wq] if i == 0 else []),
                                inc='t_q')
                    t_k = P.add('T', mm_proj(P2, wk_s, D, 0),
                                waits=([t_wk] if i == 0 else []), inc='t_k')
                    t_va = P.add('T', mm_proj(P3, wv_s, DV, 0),
                                 waits=[pv.get('stt3')] + ([t_wv] if i == 0 else []),
                                 inc='t_va')

                    t_qe = P.add('V', lambda e: e.tensor_copy(q_bf[:], P1[:]),
                                 waits=[t_q], inc='v_q')
                    t_ke = P.add('V', lambda e: e.tensor_copy(k_bf[:], P2[:]),
                                 waits=[t_k], inc='v_k')
                    t_vae = P.add('V', lambda e: e.tensor_copy(v_bf[:, 0:1024], P3[:]),
                                  waits=[t_va], inc='v_va')
                    t_vb = P.add('T', mm_proj(P1, wv_s, DV, 1024), waits=[t_qe], inc='t_vb')
                    pv['tvb'] = t_vb
                    t_vbe = P.add('V', lambda e: e.tensor_copy(v_bf[:, 1024:2048], P1[:]),
                                  waits=[t_vb], inc='v_vb')

                    def mm_cum(dst, tri):
                        def f(e):
                            e.matmul(dst[:, 0:512], tri[:], sp_f[:, 0:512], start=True, stop=True)
                            return e.matmul(dst[:, 512:1024], tri[:], sp_f[:, 512:1024],
                                            start=True, stop=True)
                        return f
                    pv['spcl'] = P.add('V', lambda e: e.tensor_single_scalar(
                        sp_f[:], sp_f[:], 80.0, OP.min), waits=[pv['sp0']], inc='v_spcl')
                    t_cs = P.add('T', mm_cum(P1, utri), waits=[pv['spcl'], t_vbe], inc='t_cs')
                    t_t1 = P.add('T', mm_cum(P2, mtri), waits=[t_ke], inc='t_t1')

                    def mm_cols(e):
                        last = None
                        for j in range(8):
                            last = e.matmul(P4[:, 256 + 3 * j:259 + 3 * j],
                                            sp_f[:, 128 * j:128 * (j + 1)], hones[:],
                                            start=True, stop=True)
                        return last
                    t_cols = P.add('T', mm_cols, waits=[pv['spcl'], pv.get('sE')], inc='t_cols')
                    pv['tcols'] = t_cols

                    t_eA = P.add('S', lambda e: e.activation(eA_bf[:], P1[:], AF.Exp,
                                                             scale=cp(C_NI16), bias=cp(C_NLN16)),
                                 waits=[t_cs], inc='s_eA')
                    t_eK = P.add('S', lambda e: e.activation(eK_bf[:], P2[:], AF.Exp,
                                                             scale=cp(C_PI16)),
                                 waits=[t_t1], inc='s_eK')

                    spC_ap = bass.AP(P4, 256, [[512, 128], [3, 8]])
                    spm_ap = bass.AP(P4, 257, [[512, 128], [3, 8]])
                    sph_ap = bass.AP(P4, 258, [[512, 128], [3, 8]])

                    def v_colpre(e):
                        e.tensor_single_scalar(colv[:, 32:40], spm_ap, 552.0, OP.min)
                        return e.tensor_single_scalar(colv[:, 8:16], sph_ap, 552.0, OP.min)
                    t_vcp = P.add('V', v_colpre, waits=[t_cols], inc='v_colp')

                    def s_cols(e):
                        e.activation(colv[:, 0:8], spC_ap, AF.Exp, scale=cp(C_NI16))
                        e.activation(colv[:, 16:24], colv[:, 32:40], AF.Exp, scale=cp(C_PI16))
                        return e.activation(colv[:, 24:32], colv[:, 8:16], AF.Exp,
                                            scale=cp(C_PI16))
                    t_vc = P.add('S', s_cols, waits=[t_vcp], inc='s_cols')
                    pv['scols'] = t_vc

                    if i + 1 < NTOT:
                        pv['d_x'] = pv['xq'].pop(0)
                        emit_frontA(i + 1)
                        emit_gk(i + 1)

                    t_qh = P.add('V', lambda e: e.tensor_tensor(qh_tm[:], q_bf[:], eA_bf[:],
                                                                OP.mult),
                                 waits=[t_eA], inc='v_qh')
                    t_kh = P.add('V', lambda e: e.tensor_tensor(kh_tm[:], k_bf[:], eK_bf[:],
                                                                OP.mult),
                                 waits=[t_eK], inc='v_kh')

                    t_tq = P.add('Y', dma_tr8(qh_tm, qh_fm),
                                 waits=[t_qh, pv.get('tio3')], inc='d_tq', inc_k=16)
                    pv['d_tq'] = t_tq

                    def ev_tq(e):
                        last = None
                        for j in range(8):
                            last = e.tensor_single_scalar(qin_fm[:, 128 * j:128 * (j + 1)],
                                                          qh_fm[:, 128 * j:128 * (j + 1)],
                                                          colv[:, 16 + j:17 + j], OP.mult)
                        return last
                    t_tqe = P.add('V', ev_tq, waits=[t_tq, t_vc, pv.get('tP')], inc='v_tqe')

                    t_tk = P.add('Y', dma_tr8(kh_tm, kin_fm),
                                 waits=[t_kh, pv.get('tP')], inc='d_tk', inc_k=16)
                    pv['d_tkp'] = t_tk

                    def ev_tk(e):
                        last = None
                        for j in range(8):
                            last = e.tensor_single_scalar(kin_fm[:, 128 * j:128 * (j + 1)],
                                                          kin_fm[:, 128 * j:128 * (j + 1)],
                                                          colv[:, 24 + j:25 + j], OP.mult)
                        return last
                    t_tke = P.add('V', ev_tk, waits=[t_tk, t_vc], inc='v_tke')

                    def mm_state(h):
                        def f(e):
                            last = None
                            for j in range(2):
                                last = e.matmul(pS[:, 512 * j:512 * (j + 1)],
                                                kh_tm[:, 256 * h + 128 * j:
                                                      256 * h + 128 * (j + 1)],
                                                v_bf[:, 512 * h:512 * (h + 1)],
                                                start=True, stop=True)
                            return last
                        return f

                    t_st = [None] * 4
                    t_st[0] = P.add('T', mm_state(0), waits=[t_kh, t_vae], inc='t_s0')

                    ps = [P1[:, 0:128], P1[:, 512:640], P2[:, 0:128], P2[:, 512:640]]
                    po = [P1[:, 0:512], P1[:, 512:1024], P2[:, 0:512], P2[:, 512:1024]]

                    def mm_P(e):
                        last = None
                        for h in range(4):
                            for j in range(2):
                                dblk = 2 * h + j
                                last = e.matmul(ps[h],
                                                kin_fm[:, 128 * dblk:128 * (dblk + 1)],
                                                qin_fm[:, 128 * dblk:128 * (dblk + 1)],
                                                start=(j == 0), stop=(j == 1))
                        return last
                    t_P = P.add('T', mm_P, waits=[t_tke, t_tqe, t_eA, t_eK], inc='t_P')
                    pv['tP'] = t_P

                    def v_sel(e):
                        last = None
                        for h in range(4):
                            last = e.select(Pm[:, 128 * h:128 * (h + 1)], cmask[:], ps[h],
                                            zeros[:])
                        return last
                    t_Pm = P.add('V', v_sel, waits=[t_P], inc='v_Pm')

                    t_io = [None] * 4
                    t_stt = [None] * 4
                    for h in range(4):
                        def mm_io(e, h=h):
                            e.matmul(po[h], Pm[:, 128 * h:128 * (h + 1)],
                                     v_bf[:, 512 * h:512 * (h + 1)], start=True, stop=False)
                            last = None
                            for j in range(2):
                                last = e.matmul(po[h],
                                                qh_fm[:, 128 * (2 * h + j):128 * (2 * h + j + 1)],
                                                S_tile(h, j), start=False, stop=(j == 1))
                            return last
                        io_w = []
                        if h == 0:
                            io_w = [t_Pm, pv.get('stt3'), t_tq]
                            if c == 0:
                                io_w.append(pv['sinit'])
                        t_io[h] = P.add('T', mm_io, waits=io_w, inc=f'tio{h}')

                        def v_stt(e, h=h):
                            last = None
                            for j in range(2):
                                last = e.scalar_tensor_tensor(
                                    S_tile(h, j), S_tile(h, j), colv[:, 2 * h + j:2 * h + j + 1],
                                    pS[:, 512 * j:512 * (j + 1)], OP.mult, OP.add)
                            return last
                        t_stt[h] = P.add('V', v_stt, waits=[t_st[h], t_io[h], t_vc],
                                         inc=f'vstt{h}')
                        if h < 3:
                            t_st[h + 1] = P.add('T', mm_state(h + 1),
                                                waits=[t_stt[h]], inc=f'ts{h + 1}')
                    pv['stt3'] = t_stt[3]
                    pv['tst3'] = t_st[3]
                    pv['tio3'] = t_io[3]

                    def s_rms(e):
                        for h in range(4):
                            e.activation(qh_tm[:, 0:512], po[h], AF.Square,
                                         accum_out=ss4[:, h:h + 1])
                        e.drain()
                        e.activation(ss4[:, 4:8], ss4[:, 0:4], AF.Ln, scale=cp(C_I512),
                                     bias=cp(C_EPS))
                        e.drain()
                        return e.activation(ss4[:, 4:8], ss4[:, 4:8], AF.Exp, scale=cp(C_NHALF))
                    t_rms = P.add('S', s_rms, waits=[t_io[3]], inc='s_rms')

                    def v_on(e):
                        last = None
                        for h in range(4):
                            last = e.tensor_single_scalar(on_sb[:, 512 * h:512 * (h + 1)],
                                                          po[h], ss4[:, 4 + h:5 + h], OP.mult)
                        return last
                    t_on = P.add('V', v_on, waits=[t_rms, pv.get('d_on')], inc='v_on')
                    pv['von'] = t_on

                    pv['d_on'] = P.add('G', lambda e, row=row: e.dma_start(
                        o_scr[row:row + 128, :], on_sb[:]), waits=[t_on],
                        inc='d_on', inc_k=16)

                    if debug:
                        pv['dbg_h'] = P.add('Y', lambda e, row=row: e.dma_start(
                            dbg['h'][row:row + 128, :], h_fm[:]), waits=[t_h],
                            inc='d_dbg', inc_k=16)
                        pv['dbg_sp'] = P.add('Y', lambda e, row=row: e.dma_start(
                            dbg['sp'][row:row + 128, :], sp_f[:]),
                            waits=[pv['spcl']], inc='d_dbg', inc_k=16)

                    inject_bjob()

                assert bstate['idx'] == len(bjob)
                bw['wrT'] = bstate['prev']
                P.add('G', lambda e: e.nop(), waits=[pv['d_on']])
                P.run_block(nc)


        with contextlib.ExitStack() as BS:
            EB = BS.enter_context
            wo_s = EB(nc.sbuf_tensor("wo_s", [128, 16 * D], BF16))
            wkT_s = EB(nc.sbuf_tensor("wkT_s", [128, 8 * D], BF16))
            wvT_s = EB(nc.sbuf_tensor("wvT_s", [128, 8 * D], BF16))

            xb2 = EB(nc.sbuf_tensor("xb2", [128, 2 * D], F32))
            onb = EB(nc.sbuf_tensor("onb", [128, 2 * DV], BF16))
            y2_bf = EB(nc.sbuf_tensor("y2_bf", [128, D], BF16))
            y_b = EB(nc.sbuf_tensor("y_b", [128, D], BF16))
            hraw2 = EB(nc.sbuf_tensor("hraw2", [128, D], BF16))
            hraw2b = EB(nc.sbuf_tensor("hraw2b", [128, D], BF16))
            hB_fm = EB(nc.sbuf_tensor("hB_fm", [128, D], BF16))
            h2_fm = EB(nc.sbuf_tensor("h2_fm", [128, D], BF16))
            h2s_fm = EB(nc.sbuf_tensor("h2s_fm", [128, D], BF16))
            carry = EB(nc.sbuf_tensor("carry", [128, 8], BF16))
            g_bf = EB(nc.sbuf_tensor("g_bf", [128, DV], BF16))
            Ebuf = EB(nc.sbuf_tensor("Ebuf", [128, DV], BF16))
            Fbuf = EB(nc.sbuf_tensor("Fbuf", [128, 2 * DV], BF16))
            og_fm = EB(nc.sbuf_tensor("og_fm", [128, DV], BF16))
            tr_bf = EB(nc.sbuf_tensor("tr_bf", [128, D], BF16))
            xnew = EB(nc.sbuf_tensor("xnew", [128, 2 * D], F32))
            xr_fm = EB(nc.sbuf_tensor("xr_fm", [128, D], BF16))
            xk_fm = EB(nc.sbuf_tensor("xk_fm", [128, D], BF16))
            t2r = EB(nc.sbuf_tensor("t2r", [128, D], BF16))
            ka_bf = EB(nc.sbuf_tensor("ka_bf", [128, D], BF16))
            kaf = EB(nc.sbuf_tensor("kaf", [128, D], BF16))
            rr_bf = t2r
            f_bf = ka_bf
            x2B = EB(nc.sbuf_tensor("x2B", [128, D], BF16))
            lnB = EB(nc.sbuf_tensor("lnB", [128, 8], F32))
            ln2B = EB(nc.sbuf_tensor("ln2B", [128, 8], F32))

            Q1 = EB(nc.psum_tensor("Q1_B", [128, 1024], F32))
            Q2 = EB(nc.psum_tensor("Q2_B", [128, 1024], F32))
            Q3 = EB(nc.psum_tensor("Q3_B", [128, 1024], F32))
            Q4 = EB(nc.psum_tensor("Q4_B", [128, 1024], F32))

            P = Prog(book)
            pv = {}

            def dma_trn(src, dst, n):
                w = 128 * n
                sap = bass.AP(src, 0, [[w, 128], [128, n], [1, 128]])
                dap = bass.AP(dst, 0, [[w, 128], [128, n], [1, 128]])
                return lambda e: e.dma_start(dap, sap, transpose=True)

            t_wo = P.add('G', lambda e: e.dma_start(wdst(wo_s, D, 16), wsrc(wo_d, D, 16)),
                         inc='d_w', inc_k=16)
            t_wkT = P.add('G', lambda e: e.dma_start(wkT_s[:], wkT_scr),
                          inc='d_w', inc_k=16)
            t_wvT = P.add('G', lambda e: e.dma_start(wvT_s[:], wvT_scr),
                          inc='d_w', inc_k=16)

            pv['d_x0'] = P.add('G', lambda e: e.dma_start(xb2[:, 0:D], xin[0:128, :]),
                               inc='d_x2', inc_k=16)
            pv['d_x1'] = P.add('G', lambda e: e.dma_start(xb2[:, D:2 * D], xin[128:256, :]),
                               inc='d_x2', inc_k=16)
            pv['d_o0'] = P.add('G', lambda e: e.dma_start(onb[:, 0:DV], o_scr[0:128, :]),
                               inc='d_on2', inc_k=16)

            def emit_front(i):
                b = i // NCH
                pb = i % 2
                xcur = xb2[:, pb * D:(pb + 1) * D]
                d_x_tok = pv[f'd_x{i}'] if i else pv['d_x0']

                t_x2 = P.add('S', lambda e, xc=xcur: e.activation(x2B[:], xc, AF.Square),
                             waits=[d_x_tok, pv.get('vka')], inc='s_x2B')
                t_m = P.add('V', lambda e, xc=xcur: e.tensor_reduce(
                    lnB[:, 0:1], xc, AXX, OP.add), waits=[d_x_tok], inc='v_mB')
                t_m2 = P.add('V', lambda e: e.tensor_reduce(lnB[:, 2:3], x2B[:], AXX, OP.add),
                             waits=[t_x2], inc='v_m2B')
                pv['vm2B'] = t_m2

                def s_statsB(e):
                    e.activation(lnB[:, 1:2], lnB[:, 0:1], AF.Identity, scale=cp(C_I1024))
                    e.activation(lnB[:, 4:5], lnB[:, 2:3], AF.Identity,
                                 scale=cp(C_I1024))
                    e.drain()
                    return e.activation(lnB[:, 3:4], lnB[:, 1:2], AF.Square)
                t_stB = P.add('S', s_statsB, waits=[t_m, t_m2], inc='s_statsB')
                t_var = P.add('V', lambda e: e.tensor_tensor(
                    lnB[:, 5:6], lnB[:, 4:5], lnB[:, 3:4], OP.subtract),
                    waits=[t_stB], inc='v_varB')

                def rstdB(e):
                    e.activation(lnB[:, 6:7], lnB[:, 5:6], AF.Ln, bias=cp(C_EPS))
                    e.drain()
                    return e.activation(lnB[:, 6:7], lnB[:, 6:7], AF.Exp, scale=cp(C_NHALF))
                t_rstd = P.add('S', rstdB, waits=[t_var], inc='s_rstdB')

                t_y = P.add('V', lambda e, xc=xcur: e.tensor_scalar(
                    y2_bf[:], xc, lnB[:, 1:2], lnB[:, 6:7], OP.subtract, OP.mult),
                    waits=[t_rstd, pv.get('d_tyf')], inc='v_y2')

                t_ty = P.add('Y', dma_trn(y2_bf, hraw2, 8),
                             waits=[t_y, pv.get('s_hB')], inc='d_tyB', inc_k=16)
                pv['d_tyf'] = t_ty

                def hev(e, b=b):
                    last = None
                    for j in range(8):
                        last = e.activation(hB_fm[:, 128 * j:128 * (j + 1)],
                                            hraw2[:, 128 * j:128 * (j + 1)], AF.Identity,
                                            scale=modcol(1, j, b), bias=modcol(0, j, b))
                    return last
                t_h = P.add('S', hev, waits=[t_ty, pv.get('tgB')], inc='s_hB')
                pv['s_hB'] = t_h

                def mm_g(dst, base):
                    def f(e):
                        last = None
                        for half in range(2):
                            for dj in range(8):
                                last = e.matmul(dst[:, 512 * half:512 * (half + 1)],
                                                hB_fm[:, 128 * dj:128 * (dj + 1)],
                                                wg_s[:, DV * dj + base + 512 * half:
                                                     DV * dj + base + 512 * (half + 1)],
                                                start=(dj == 0), stop=(dj == 7))
                        return last
                    return f
                t_gA = P.add('T', mm_g(Q4, 0),
                             waits=[t_h, pv.get('vgA')] + ([bw['wg']] if i == 0 else []),
                             inc='t_gA')
                t_gB = P.add('T', mm_g(Q2, 1024),
                             waits=[pv.get('skr') if i >= 2 else pv.get('vgB')], inc='t_gB')
                pv['tgB'] = t_gB

                t_gAe = P.add('V', lambda e: e.tensor_copy(g_bf[:, 0:1024], Q4[:]),
                              waits=[t_gA], inc='v_gA')
                pv['vgA'] = t_gAe
                t_gBe = P.add('V', lambda e: e.tensor_copy(g_bf[:, 1024:2048], Q2[:]),
                              waits=[t_gB], inc='v_gB')
                pv['vgB'] = t_gBe

                def sg(e):
                    e.activation(Ebuf[:], g_bf[:], AF.Exp, scale=cp(C_NEG1))
                    e.activation(Ebuf[:], Ebuf[:], AF.Ln, bias=cp(C_ONE))
                    return e.activation(Ebuf[:], Ebuf[:], AF.Exp, scale=cp(C_NEG1))
                t_sg = P.add('S', sg, waits=[t_gAe, t_gBe], inc='s_sg')

                def fchain(e, i=i):
                    fb = Fbuf[:, (i % 2) * DV:(i % 2 + 1) * DV]
                    e.tensor_tensor(fb, g_bf[:], Ebuf[:], OP.mult)
                    return e.tensor_tensor(fb, fb, gnw_rep[:], OP.mult)
                t_F = P.add('V', fchain, waits=[t_sg, pv.get('vog2'), pv.get('vf')],
                            inc='v_F')
                pv[f'vF{i}'] = t_F

            def emit_mid(i):
                b = i // NCH
                row = 128 * i
                pb = i % 2
                xcur = xb2[:, pb * D:(pb + 1) * D]
                ocur = onb[:, pb * DV:(pb + 1) * DV]
                xncur = xnew[:, pb * D:(pb + 1) * D]
                d_o_tok = pv[f'd_o{i}'] if i else pv['d_o0']

                fslot = Fbuf[:, (i % 2) * DV:(i % 2 + 1) * DV]
                t_og = P.add('V', lambda e, oc=ocur, fs=fslot: e.tensor_tensor(
                    fs, oc, fs, OP.mult),
                    waits=[d_o_tok, pv[f'vF{i}'], pv.get('dogT')], inc='v_og')
                pv['vog2'] = t_og

                if i + 1 < NTOT:
                    pv[f'd_o{i + 1}'] = P.add('G', lambda e, r=row + 128,
                                              nb=(i + 1) % 2: e.dma_start(
                        onb[:, nb * DV:(nb + 1) * DV], o_scr[r:r + 128, :]),
                        waits=[t_og], inc='d_on2', inc_k=16)

                sap16 = bass.AP(Fbuf, (i % 2) * DV, [[2 * DV, 128], [128, 16], [1, 128]])
                dap16 = bass.AP(og_fm, 0, [[DV, 128], [128, 16], [1, 128]])
                t_ogT = P.add('Y', lambda e, sap=sap16, dap=dap16: e.dma_start(
                    dap, sap, transpose=True),
                    waits=[t_og, pv.get('two'), pv.get('vf')], inc='d_ogT', inc_k=16)
                pv['dogT'] = t_ogT

                def mm_wo(e):
                    last = None
                    for half in range(2):
                        for vj in range(16):
                            last = e.matmul(Q3[:, 512 * half:512 * (half + 1)],
                                            og_fm[:, 128 * vj:128 * (vj + 1)],
                                            wo_s[:, D * vj + 512 * half:
                                                 D * vj + 512 * (half + 1)],
                                            start=(vj == 0), stop=(vj == 15))
                    return last
                t_wom = P.add('T', mm_wo,
                              waits=[t_ogT, pv.get('vf')] + ([t_wo] if i == 0 else []),
                              inc='t_wo')
                pv['two'] = t_wom

                def v_xn(e, b=b, xc=xcur, xn=xncur):
                    e.tensor_tensor(tr_bf[:], Q3[:], ga_rep[:, b * D:(b + 1) * D], OP.mult)
                    return e.tensor_tensor(xn, tr_bf[:], xc, OP.add)
                t_xn = P.add('V', v_xn, waits=[t_wom, pv.get(f'd_out{i - 2}')], inc='v_xn')
                pv[f'vxn{i}'] = t_xn
                if i + 2 < NTOT:
                    pv[f'd_x{i + 2}'] = P.add('G', lambda e, r=128 * (i + 2),
                                              nb=i % 2: e.dma_start(
                        xb2[:, nb * D:(nb + 1) * D], xin[r:r + 128, :]),
                        waits=[t_xn], inc='d_x2', inc_k=16)

            def emit_tail(i):
                b = i // NCH
                c = i % NCH
                row = 128 * i
                pb = i % 2
                xncur = xnew[:, pb * D:(pb + 1) * D]
                t_xn = pv[f'vxn{i}']

                t_x22 = P.add('S', lambda e, xn=xncur: e.activation(x2B[:], xn, AF.Square),
                              waits=[t_xn, pv.get('vm2B')], inc='s_x2B2')
                t_m2 = P.add('V', lambda e, xn=xncur: e.tensor_reduce(
                    ln2B[:, 0:1], xn, AXX, OP.add), waits=[t_xn], inc='v_m2')
                t_m2b = P.add('V', lambda e: e.tensor_reduce(ln2B[:, 2:3], x2B[:], AXX, OP.add),
                              waits=[t_x22], inc='v_m2b')

                def s_stats2(e):
                    e.activation(ln2B[:, 1:2], ln2B[:, 0:1], AF.Identity, scale=cp(C_I1024))
                    e.activation(ln2B[:, 4:5], ln2B[:, 2:3], AF.Identity,
                                 scale=cp(C_I1024))
                    e.drain()
                    return e.activation(ln2B[:, 3:4], ln2B[:, 1:2], AF.Square)
                t_st2 = P.add('S', s_stats2, waits=[t_m2, t_m2b], inc='s_stats2')
                t_var2 = P.add('V', lambda e: e.tensor_tensor(
                    ln2B[:, 5:6], ln2B[:, 4:5], ln2B[:, 3:4], OP.subtract),
                    waits=[t_st2], inc='v_var2')

                def rstd2(e):
                    e.activation(ln2B[:, 6:7], ln2B[:, 5:6], AF.Ln, bias=cp(C_EPS))
                    e.drain()
                    return e.activation(ln2B[:, 6:7], ln2B[:, 6:7], AF.Exp, scale=cp(C_NHALF))
                t_rstd2 = P.add('S', rstd2, waits=[t_var2], inc='s_rstd2')

                t_y2b = P.add('V', lambda e, xn=xncur: e.tensor_scalar(
                    y_b[:], xn, ln2B[:, 1:2], ln2B[:, 6:7], OP.subtract, OP.mult),
                    waits=[t_rstd2, pv.get('d_ty2')], inc='v_y2b')

                t_ty2 = P.add('Y', dma_trn(y_b, hraw2b, 8),
                              waits=[t_y2b, pv.get('s_h2Blast')], inc='d_ty2B', inc_k=16)
                pv['d_ty2'] = t_ty2

                def h2ev(e, b=b):
                    last = None
                    for j in range(8):
                        last = e.activation(h2_fm[:, 128 * j:128 * (j + 1)],
                                            hraw2b[:, 128 * j:128 * (j + 1)], AF.Identity,
                                            scale=modcol(4, j, b), bias=modcol(3, j, b))
                    return last
                t_h2 = P.add('S', h2ev, waits=[t_ty2], inc='s_h2B')
                pv['s_h2Blast'] = t_h2

                def shift(e, c=c):
                    if c == 0:
                        e.memset(carry[:], 0.0)
                    e.tensor_copy(bass.AP(h2s_fm, 0, [[D, 128], [128, 8]]),
                                  bass.AP(carry, 0, [[8, 128], [1, 8]]))
                    e.tensor_copy(bass.AP(h2s_fm, 1, [[D, 128], [128, 8], [1, 127]]),
                                  bass.AP(h2_fm, 0, [[D, 128], [128, 8], [1, 127]]))
                    return e.tensor_copy(carry[:],
                                         bass.AP(h2_fm, 127, [[D, 128], [128, 8]]))
                t_sh = P.add('V', shift, waits=[t_h2], inc='v_sh')

                def s_lerp_r(e):
                    last = None
                    for j in range(8):
                        last = e.activation(t2r[:, 128 * j:128 * (j + 1)],
                                            h2s_fm[:, 128 * j:128 * (j + 1)], AF.Identity,
                                            scale=omrcol[:, j:j + 1])
                    return last
                t_lpr = P.add('S', s_lerp_r, waits=[t_sh], inc='s_lerp')

                def v_xr(e):
                    last = None
                    for j in range(8):
                        last = e.scalar_tensor_tensor(
                            xr_fm[:, 128 * j:128 * (j + 1)], h2_fm[:, 128 * j:128 * (j + 1)],
                            murcol[:, j:j + 1], t2r[:, 128 * j:128 * (j + 1)], OP.mult, OP.add)
                    return last
                t_xr = P.add('V', v_xr, waits=[t_lpr], inc='v_xr')

                def s_lerp_k(e):
                    last = None
                    for j in range(8):
                        last = e.activation(t2r[:, 128 * j:128 * (j + 1)],
                                            h2s_fm[:, 128 * j:128 * (j + 1)], AF.Identity,
                                            scale=omkcol[:, j:j + 1])
                    return last
                t_lpk = P.add('S', s_lerp_k, waits=[t_xr], inc='s_lerpk')

                def v_xk(e):
                    last = None
                    for j in range(8):
                        last = e.scalar_tensor_tensor(
                            xk_fm[:, 128 * j:128 * (j + 1)], h2_fm[:, 128 * j:128 * (j + 1)],
                            mukcol[:, j:j + 1], t2r[:, 128 * j:128 * (j + 1)], OP.mult, OP.add)
                    return last
                t_xk = P.add('V', v_xk, waits=[t_lpk], inc='v_xk')

                def mm_ffn(dst, xsb, wsb):
                    def f(e):
                        last = None
                        for half in range(2):
                            for cj in range(8):
                                last = e.matmul(dst[:, 512 * half:512 * (half + 1)],
                                                xsb[:, 128 * cj:128 * (cj + 1)],
                                                wsb[:, D * cj + 512 * half:
                                                     D * cj + 512 * (half + 1)],
                                                start=(cj == 0), stop=(cj == 7))
                        return last
                    return f
                t_rp = P.add('T', mm_ffn(Q1, xr_fm, wrT_s),
                             waits=[t_xr, pv.get('sE2')] + ([bw['wrT'], t_wkT] if i == 0 else []),
                             inc='t_rp')

                t_E2 = P.add('S', lambda e: e.activation(Ebuf[:, 0:1024], Q1[:], AF.Exp,
                                                         scale=cp(C_NEG1)),
                             waits=[t_rp], inc='s_E2')
                pv['sE2'] = t_E2

                def rr2(e):
                    e.activation(Ebuf[:, 0:1024], Ebuf[:, 0:1024], AF.Ln, bias=cp(C_ONE))
                    return e.activation(rr_bf[:], Ebuf[:, 0:1024], AF.Exp, scale=cp(C_NEG1))
                t_rr = P.add('S', rr2, waits=[t_xk], inc='s_rr')

                t_kp = P.add('T', mm_ffn(Q2, xk_fm, wkT_s),
                             waits=[t_xk, pv.get('vgB'), pv.get('skr')] +
                                   ([t_wkT] if i == 0 else []),
                             inc='t_kp')
                t_kr = P.add('S', lambda e: e.activation(x2B[:], Q2[:], AF.Relu),
                             waits=[t_kp], inc='s_kr')
                pv['skr'] = t_kr
                t_ka = P.add('V', lambda e: e.tensor_tensor(ka_bf[:], x2B[:], x2B[:], OP.mult),
                             waits=[t_kr, pv.get('dkaT')], inc='v_ka')
                pv['vka'] = t_ka

                t_kaT = P.add('Y', dma_trn(ka_bf, kaf, 8),
                              waits=[t_ka, pv.get('tvo')], inc='d_kaT', inc_k=16)
                pv['dkaT'] = t_kaT

                t_vo = P.add('T', mm_ffn(Q3, kaf, wvT_s),
                             waits=[t_kaT, t_xn] + ([t_wvT] if i == 0 else []), inc='t_vo')
                pv['tvo'] = t_vo

                def v_f(e, b=b, xn=xncur):
                    t3 = og_fm[:, 0:1024]
                    e.tensor_tensor(f_bf[:], Q3[:], rr_bf[:], OP.mult)
                    e.tensor_tensor(t3, f_bf[:], gb_rep[:, b * D:(b + 1) * D], OP.mult)
                    return e.tensor_tensor(xn, t3, xn, OP.add)
                t_f = P.add('V', v_f, waits=[t_vo, t_rr], inc='v_f')
                pv['vf'] = t_f

                pv[f'd_out{i}'] = P.add('G', lambda e, row=row, xn=xncur: e.dma_start(
                    out_d[row:row + 128, :], xn), waits=[t_f], inc='d_out', inc_k=16)

            for i in range(NTOT):
                emit_front(i)
                if i >= 2:
                    emit_tail(i - 2)
                if i >= 1:
                    emit_mid(i - 1)
            emit_tail(NTOT - 2)
            emit_mid(NTOT - 1)
            emit_tail(NTOT - 1)

            P.add('G', lambda e: e.nop(), waits=[pv[f'd_out{NTOT - 1}']])
            P.run_block(nc)


    return nc


def _consts():
    ib = np.eye(128, dtype=ml_dtypes.bfloat16)
    iff = np.eye(128, dtype=np.float32)
    s = np.arange(128)
    ut = (s[:, None] <= s[None, :]).astype(np.float32)
    mt = -(s[:, None] > s[None, :]).astype(np.float32)
    cm = (s[:, None] <= s[None, :]).astype(np.uint8)
    ho = np.zeros((128, 3), np.float32)
    ho[:, 0] = 1.0
    ho[:64, 1] = 1.0
    ho[64:, 2] = 1.0
    return ib, iff, ut, mt, cm, ho


def kernel(L0_attn_x, L0_attn_Wq, L0_attn_Wk, L0_attn_Wv, L0_attn_Wg,
           L0_attn_Wgk1, L0_attn_Wgk2, L0_attn_bgk2, L0_attn_Wo,
           L0_attn_g_norm_weight, L0_ffn_W_r, L0_ffn_W_k, L0_ffn_W_v,
           L0_ffn_mu_r, L0_ffn_mu_k, L0_c, L0_wmod):
    x = np.ascontiguousarray(np.asarray(L0_attn_x, dtype=np.float32))
    ib, iff, ut, mt, cm, ho = _consts()

    common = dict(
        wq=np.ascontiguousarray(L0_attn_Wq, np.float32),
        wk=np.ascontiguousarray(L0_attn_Wk, np.float32),
        wv=np.ascontiguousarray(L0_attn_Wv, np.float32),
        wg=np.ascontiguousarray(L0_attn_Wg, np.float32),
        wgk1=np.ascontiguousarray(L0_attn_Wgk1, np.float32),
        wgk2=np.ascontiguousarray(L0_attn_Wgk2, np.float32),
        bgk2=np.ascontiguousarray(L0_attn_bgk2, np.float32).reshape(1, D),
        wo=np.ascontiguousarray(L0_attn_Wo, np.float32),
        gnw=np.ascontiguousarray(L0_attn_g_norm_weight, np.float32).reshape(1, DVH),
        wr=np.ascontiguousarray(L0_ffn_W_r, np.float32),
        wk2=np.ascontiguousarray(L0_ffn_W_k, np.float32),
        wv2=np.ascontiguousarray(L0_ffn_W_v, np.float32),
        mur=np.ascontiguousarray(L0_ffn_mu_r, np.float32).reshape(8, 128),
        muk=np.ascontiguousarray(L0_ffn_mu_k, np.float32).reshape(8, 128),
        wmod=np.ascontiguousarray(L0_wmod, np.float32),
        identb=ib, identf=iff, utri=ut, mtri=mt, cmask=cm, hones=ho,
    )
    cmod = np.ascontiguousarray(L0_c, np.float32)

    nc = build_nc(debug=DEBUG)
    in_maps = []
    for core in range(NCORE):
        m = dict(common)
        m["xin"] = np.ascontiguousarray(x[2 * core:2 * core + 2].reshape(TOK, D))
        m["cmod"] = np.ascontiguousarray(cmod[2 * core:2 * core + 2])
        in_maps.append(m)

    res = run_bass_kernel_spmd(nc, in_maps, core_ids=list(range(NCORE)))
    outs = [np.asarray(r["out"]).reshape(NSEQ, N, D) for r in res.results]
    return np.concatenate(outs, axis=0)




# version 09
# speedup vs baseline: 1.1711x; 1.1711x over previous
import contextlib

import numpy as np
import ml_dtypes

import concourse.bass as bass
import concourse.mybir as mybir
from concourse.bass_utils import run_bass_kernel_spmd

F32 = mybir.dt.float32
BF16 = mybir.dt.bfloat16
U8 = mybir.dt.uint8
AF = mybir.ActivationFunctionType
OP = mybir.AluOpType
AXX = mybir.AxisListType.X

NCORE = 8
NSEQ = 2
N = 2048
D = 1024
H = 4
DVH = 512
DV = H * DVH
C = 128
NCH = N // C
NTOT = NSEQ * NCH
TOK = NSEQ * N

LN16 = float(np.log(16.0))
EPS = 1e-5
CLAMP_E = 1e15

DEBUG = False


class Book:
    def __init__(self, nc, ctx):
        self.nc, self.ctx = nc, ctx
        self.sems = {}
        self.counts = {}

    @staticmethod
    def _key(name):
        if name.startswith('d_'):
            return name
        if name[0] == 't':
            return 'ET'
        if name[0] == 'v':
            return 'EV'
        return 'ES'

    def get(self, name):
        key = self._key(name)
        if key not in self.sems:
            self.sems[key] = self.ctx.enter_context(self.nc.semaphore(key))
            self.counts[key] = 0
        return self.sems[key]

    def tok(self, name, k=1):
        self.get(name)
        key = self._key(name)
        self.counts[key] += k
        return (key, self.counts[key], k)


class Prog:

    def __init__(self, book):
        self.book = book
        self.stages = []

    def add(self, eng, emit, waits=(), inc=None, inc_k=1, multi_dma=False):
        tok = self.book.tok(inc, inc_k) if inc is not None else None
        ws = [w for w in waits if w is not None]
        self.stages.append((eng, emit, ws, tok, multi_dma))
        return tok

    def emit_engine(self, eng_name, eng):
        book = self.book
        for e, emit, waits, tok, multi_dma in self.stages:
            if e != eng_name:
                continue
            for w in waits:
                eng.wait_ge(book.sems[w[0]], w[1])
            if multi_dma:
                emit(eng, book.sems[tok[0]])
            else:
                last = emit(eng)
                if tok is not None:
                    assert last is not None
                    last.then_inc(book.sems[tok[0]], tok[2])

    def run_block(self, nc):
        with nc.Block() as blk:
            @blk.tensor
            def _(t):
                self.emit_engine('T', t)

            @blk.vector
            def _(v):
                self.emit_engine('V', v)

            @blk.scalar
            def _(s):
                self.emit_engine('S', s)

            @blk.gpsimd
            def _(g):
                self.emit_engine('G', g)

            @blk.sync
            def _(y):
                self.emit_engine('Y', y)


def tr8(e, pt, src, identb):
    last = None
    for j in range(8):
        last = e.transpose(pt[:, 128 * j:128 * (j + 1)],
                           src[:, 128 * j:128 * (j + 1)], identb[:])
    return last


def build_nc(debug=False):
    nc = bass.Bass("TRN2", target_bir_lowering=False)

    xin = nc.dram_tensor("xin", [TOK, D], F32, kind="ExternalInput").ap()
    out_d = nc.dram_tensor("out", [TOK, D], F32, kind="ExternalOutput").ap()
    o_scr = nc.dram_tensor("o_scr", [TOK, DV], BF16,
                           kind="ExternalOutput" if debug else "Internal").ap()
    wkT_scr = nc.dram_tensor("wkT_scr", [128, 8 * D], BF16).ap()
    wvT_scr = nc.dram_tensor("wvT_scr", [128, 8 * D], BF16).ap()
    h_scr = nc.dram_tensor("h_scr", [TOK, D], BF16).ap()

    wq_d = nc.dram_tensor("wq", [D, D], F32, kind="ExternalInput").ap()
    wk_d = nc.dram_tensor("wk", [D, D], F32, kind="ExternalInput").ap()
    wv_d = nc.dram_tensor("wv", [D, DV], F32, kind="ExternalInput").ap()
    wg_d = nc.dram_tensor("wg", [D, DV], F32, kind="ExternalInput").ap()
    wgk1_d = nc.dram_tensor("wgk1", [D, 16], F32, kind="ExternalInput").ap()
    wgk2_d = nc.dram_tensor("wgk2", [16, D], F32, kind="ExternalInput").ap()
    bgk2_d = nc.dram_tensor("bgk2", [1, D], F32, kind="ExternalInput").ap()
    wo_d = nc.dram_tensor("wo", [DV, D], F32, kind="ExternalInput").ap()
    gnw_d = nc.dram_tensor("gnw", [1, DVH], F32, kind="ExternalInput").ap()
    wr_d = nc.dram_tensor("wr", [D, D], F32, kind="ExternalInput").ap()
    wk2_d = nc.dram_tensor("wk2", [D, D], F32, kind="ExternalInput").ap()
    wv2_d = nc.dram_tensor("wv2", [D, D], F32, kind="ExternalInput").ap()
    mur_d = nc.dram_tensor("mur", [8, 128], F32, kind="ExternalInput").ap()
    muk_d = nc.dram_tensor("muk", [8, 128], F32, kind="ExternalInput").ap()
    cmod_d = nc.dram_tensor("cmod", [NSEQ, D], F32, kind="ExternalInput").ap()
    wmod_d = nc.dram_tensor("wmod", [D, 6 * D], F32, kind="ExternalInput").ap()

    identb_d = nc.dram_tensor("identb", [128, 128], BF16, kind="ExternalInput").ap()
    identf_d = nc.dram_tensor("identf", [128, 128], F32, kind="ExternalInput").ap()
    utri_d = nc.dram_tensor("utri", [128, 128], F32, kind="ExternalInput").ap()
    mtri_d = nc.dram_tensor("mtri", [128, 128], F32, kind="ExternalInput").ap()
    cmask_d = nc.dram_tensor("cmask", [128, 128], U8, kind="ExternalInput").ap()
    hones_d = nc.dram_tensor("hones", [128, 3], F32, kind="ExternalInput").ap()

    dbg = {}
    if debug:
        dbg['h'] = nc.dram_tensor("dbg_h", [TOK, D], BF16, kind="ExternalOutput").ap()
        dbg['sp'] = nc.dram_tensor("dbg_sp", [TOK, D], F32, kind="ExternalOutput").ap()
        dbg['mod'] = nc.dram_tensor("dbg_mod", [NSEQ, 6 * D], F32, kind="ExternalOutput").ap()
        for nm, w in [('eA', D), ('eK', D), ('qh', D), ('kh', D), ('qhfm', D),
                      ('qinfm', D), ('kinfm', D), ('Pm', 512)]:
            dbg[nm] = nc.dram_tensor("dbg_" + nm, [128, w], BF16, kind="ExternalOutput").ap()
        dbg['colv'] = nc.dram_tensor("dbg_colv", [128, 48], F32, kind="ExternalOutput").ap()

    with contextlib.ExitStack() as CTX:
        E = CTX.enter_context
        book = Book(nc, CTX)

        wg_s = E(nc.sbuf_tensor("wg_s", [128, 8 * DV], BF16))
        wrT_s = E(nc.sbuf_tensor("wrT_s", [128, 8 * D], BF16))
        identb = E(nc.sbuf_tensor("identb_s", [128, 128], BF16))
        identf = E(nc.sbuf_tensor("identf_s", [128, 128], F32))
        utri = E(nc.sbuf_tensor("utri_s", [128, 128], F32))
        mtri = E(nc.sbuf_tensor("mtri_s", [128, 128], F32))
        cmask = E(nc.sbuf_tensor("cmask_s", [128, 128], U8))
        zeros = E(nc.sbuf_tensor("zeros", [128, 128], F32))
        onesrow = E(nc.sbuf_tensor("onesrow", [1, 128], F32))
        onescol = E(nc.sbuf_tensor("onescol", [128, 1], F32))
        hones = E(nc.sbuf_tensor("hones_s", [128, 3], F32))
        cpool = E(nc.sbuf_tensor("cpool", [128, 10], F32))
        (C_NEG1, C_NLN16, C_NI16, C_PI16, C_NHALF, C_I512, C_EPS, C_I1024,
         C_ONE, C_SPCL) = range(10)
        CPV = [-1.0, -LN16, -1.0 / 16, 1.0 / 16, -0.5, 1.0 / 512, EPS,
               1.0 / 1024, 1.0, 80.0]

        def cp(i):
            return cpool[:, i:i + 1]

        mod_fm = E(nc.sbuf_tensor("mod_fm", [128, 96], F32))

        def modcol(v, dj, b):
            i = (v * 8 + dj) * 2 + b
            return mod_fm[:, i:i + 1]

        murcol = E(nc.sbuf_tensor("murcol", [128, 8], F32))
        mukcol = E(nc.sbuf_tensor("mukcol", [128, 8], F32))
        omrcol = E(nc.sbuf_tensor("omrcol", [128, 8], F32))
        omkcol = E(nc.sbuf_tensor("omkcol", [128, 8], F32))
        gnw_rep = E(nc.sbuf_tensor("gnw_rep", [128, DV], BF16))
        ga_rep = E(nc.sbuf_tensor("ga_rep", [128, 2 * D], BF16))
        gb_rep = E(nc.sbuf_tensor("gb_rep", [128, 2 * D], BF16))
        S_bf = E(nc.sbuf_tensor("S_bf", [128, H * 2 * DVH], BF16))

        def S_tile(h, j):
            o = (2 * h + j) * DVH
            return S_bf[:, o:o + DVH]

        def wsrc(dr, F, nch=8):
            return bass.AP(dr.tensor, 0, [[F, 128], [128 * F, nch], [1, F]])

        def wdst(sb, F, nch=8):
            return bass.AP(sb, 0, [[nch * F, 128], [F, nch], [1, F]])

        bw = {}

        with contextlib.ExitStack() as SA:
            ES = SA.enter_context
            wq_s = ES(nc.sbuf_tensor("wq_s", [128, 8 * D], BF16))
            wk_s = ES(nc.sbuf_tensor("wk_s", [128, 8 * D], BF16))
            wv_s = ES(nc.sbuf_tensor("wv_s", [128, 8 * DV], BF16))
            wgk1_s = ES(nc.sbuf_tensor("wgk1_s", [128, 8 * 16], BF16))
            wgk2b = ES(nc.sbuf_tensor("wgk2b", [17, D], F32))
            z1ones = ES(nc.sbuf_tensor("z1ones", [17, 128], F32))

            with contextlib.ExitStack() as ST:
                EE = ST.enter_context
                wmod_bf = EE(nc.sbuf_tensor("wmod_bf", [128, 8 * D], BF16))
                cmod_s = EE(nc.sbuf_tensor("cmod_s", [NSEQ, D], F32))
                sc_bf = EE(nc.sbuf_tensor("sc_bf", [NSEQ, D], BF16))
                scfm = EE(nc.sbuf_tensor("scfm", [128, 16], BF16))
                modsb = EE(nc.sbuf_tensor("modsb", [NSEQ, 6 * D], F32))
                mtmp = EE(nc.sbuf_tensor("mtmp", [NSEQ, D], F32))
                mtmp2 = EE(nc.sbuf_tensor("mtmp2", [NSEQ, D], F32))
                murow = EE(nc.sbuf_tensor("murow", [8, 128], F32))
                mukrow = EE(nc.sbuf_tensor("mukrow", [8, 128], F32))
                gnwrow = EE(nc.sbuf_tensor("gnwrow", [1, DVH], F32))
                brow = EE(nc.sbuf_tensor("brow", [1, D], F32))

                pmod = EE(nc.psum_tensor("pmod", [NSEQ, 1024], F32))
                ptr0 = EE(nc.psum_tensor("ptr0", [128, 1024], BF16))
                ptf0 = EE(nc.psum_tensor("ptf0", [128, 512], F32))

                P = Prog(book)

                def dma(dst, src):
                    return lambda e: e.dma_start(dst, src)

                t_con = None
                for dst, src in [(identb[:], identb_d), (identf[:], identf_d),
                                 (utri[:], utri_d), (mtri[:], mtri_d),
                                 (cmask[:], cmask_d),
                                 (wgk2b[0:16, :], wgk2_d),
                                 (wgk2b[16:17, :], bgk2_d),
                                 (cmod_s[:], cmod_d),
                                 (murow[:], mur_d), (mukrow[:], muk_d),
                                 (gnwrow[:], gnw_d), (hones[:], hones_d)]:
                    t_con = P.add('Y', dma(dst, src), inc='d_st', inc_k=16)

                t_wq = P.add('G', dma(wdst(wq_s, D), wsrc(wq_d, D)), inc='d_w', inc_k=16)
                t_wk = P.add('G', dma(wdst(wk_s, D), wsrc(wk_d, D)), inc='d_w', inc_k=16)
                t_wv = P.add('G', dma(wdst(wv_s, DV), wsrc(wv_d, DV)), inc='d_w', inc_k=16)
                t_wgk1 = P.add('G', dma(wdst(wgk1_s, 16), wsrc(wgk1_d, 16)), inc='d_w', inc_k=16)
                t_wg = P.add('G', dma(wdst(wg_s, DV), wsrc(wg_d, DV)), inc='d_w', inc_k=16)
                bw['wg'] = t_wg

                def ms(e):
                    e.memset(zeros[:], 0.0)
                    e.memset(onesrow[:], 1.0)
                    e.memset(onescol[:], 1.0)
                    e.memset(z1ones[:], 1.0)
                    last = e.memset(S_bf[:], 0.0)
                    for i, val in enumerate(CPV):
                        last = e.memset(cpool[:, i:i + 1], float(val))
                    return last
                t_ms = P.add('V', ms, inc='v_st')

                def silu_c(e):
                    e.activation(mtmp[:], cmod_s[:], AF.Exp, scale=cpool[0:NSEQ, C_NEG1:C_NEG1 + 1])
                    e.activation(mtmp[:], mtmp[:], AF.Ln, bias=cpool[0:NSEQ, C_ONE:C_ONE + 1])
                    return e.activation(mtmp[:], mtmp[:], AF.Exp, scale=cpool[0:NSEQ, C_NEG1:C_NEG1 + 1])
                t_sig = P.add('S', silu_c, waits=[t_con, t_ms], inc='s_st')

                t_sc = P.add('V', lambda e: e.tensor_tensor(sc_bf[:], cmod_s[:], mtmp[:], OP.mult),
                             waits=[t_sig], inc='v_st')

                def tr_sc(e):
                    last = None
                    for j in range(8):
                        last = e.transpose(ptr0[:, 2 * j:2 * j + 2],
                                           sc_bf[:, 128 * j:128 * j + 128],
                                           identb[0:NSEQ, 0:NSEQ])
                    return last
                t_trsc = P.add('T', tr_sc, waits=[t_sc], inc='t_st')
                t_evsc = P.add('V', lambda e: e.tensor_copy(scfm[:], ptr0[:, 0:16]),
                               waits=[t_trsc], inc='v_st')

                t_prev_mm = t_evsc
                t_wmld = P.add('G', lambda e: e.dma_start(
                    wdst(wmod_bf, D),
                    bass.AP(wmod_d.tensor, 0, [[6 * D, 128], [128 * 6 * D, 8], [1, D]])),
                    inc='d_w', inc_k=16)
                t_modsb = None
                for v in range(6):
                    def mm_mod(e):
                        last = None
                        for half in range(2):
                            for dj in range(8):
                                last = e.matmul(
                                    pmod[:, 512 * half:512 * (half + 1)],
                                    scfm[:, 2 * dj:2 * dj + 2],
                                    wmod_bf[:, D * dj + 512 * half: D * dj + 512 * (half + 1)],
                                    start=(dj == 0), stop=(dj == 7))
                        return last
                    t_mm = P.add('T', mm_mod, waits=[t_prev_mm, t_wmld], inc='t_st')
                    if v < 5:
                        t_wmld = P.add('G', lambda e, v=v: e.dma_start(
                            wdst(wmod_bf, D),
                            bass.AP(wmod_d.tensor, (v + 1) * D,
                                    [[6 * D, 128], [128 * 6 * D, 8], [1, D]])),
                            waits=[t_mm], inc='d_w', inc_k=16)
                    t_prev_mm = P.add('V', lambda e, v=v: e.tensor_copy(
                        modsb[:, v * D:(v + 1) * D], pmod[:]),
                        waits=[t_mm], inc='v_st')
                t_modsb = t_prev_mm

                def mod_xf(e):
                    e.tensor_single_scalar(modsb[:, 1 * D:2 * D], modsb[:, 1 * D:2 * D], 1.0, OP.add)
                    e.tensor_single_scalar(modsb[:, 4 * D:5 * D], modsb[:, 4 * D:5 * D], 1.0, OP.add)
                    last = None
                    for v in (2, 5):
                        sl = modsb[:, v * D:(v + 1) * D]
                        e.tensor_single_scalar(mtmp[:], sl, 0.0, OP.is_ge)
                        e.tensor_scalar(mtmp[:], mtmp[:], 2.0, -1.0, OP.mult, OP.add)
                        e.tensor_single_scalar(mtmp2[:], sl, -1.0, OP.mult)
                        e.tensor_tensor(mtmp2[:], mtmp2[:], sl, OP.max)
                        e.tensor_single_scalar(mtmp2[:], mtmp2[:], 0.25, OP.max)
                        last = e.tensor_tensor(sl, mtmp[:], mtmp2[:], OP.mult)
                    return last
                t_modxf = P.add('V', mod_xf, waits=[t_modsb], inc='v_st')
                if debug:
                    P.add('Y', dma(dbg['mod'], modsb[:]), waits=[t_modxf], inc='d_dbg', inc_k=16)

                t_prev = t_modxf
                for v in range(6):
                    def tr_mod(e, v=v):
                        last = None
                        for dj in range(8):
                            last = e.transpose(ptf0[:, 2 * dj:2 * dj + 2],
                                               modsb[:, v * D + 128 * dj: v * D + 128 * dj + 128],
                                               identf[0:NSEQ, 0:NSEQ])
                        return last
                    tm = P.add('T', tr_mod, waits=[t_prev], inc='t_st')
                    t_prev = P.add('V', lambda e, v=v: e.tensor_copy(
                        mod_fm[:, v * 16:v * 16 + 16], ptf0[:, 0:16]),
                        waits=[tm], inc='v_st')
                t_modfm = t_prev

                def tr_mu(e):
                    e.transpose(ptf0[:, 32:40], murow[:], identf[0:8, 0:8])
                    return e.transpose(ptf0[:, 40:48], mukrow[:], identf[0:8, 0:8])
                t_trmu = P.add('T', tr_mu, waits=[t_modfm], inc='t_st')

                def ev_mu(e):
                    e.tensor_copy(murcol[:], ptf0[:, 32:40])
                    e.tensor_copy(mukcol[:], ptf0[:, 40:48])
                    e.drain()
                    e.tensor_scalar(omrcol[:], murcol[:], -1.0, 1.0, OP.mult, OP.add)
                    return e.tensor_scalar(omkcol[:], mukcol[:], -1.0, 1.0, OP.mult, OP.add)
                t_evmu = P.add('V', ev_mu, waits=[t_trmu], inc='v_st')

                t_bg = P.add('T', lambda e: e.matmul(ptf0[:, 0:512], onesrow[:], gnwrow[:],
                                                     start=True, stop=True),
                             waits=[t_evmu], inc='t_st')

                def ev_gnw(e):
                    last = None
                    for r in range(H):
                        last = e.tensor_copy(gnw_rep[:, r * 512:(r + 1) * 512], ptf0[:, 0:512])
                    return last
                t_prevb = P.add('V', ev_gnw, waits=[t_bg], inc='v_st')

                for b in range(NSEQ):
                    for rep, v in ((ga_rep, 2), (gb_rep, 5)):
                        t_row = P.add('Y', lambda e, b=b, v=v: e.dma_start(
                            brow[:], modsb[b:b + 1, v * D:(v + 1) * D]),
                            waits=[t_prevb], inc='d_st', inc_k=16)
                        for half in range(2):
                            tb = P.add('T', lambda e, half=half: e.matmul(
                                ptf0[:, 0:512], onesrow[:],
                                brow[:, 512 * half:512 * (half + 1)],
                                start=True, stop=True),
                                waits=[t_prevb, t_row], inc='t_st')
                            t_prevb = P.add('V', lambda e, b=b, rep=rep, half=half: e.tensor_copy(
                                rep[:, b * D + 512 * half: b * D + 512 * (half + 1)],
                                ptf0[:, 0:512]),
                                waits=[tb], inc='v_st')

                P.run_block(nc)

            with contextlib.ExitStack() as AS:
                EA = AS.enter_context
                xa2 = EA(nc.sbuf_tensor("xa2", [128, 2 * D], F32))
                y_bf = EA(nc.sbuf_tensor("y_bf", [128, D], BF16))
                hraw = EA(nc.sbuf_tensor("hraw", [128, D], BF16))
                h_fm = EA(nc.sbuf_tensor("h_fm", [128, D], BF16))
                q_bf = EA(nc.sbuf_tensor("q_bf", [128, D], BF16))
                k_bf = EA(nc.sbuf_tensor("k_bf", [128, D], BF16))
                v_bf = EA(nc.sbuf_tensor("v_bf", [128, DV], BF16))
                z1_f = EA(nc.sbuf_tensor("z1_f", [128, 16], F32))
                sp_f = EA(nc.sbuf_tensor("sp_f", [128, D], F32))
                eA_bf = EA(nc.sbuf_tensor("eA_bf", [128, D], BF16))
                eK_bf = EA(nc.sbuf_tensor("eK_bf", [128, D], BF16))
                qh_tm = EA(nc.sbuf_tensor("qh_tm", [128, D], BF16))
                kh_tm = EA(nc.sbuf_tensor("kh_tm", [128, D], BF16))
                qh_fm = EA(nc.sbuf_tensor("qh_fm", [128, D], BF16))
                qin_fm = EA(nc.sbuf_tensor("qin_fm", [128, D], BF16))
                kin_fm = EA(nc.sbuf_tensor("kin_fm", [128, D], BF16))
                Pm = EA(nc.sbuf_tensor("Pm", [128, 512], BF16))
                on_sb = EA(nc.sbuf_tensor("on_sb", [128, DV], BF16))
                wtr_bf = EA(nc.sbuf_tensor("wtr_bf", [128, 8 * D], BF16))
                ln_t = EA(nc.sbuf_tensor("ln_t", [128, 8], F32))
                x2a = EA(nc.sbuf_tensor("x2a", [128, D], BF16))
                EbA = EA(nc.sbuf_tensor("EbA", [128, D], BF16))
                colv = EA(nc.sbuf_tensor("colv", [128, 48], F32))
                ss4 = EA(nc.sbuf_tensor("ss4", [128, 8], F32))

                P1 = EA(nc.psum_tensor("P1_A", [128, 1024], F32))
                P2 = EA(nc.psum_tensor("P2_A", [128, 1024], F32))
                P3 = EA(nc.psum_tensor("P3_A", [128, 1024], F32))
                P4 = EA(nc.psum_tensor("P4_A", [128, 512], F32))
                P5 = EA(nc.psum_tensor("P5_A", [128, 512], F32))
                pS = P3

                P = Prog(book)
                pv = {}

                def dma_tr8(src, dst):
                    sap = bass.AP(src, 0, [[1024, 128], [128, 8], [1, 128]])
                    dap = bass.AP(dst, 0, [[1024, 128], [128, 8], [1, 128]])
                    return lambda e: e.dma_start(dap, sap, transpose=True)

                pv['d_x'] = P.add('G', lambda e: e.dma_start(xa2[:, 0:D], xin[0:128, :]),
                                  inc='d_x', inc_k=16)
                pv['d_x1'] = P.add('G', lambda e: e.dma_start(xa2[:, D:2 * D], xin[128:256, :]),
                                   inc='d_x', inc_k=16)

                def trs_group(dst_s, cj0):
                    def f(e, sem):
                        for cj in (cj0, cj0 + 1):
                            for dj in range(8):
                                e.dma_start(
                                    dst_s[:, 1024 * cj + 128 * dj: 1024 * cj + 128 * dj + 128],
                                    wtr_bf[:, 1024 * dj + 128 * cj: 1024 * dj + 128 * cj + 128],
                                    transpose=True).then_inc(sem, 16)
                        return None
                    return f
                bjob = []
                bjob.append(('G', dma(wdst(wtr_bf, D), wsrc(wk2_d, D)), 16))
                for cj0 in range(0, 8, 2):
                    bjob.append(('S', trs_group(wrT_s, cj0), 16 * 16))
                bjob.append(('S', lambda e: e.dma_start(wkT_scr, wrT_s[:]), 16))
                bjob.append(('G', dma(wdst(wtr_bf, D), wsrc(wv2_d, D)), 16))
                for cj0 in range(0, 8, 2):
                    bjob.append(('S', trs_group(wrT_s, cj0), 16 * 16))
                bjob.append(('S', lambda e: e.dma_start(wvT_scr, wrT_s[:]), 16))
                bjob.append(('G', dma(wdst(wtr_bf, D), wsrc(wr_d, D)), 16))
                for cj0 in range(0, 8, 2):
                    bjob.append(('S', trs_group(wrT_s, cj0), 16 * 16))

                bstate = {'prev': None, 'idx': 0}

                def inject_bjob():
                    if bstate['idx'] >= len(bjob):
                        return
                    eng, emit, k = bjob[bstate['idx']]
                    sem = 'd_wt' if eng in ('Y', 'S') else 'd_w'
                    bstate['prev'] = P.add(eng, emit, waits=[bstate['prev']],
                                           inc=sem, inc_k=k, multi_dma=(k > 16))
                    bstate['idx'] += 1

                def emit_frontA(i):
                    b = i // NCH
                    xa = xa2[:, (i % 2) * D:(i % 2 + 1) * D]
                    t_x2 = P.add('S', lambda e, xa=xa: e.activation(x2a[:], xa, AF.Square),
                                 waits=[pv['d_x'], pv.get('vm2r')], inc='s_x2')
                    t_m = P.add('V', lambda e, xa=xa: e.tensor_reduce(
                        ln_t[:, 0:1], xa, AXX, OP.add), waits=[pv['d_x']], inc='v_m')
                    t_m2 = P.add('V', lambda e: e.tensor_reduce(ln_t[:, 2:3], x2a[:],
                                                                AXX, OP.add),
                                 waits=[t_x2], inc='v_m2r')
                    pv['vm2r'] = t_m2

                    def s_stats(e):
                        e.activation(ln_t[:, 1:2], ln_t[:, 0:1], AF.Identity, scale=cp(C_I1024))
                        e.activation(ln_t[:, 4:5], ln_t[:, 2:3], AF.Identity,
                                     scale=cp(C_I1024))
                        e.drain()
                        return e.activation(ln_t[:, 3:4], ln_t[:, 1:2], AF.Square)
                    t_st_ = P.add('S', s_stats, waits=[t_m, t_m2], inc='s_stats')

                    t_var = P.add('V', lambda e: e.tensor_tensor(
                        ln_t[:, 5:6], ln_t[:, 4:5], ln_t[:, 3:4], OP.subtract),
                        waits=[t_st_], inc='v_var')

                    def rstd(e):
                        e.activation(ln_t[:, 6:7], ln_t[:, 5:6], AF.Ln, bias=cp(C_EPS))
                        e.drain()
                        return e.activation(ln_t[:, 6:7], ln_t[:, 6:7], AF.Exp, scale=cp(C_NHALF))
                    t_rstd = P.add('S', rstd, waits=[t_var], inc='s_rstd')

                    t_y = P.add('V', lambda e, xa=xa: e.tensor_scalar(
                        y_bf[:], xa, ln_t[:, 1:2], ln_t[:, 6:7], OP.subtract, OP.mult),
                        waits=[t_rstd, pv.get('d_ty')], inc='v_y')

                    if i + 2 < NTOT:
                        pv['xq'].append(P.add(
                            'G', lambda e, r=128 * (i + 2), p=(i % 2): e.dma_start(
                                xa2[:, p * D:(p + 1) * D], xin[r:r + 128, :]),
                            waits=[t_y], inc='d_x', inc_k=16))

                    t_ty = P.add('Y', dma_tr8(y_bf, hraw),
                                 waits=[t_y, pv.get('s_h')], inc='d_ty', inc_k=16)
                    pv['d_ty'] = t_ty

                    def hev(e, b=b):
                        last = None
                        for j in range(8):
                            last = e.activation(h_fm[:, 128 * j:128 * (j + 1)],
                                                hraw[:, 128 * j:128 * (j + 1)], AF.Identity,
                                                scale=modcol(1, j, b), bias=modcol(0, j, b))
                        return last
                    t_h = P.add('S', hev, waits=[t_ty, pv.get('tvb'), pv.get('d_h')],
                                inc='s_h')
                    pv['s_h'] = t_h
                    pv['d_h'] = P.add('G', lambda e, row=128 * i: e.dma_start(
                        h_scr[row:row + 128, :], h_fm[:]), waits=[t_h],
                        inc='d_h', inc_k=16)

                def emit_gk(i):
                    def mm_z1(e):
                        last = None
                        for dj in range(8):
                            last = e.matmul(P4[:, 0:16], h_fm[:, 128 * dj:128 * (dj + 1)],
                                            wgk1_s[:, 16 * dj:16 * (dj + 1)],
                                            start=(dj == 0), stop=(dj == 7))
                        return last
                    t_z1 = P.add('T', mm_z1,
                                 waits=[pv.get('scols'), pv['s_h']] +
                                       ([t_wq] if i == 0 else []),
                                 inc='t_z1')
                    t_z1e = P.add('V', lambda e: e.tensor_copy(z1_f[:], P4[:, 0:16]),
                                  waits=[t_z1], inc='v_z1')
                    t_z1t = P.add('T', lambda e: e.transpose(P4[0:16, 128:256], z1_f[:],
                                                             identf[:]),
                                  waits=[t_z1e], inc='t_z1t')
                    t_z1te = P.add('V', lambda e: e.tensor_copy(z1ones[0:16, :],
                                                                P4[0:16, 128:256]),
                                   waits=[t_z1t], inc='v_z1t')

                    def mm_gkp(e):
                        e.matmul(P4[:, 0:512], z1ones[:], wgk2b[:, 0:512], start=True, stop=True)
                        return e.matmul(P5[:, 0:512], z1ones[:], wgk2b[:, 512:1024],
                                        start=True, stop=True)
                    t_gkp = P.add('T', mm_gkp, waits=[t_z1te, pv.get('vP5')], inc='t_gkp')

                    def s_E(e):
                        e.activation(EbA[:, 0:512], P4[:, 0:512], AF.Exp, scale=cp(C_NEG1))
                        return e.activation(EbA[:, 512:1024], P5[:, 0:512], AF.Exp,
                                            scale=cp(C_NEG1))
                    t_E = P.add('S', s_E, waits=[t_gkp], inc='s_E')
                    pv['sE'] = t_E
                    t_sp0 = P.add('S', lambda e: e.activation(sp_f[:], EbA[:], AF.Ln,
                                                              bias=cp(C_ONE)),
                                  waits=[pv.get('tcols'), pv.get('dbg_sp')], inc='s_sp0')
                    pv['sp0'] = t_sp0
                    pv['vP5'] = pv['sE']

                pv['xq'] = [pv['d_x1']]
                emit_frontA(0)
                emit_gk(0)
                for i in range(NTOT):
                    b = i // NCH
                    c = i % NCH
                    row = 128 * i

                    if c == 0:
                        pv['sinit'] = P.add('V', lambda e: e.memset(S_bf[:], 0.0),
                                            inc='v_sinit')
                    t_h = pv['s_h']

                    def mm_proj(dst, wsb, stride, base):
                        def f(e):
                            last = None
                            for half in range(2):
                                for dj in range(8):
                                    last = e.matmul(
                                        dst[:, 512 * half:512 * (half + 1)],
                                        h_fm[:, 128 * dj:128 * (dj + 1)],
                                        wsb[:, stride * dj + base + 512 * half:
                                            stride * dj + base + 512 * (half + 1)],
                                        start=(dj == 0), stop=(dj == 7))
                            return last
                        return f

                    t_q = P.add('T', mm_proj(P1, wq_s, D, 0),
                                waits=[t_h, pv.get('von')] + ([t_wq] if i == 0 else []),
                                inc='t_q')
                    t_k = P.add('T', mm_proj(P2, wk_s, D, 0),
                                waits=([t_wk] if i == 0 else []), inc='t_k')
                    t_va = P.add('T', mm_proj(P3, wv_s, DV, 0),
                                 waits=[pv.get('stt3')] + ([t_wv] if i == 0 else []),
                                 inc='t_va')

                    t_qe = P.add('V', lambda e: e.tensor_copy(q_bf[:], P1[:]),
                                 waits=[t_q], inc='v_q')
                    t_ke = P.add('V', lambda e: e.tensor_copy(k_bf[:], P2[:]),
                                 waits=[t_k], inc='v_k')
                    t_vae = P.add('V', lambda e: e.tensor_copy(v_bf[:, 0:1024], P3[:]),
                                  waits=[t_va], inc='v_va')
                    t_vb = P.add('T', mm_proj(P1, wv_s, DV, 1024), waits=[t_qe], inc='t_vb')
                    pv['tvb'] = t_vb
                    t_vbe = P.add('V', lambda e: e.tensor_copy(v_bf[:, 1024:2048], P1[:]),
                                  waits=[t_vb], inc='v_vb')

                    def mm_cum(dst, tri):
                        def f(e):
                            e.matmul(dst[:, 0:512], tri[:], sp_f[:, 0:512], start=True, stop=True)
                            return e.matmul(dst[:, 512:1024], tri[:], sp_f[:, 512:1024],
                                            start=True, stop=True)
                        return f
                    pv['spcl'] = P.add('V', lambda e: e.tensor_single_scalar(
                        sp_f[:], sp_f[:], 80.0, OP.min), waits=[pv['sp0']], inc='v_spcl')
                    t_cs = P.add('T', mm_cum(P1, utri), waits=[pv['spcl'], t_vbe], inc='t_cs')
                    t_t1 = P.add('T', mm_cum(P2, mtri), waits=[t_ke], inc='t_t1')

                    def mm_cols(e):
                        last = None
                        for j in range(8):
                            last = e.matmul(P4[:, 256 + 3 * j:259 + 3 * j],
                                            sp_f[:, 128 * j:128 * (j + 1)], hones[:],
                                            start=True, stop=True)
                        return last
                    t_cols = P.add('T', mm_cols, waits=[pv['spcl'], pv.get('sE')], inc='t_cols')
                    pv['tcols'] = t_cols

                    t_eA = P.add('S', lambda e: e.activation(eA_bf[:], P1[:], AF.Exp,
                                                             scale=cp(C_NI16), bias=cp(C_NLN16)),
                                 waits=[t_cs], inc='s_eA')
                    t_eK = P.add('S', lambda e: e.activation(eK_bf[:], P2[:], AF.Exp,
                                                             scale=cp(C_PI16)),
                                 waits=[t_t1], inc='s_eK')

                    spC_ap = bass.AP(P4, 256, [[512, 128], [3, 8]])
                    spm_ap = bass.AP(P4, 257, [[512, 128], [3, 8]])
                    sph_ap = bass.AP(P4, 258, [[512, 128], [3, 8]])

                    def v_colpre(e):
                        e.tensor_single_scalar(colv[:, 32:40], spm_ap, 552.0, OP.min)
                        return e.tensor_single_scalar(colv[:, 8:16], sph_ap, 552.0, OP.min)
                    t_vcp = P.add('V', v_colpre, waits=[t_cols], inc='v_colp')

                    def s_cols(e):
                        e.activation(colv[:, 0:8], spC_ap, AF.Exp, scale=cp(C_NI16))
                        e.activation(colv[:, 16:24], colv[:, 32:40], AF.Exp, scale=cp(C_PI16))
                        return e.activation(colv[:, 24:32], colv[:, 8:16], AF.Exp,
                                            scale=cp(C_PI16))
                    t_vc = P.add('S', s_cols, waits=[t_vcp], inc='s_cols')
                    pv['scols'] = t_vc

                    if i + 1 < NTOT:
                        pv['d_x'] = pv['xq'].pop(0)
                        emit_frontA(i + 1)
                        emit_gk(i + 1)

                    t_qh = P.add('V', lambda e: e.tensor_tensor(qh_tm[:], q_bf[:], eA_bf[:],
                                                                OP.mult),
                                 waits=[t_eA], inc='v_qh')
                    t_kh = P.add('V', lambda e: e.tensor_tensor(kh_tm[:], k_bf[:], eK_bf[:],
                                                                OP.mult),
                                 waits=[t_eK], inc='v_kh')

                    t_tq = P.add('Y', dma_tr8(qh_tm, qh_fm),
                                 waits=[t_qh, pv.get('tio3')], inc='d_tq', inc_k=16)
                    pv['d_tq'] = t_tq

                    def ev_tq(e):
                        last = None
                        for j in range(8):
                            last = e.tensor_single_scalar(qin_fm[:, 128 * j:128 * (j + 1)],
                                                          qh_fm[:, 128 * j:128 * (j + 1)],
                                                          colv[:, 16 + j:17 + j], OP.mult)
                        return last
                    t_tqe = P.add('V', ev_tq, waits=[t_tq, t_vc, pv.get('tP')], inc='v_tqe')

                    t_tk = P.add('Y', dma_tr8(kh_tm, kin_fm),
                                 waits=[t_kh, pv.get('tP')], inc='d_tk', inc_k=16)
                    pv['d_tkp'] = t_tk

                    def ev_tk(e):
                        last = None
                        for j in range(8):
                            last = e.tensor_single_scalar(kin_fm[:, 128 * j:128 * (j + 1)],
                                                          kin_fm[:, 128 * j:128 * (j + 1)],
                                                          colv[:, 24 + j:25 + j], OP.mult)
                        return last
                    t_tke = P.add('V', ev_tk, waits=[t_tk, t_vc], inc='v_tke')

                    def mm_state(h):
                        def f(e):
                            last = None
                            for j in range(2):
                                last = e.matmul(pS[:, 512 * j:512 * (j + 1)],
                                                kh_tm[:, 256 * h + 128 * j:
                                                      256 * h + 128 * (j + 1)],
                                                v_bf[:, 512 * h:512 * (h + 1)],
                                                start=True, stop=True)
                            return last
                        return f

                    t_st = [None] * 4
                    t_st[0] = P.add('T', mm_state(0), waits=[t_kh, t_vae], inc='t_s0')

                    ps = [P1[:, 0:128], P1[:, 512:640], P2[:, 0:128], P2[:, 512:640]]
                    po = [P1[:, 0:512], P1[:, 512:1024], P2[:, 0:512], P2[:, 512:1024]]

                    def mm_P(e):
                        last = None
                        for h in range(4):
                            for j in range(2):
                                dblk = 2 * h + j
                                last = e.matmul(ps[h],
                                                kin_fm[:, 128 * dblk:128 * (dblk + 1)],
                                                qin_fm[:, 128 * dblk:128 * (dblk + 1)],
                                                start=(j == 0), stop=(j == 1))
                        return last
                    t_P = P.add('T', mm_P, waits=[t_tke, t_tqe, t_eA, t_eK], inc='t_P')
                    pv['tP'] = t_P

                    def v_sel(e):
                        last = None
                        for h in range(4):
                            last = e.select(Pm[:, 128 * h:128 * (h + 1)], cmask[:], ps[h],
                                            zeros[:])
                        return last
                    t_Pm = P.add('V', v_sel, waits=[t_P], inc='v_Pm')

                    t_io = [None] * 4
                    t_stt = [None] * 4
                    for h in range(4):
                        def mm_io(e, h=h):
                            e.matmul(po[h], Pm[:, 128 * h:128 * (h + 1)],
                                     v_bf[:, 512 * h:512 * (h + 1)], start=True, stop=False)
                            last = None
                            for j in range(2):
                                last = e.matmul(po[h],
                                                qh_fm[:, 128 * (2 * h + j):128 * (2 * h + j + 1)],
                                                S_tile(h, j), start=False, stop=(j == 1))
                            return last
                        io_w = []
                        if h == 0:
                            io_w = [t_Pm, pv.get('stt3'), t_tq]
                            if c == 0:
                                io_w.append(pv['sinit'])
                        t_io[h] = P.add('T', mm_io, waits=io_w, inc=f'tio{h}')

                        def v_stt(e, h=h):
                            last = None
                            for j in range(2):
                                last = e.scalar_tensor_tensor(
                                    S_tile(h, j), S_tile(h, j), colv[:, 2 * h + j:2 * h + j + 1],
                                    pS[:, 512 * j:512 * (j + 1)], OP.mult, OP.add)
                            return last
                        t_stt[h] = P.add('V', v_stt, waits=[t_st[h], t_io[h], t_vc],
                                         inc=f'vstt{h}')
                        if h < 3:
                            t_st[h + 1] = P.add('T', mm_state(h + 1),
                                                waits=[t_stt[h]], inc=f'ts{h + 1}')
                    pv['stt3'] = t_stt[3]
                    pv['tst3'] = t_st[3]
                    pv['tio3'] = t_io[3]

                    def s_rms(e):
                        for h in range(4):
                            e.activation(qh_tm[:, 0:512], po[h], AF.Square,
                                         accum_out=ss4[:, h:h + 1])
                        e.drain()
                        e.activation(ss4[:, 4:8], ss4[:, 0:4], AF.Ln, scale=cp(C_I512),
                                     bias=cp(C_EPS))
                        e.drain()
                        return e.activation(ss4[:, 4:8], ss4[:, 4:8], AF.Exp, scale=cp(C_NHALF))
                    t_rms = P.add('S', s_rms, waits=[t_io[3]], inc='s_rms')

                    def v_on(e):
                        last = None
                        for h in range(4):
                            last = e.tensor_single_scalar(on_sb[:, 512 * h:512 * (h + 1)],
                                                          po[h], ss4[:, 4 + h:5 + h], OP.mult)
                        return last
                    t_on = P.add('V', v_on, waits=[t_rms, pv.get('d_on')], inc='v_on')
                    pv['von'] = t_on

                    pv['d_on'] = P.add('G', lambda e, row=row: e.dma_start(
                        o_scr[row:row + 128, :], on_sb[:]), waits=[t_on],
                        inc='d_on', inc_k=16)

                    if debug:
                        pv['dbg_h'] = P.add('Y', lambda e, row=row: e.dma_start(
                            dbg['h'][row:row + 128, :], h_fm[:]), waits=[t_h],
                            inc='d_dbg', inc_k=16)
                        pv['dbg_sp'] = P.add('Y', lambda e, row=row: e.dma_start(
                            dbg['sp'][row:row + 128, :], sp_f[:]),
                            waits=[t_spcl], inc='d_dbg', inc_k=16)

                    inject_bjob()

                assert bstate['idx'] == len(bjob)
                bw['wrT'] = bstate['prev']
                P.add('G', lambda e: e.nop(), waits=[pv['d_on'], pv['d_h']])
                P.run_block(nc)


        with contextlib.ExitStack() as BS:
            EB = BS.enter_context
            wo_s = EB(nc.sbuf_tensor("wo_s", [128, 16 * D], BF16))
            wkT_s = EB(nc.sbuf_tensor("wkT_s", [128, 8 * D], BF16))
            wvT_s = EB(nc.sbuf_tensor("wvT_s", [128, 8 * D], BF16))

            xb2 = EB(nc.sbuf_tensor("xb2", [128, 2 * D], F32))
            onb = EB(nc.sbuf_tensor("onb", [128, 2 * DV], BF16))
            y_b = EB(nc.sbuf_tensor("y_b", [128, D], BF16))
            hraw2b = EB(nc.sbuf_tensor("hraw2b", [128, D], BF16))
            hB2 = EB(nc.sbuf_tensor("hB2", [128, 2 * D], BF16))
            h2_fm = EB(nc.sbuf_tensor("h2_fm", [128, D], BF16))
            h2s_fm = EB(nc.sbuf_tensor("h2s_fm", [128, D], BF16))
            carry = EB(nc.sbuf_tensor("carry", [128, 8], BF16))
            g_bf = EB(nc.sbuf_tensor("g_bf", [128, DV], BF16))
            Ebuf = EB(nc.sbuf_tensor("Ebuf", [128, DV], BF16))
            Fbuf = EB(nc.sbuf_tensor("Fbuf", [128, 2 * DV], BF16))
            og_fm = EB(nc.sbuf_tensor("og_fm", [128, DV], BF16))
            tr_bf = EB(nc.sbuf_tensor("tr_bf", [128, D], BF16))
            xnew = EB(nc.sbuf_tensor("xnew", [128, 2 * D], F32))
            xr_fm = EB(nc.sbuf_tensor("xr_fm", [128, D], BF16))
            xk_fm = EB(nc.sbuf_tensor("xk_fm", [128, D], BF16))
            t2r = EB(nc.sbuf_tensor("t2r", [128, D], BF16))
            ka_bf = EB(nc.sbuf_tensor("ka_bf", [128, D], BF16))
            kaf = EB(nc.sbuf_tensor("kaf", [128, D], BF16))
            rr_bf = t2r
            f_bf = ka_bf
            x2B = EB(nc.sbuf_tensor("x2B", [128, D], BF16))
            ln2B = EB(nc.sbuf_tensor("ln2B", [128, 8], F32))

            Q1 = EB(nc.psum_tensor("Q1_B", [128, 1024], F32))
            Q2 = EB(nc.psum_tensor("Q2_B", [128, 1024], F32))
            Q3 = EB(nc.psum_tensor("Q3_B", [128, 1024], F32))
            Q4 = EB(nc.psum_tensor("Q4_B", [128, 1024], F32))

            P = Prog(book)
            pv = {}

            def dma_trn(src, dst, n):
                w = 128 * n
                sap = bass.AP(src, 0, [[w, 128], [128, n], [1, 128]])
                dap = bass.AP(dst, 0, [[w, 128], [128, n], [1, 128]])
                return lambda e: e.dma_start(dap, sap, transpose=True)

            t_wo = P.add('G', lambda e: e.dma_start(wdst(wo_s, D, 16), wsrc(wo_d, D, 16)),
                         inc='d_w', inc_k=16)
            t_wkT = P.add('G', lambda e: e.dma_start(wkT_s[:], wkT_scr),
                          inc='d_w', inc_k=16)
            t_wvT = P.add('G', lambda e: e.dma_start(wvT_s[:], wvT_scr),
                          inc='d_w', inc_k=16)

            pv['d_x0'] = P.add('G', lambda e: e.dma_start(xb2[:, 0:D], xin[0:128, :]),
                               inc='d_x2', inc_k=16)
            pv['d_x1'] = P.add('G', lambda e: e.dma_start(xb2[:, D:2 * D], xin[128:256, :]),
                               inc='d_x2', inc_k=16)
            pv['d_o0'] = P.add('G', lambda e: e.dma_start(onb[:, 0:DV], o_scr[0:128, :]),
                               inc='d_on2', inc_k=16)
            pv['d_hB0'] = P.add('G', lambda e: e.dma_start(hB2[:, 0:D], h_scr[0:128, :]),
                                inc='d_hB', inc_k=16)

            def emit_front(i):
                b = i // NCH
                hcur = hB2[:, (i % 2) * D:(i % 2 + 1) * D]
                d_h_tok = pv[f'd_hB{i}'] if i else pv['d_hB0']
                if i + 1 < NTOT:
                    pv[f'd_hB{i + 1}'] = P.add(
                        'G', lambda e, r=128 * (i + 1), nb=(i + 1) % 2: e.dma_start(
                            hB2[:, nb * D:(nb + 1) * D], h_scr[r:r + 128, :]),
                        waits=[pv.get('tgB')], inc='d_hB', inc_k=16)

                def mm_g(dst, base):
                    def f(e):
                        last = None
                        for half in range(2):
                            for dj in range(8):
                                last = e.matmul(dst[:, 512 * half:512 * (half + 1)],
                                                hcur[:, 128 * dj:128 * (dj + 1)],
                                                wg_s[:, DV * dj + base + 512 * half:
                                                     DV * dj + base + 512 * (half + 1)],
                                                start=(dj == 0), stop=(dj == 7))
                        return last
                    return f
                t_gA = P.add('T', mm_g(Q4, 0),
                             waits=[d_h_tok, pv.get('vgA')] + ([bw['wg']] if i == 0 else []),
                             inc='t_gA')
                t_gB = P.add('T', mm_g(Q2, 1024),
                             waits=[pv.get('skr') if i >= 2 else pv.get('vgB')], inc='t_gB')
                pv['tgB'] = t_gB

                t_gAe = P.add('V', lambda e: e.tensor_copy(g_bf[:, 0:1024], Q4[:]),
                              waits=[t_gA], inc='v_gA')
                pv['vgA'] = t_gAe
                t_gBe = P.add('V', lambda e: e.tensor_copy(g_bf[:, 1024:2048], Q2[:]),
                              waits=[t_gB], inc='v_gB')
                pv['vgB'] = t_gBe

                def sg(e):
                    e.activation(Ebuf[:], g_bf[:], AF.Exp, scale=cp(C_NEG1))
                    e.activation(Ebuf[:], Ebuf[:], AF.Ln, bias=cp(C_ONE))
                    return e.activation(Ebuf[:], Ebuf[:], AF.Exp, scale=cp(C_NEG1))
                t_sg = P.add('S', sg, waits=[t_gAe, t_gBe], inc='s_sg')

                def fchain(e, i=i):
                    fb = Fbuf[:, (i % 2) * DV:(i % 2 + 1) * DV]
                    e.tensor_tensor(fb, g_bf[:], Ebuf[:], OP.mult)
                    return e.tensor_tensor(fb, fb, gnw_rep[:], OP.mult)
                t_F = P.add('V', fchain, waits=[t_sg, pv.get('vog2'), pv.get('vf')],
                            inc='v_F')
                pv[f'vF{i}'] = t_F

            def emit_back(i):
                b = i // NCH
                c = i % NCH
                row = 128 * i
                pb = i % 2
                xcur = xb2[:, pb * D:(pb + 1) * D]
                ocur = onb[:, pb * DV:(pb + 1) * DV]
                xncur = xnew[:, pb * D:(pb + 1) * D]
                d_o_tok = pv[f'd_o{i}'] if i else pv['d_o0']

                fslot = Fbuf[:, (i % 2) * DV:(i % 2 + 1) * DV]
                t_og = P.add('V', lambda e, oc=ocur, fs=fslot: e.tensor_tensor(
                    fs, oc, fs, OP.mult),
                    waits=[d_o_tok, pv[f'vF{i}'], pv.get('dogT')], inc='v_og')
                pv['vog2'] = t_og

                if i + 1 < NTOT:
                    nb = (i + 1) % 2
                    pv[f'd_o{i + 1}'] = P.add('G', lambda e, r=row + 128, nb=nb: e.dma_start(
                        onb[:, nb * DV:(nb + 1) * DV], o_scr[r:r + 128, :]),
                        waits=[t_og], inc='d_on2', inc_k=16)

                w16 = 2048
                sap16 = bass.AP(Fbuf, (i % 2) * DV, [[2 * DV, 128], [128, 16], [1, 128]])
                dap16 = bass.AP(og_fm, 0, [[DV, 128], [128, 16], [1, 128]])
                t_ogT = P.add('Y', lambda e, sap=sap16, dap=dap16: e.dma_start(
                    dap, sap, transpose=True),
                    waits=[t_og, pv.get('two')], inc='d_ogT', inc_k=16)
                pv['dogT'] = t_ogT

                def mm_wo(e):
                    last = None
                    for half in range(2):
                        for vj in range(16):
                            last = e.matmul(Q3[:, 512 * half:512 * (half + 1)],
                                            og_fm[:, 128 * vj:128 * (vj + 1)],
                                            wo_s[:, D * vj + 512 * half:
                                                 D * vj + 512 * (half + 1)],
                                            start=(vj == 0), stop=(vj == 15))
                    return last
                t_wom = P.add('T', mm_wo,
                              waits=[t_ogT, pv.get('vf')] + ([t_wo] if i == 0 else []),
                              inc='t_wo')
                pv['two'] = t_wom

                def v_xn(e, b=b, xc=xcur, xn=xncur):
                    e.tensor_tensor(tr_bf[:], Q3[:], ga_rep[:, b * D:(b + 1) * D], OP.mult)
                    return e.tensor_tensor(xn, tr_bf[:], xc, OP.add)
                t_xn = P.add('V', v_xn,
                             waits=[t_wom, pv.get(f'd_out{i - 2}'),
                                    pv[f'd_x{i}'] if i else pv['d_x0']], inc='v_xn')
                if i + 2 < NTOT:
                    pv[f'd_x{i + 2}'] = P.add('G', lambda e, r=128 * (i + 2), nb=i % 2: e.dma_start(
                        xb2[:, nb * D:(nb + 1) * D], xin[r:r + 128, :]),
                        waits=[t_xn], inc='d_x2', inc_k=16)

                t_x22 = P.add('S', lambda e, xn=xncur: e.activation(x2B[:], xn, AF.Square),
                              waits=[t_xn], inc='s_x2B2')
                t_m2 = P.add('V', lambda e, xn=xncur: e.tensor_reduce(
                    ln2B[:, 0:1], xn, AXX, OP.add), waits=[t_xn], inc='v_m2')
                t_m2b = P.add('V', lambda e: e.tensor_reduce(ln2B[:, 2:3], x2B[:], AXX, OP.add),
                              waits=[t_x22], inc='v_m2b')

                def s_stats2(e):
                    e.activation(ln2B[:, 1:2], ln2B[:, 0:1], AF.Identity, scale=cp(C_I1024))
                    e.activation(ln2B[:, 4:5], ln2B[:, 2:3], AF.Identity,
                                 scale=cp(C_I1024))
                    e.drain()
                    return e.activation(ln2B[:, 3:4], ln2B[:, 1:2], AF.Square)
                t_st2 = P.add('S', s_stats2, waits=[t_m2, t_m2b], inc='s_stats2')
                t_var2 = P.add('V', lambda e: e.tensor_tensor(
                    ln2B[:, 5:6], ln2B[:, 4:5], ln2B[:, 3:4], OP.subtract),
                    waits=[t_st2], inc='v_var2')

                def rstd2(e):
                    e.activation(ln2B[:, 6:7], ln2B[:, 5:6], AF.Ln, bias=cp(C_EPS))
                    e.drain()
                    return e.activation(ln2B[:, 6:7], ln2B[:, 6:7], AF.Exp, scale=cp(C_NHALF))
                t_rstd2 = P.add('S', rstd2, waits=[t_var2], inc='s_rstd2')

                t_y2b = P.add('V', lambda e, xn=xncur: e.tensor_scalar(
                    y_b[:], xn, ln2B[:, 1:2], ln2B[:, 6:7], OP.subtract, OP.mult),
                    waits=[t_rstd2, pv.get('d_ty2')], inc='v_y2b')

                t_ty2 = P.add('Y', dma_trn(y_b, hraw2b, 8),
                              waits=[t_y2b, pv.get('s_h2Blast')], inc='d_ty2B', inc_k=16)
                pv['d_ty2'] = t_ty2

                def h2ev(e, b=b):
                    last = None
                    for j in range(8):
                        last = e.activation(h2_fm[:, 128 * j:128 * (j + 1)],
                                            hraw2b[:, 128 * j:128 * (j + 1)], AF.Identity,
                                            scale=modcol(4, j, b), bias=modcol(3, j, b))
                    return last
                t_h2 = P.add('S', h2ev, waits=[t_ty2], inc='s_h2B')
                pv['s_h2Blast'] = t_h2

                def shift(e, c=c):
                    if c == 0:
                        e.memset(carry[:], 0.0)
                    e.tensor_copy(bass.AP(h2s_fm, 0, [[D, 128], [128, 8]]),
                                  bass.AP(carry, 0, [[8, 128], [1, 8]]))
                    e.tensor_copy(bass.AP(h2s_fm, 1, [[D, 128], [128, 8], [1, 127]]),
                                  bass.AP(h2_fm, 0, [[D, 128], [128, 8], [1, 127]]))
                    return e.tensor_copy(carry[:],
                                         bass.AP(h2_fm, 127, [[D, 128], [128, 8]]))
                t_sh = P.add('V', shift, waits=[t_h2], inc='v_sh')

                def s_lerp_r(e):
                    last = None
                    for j in range(8):
                        last = e.activation(t2r[:, 128 * j:128 * (j + 1)],
                                            h2s_fm[:, 128 * j:128 * (j + 1)], AF.Identity,
                                            scale=omrcol[:, j:j + 1])
                    return last
                t_lpr = P.add('S', s_lerp_r, waits=[t_sh], inc='s_lerp')

                def v_xr(e):
                    last = None
                    for j in range(8):
                        last = e.scalar_tensor_tensor(
                            xr_fm[:, 128 * j:128 * (j + 1)], h2_fm[:, 128 * j:128 * (j + 1)],
                            murcol[:, j:j + 1], t2r[:, 128 * j:128 * (j + 1)], OP.mult, OP.add)
                    return last
                t_xr = P.add('V', v_xr, waits=[t_lpr], inc='v_xr')

                def s_lerp_k(e):
                    last = None
                    for j in range(8):
                        last = e.activation(t2r[:, 128 * j:128 * (j + 1)],
                                            h2s_fm[:, 128 * j:128 * (j + 1)], AF.Identity,
                                            scale=omkcol[:, j:j + 1])
                    return last
                t_lpk = P.add('S', s_lerp_k, waits=[t_xr], inc='s_lerpk')

                def v_xk(e):
                    last = None
                    for j in range(8):
                        last = e.scalar_tensor_tensor(
                            xk_fm[:, 128 * j:128 * (j + 1)], h2_fm[:, 128 * j:128 * (j + 1)],
                            mukcol[:, j:j + 1], t2r[:, 128 * j:128 * (j + 1)], OP.mult, OP.add)
                    return last
                t_xk = P.add('V', v_xk, waits=[t_lpk], inc='v_xk')

                def mm_ffn(dst, xsb, wsb):
                    def f(e):
                        last = None
                        for half in range(2):
                            for cj in range(8):
                                last = e.matmul(dst[:, 512 * half:512 * (half + 1)],
                                                xsb[:, 128 * cj:128 * (cj + 1)],
                                                wsb[:, D * cj + 512 * half:
                                                     D * cj + 512 * (half + 1)],
                                                start=(cj == 0), stop=(cj == 7))
                        return last
                    return f
                t_rp = P.add('T', mm_ffn(Q1, xr_fm, wrT_s),
                             waits=[t_xr, pv.get('sE2')] + ([bw['wrT'], t_wkT] if i == 0 else []),
                             inc='t_rp')

                t_E2 = P.add('S', lambda e: e.activation(Ebuf[:, 0:1024], Q1[:], AF.Exp,
                                                         scale=cp(C_NEG1)),
                             waits=[t_rp], inc='s_E2')
                pv['sE2'] = t_E2

                def rr2(e):
                    e.activation(Ebuf[:, 0:1024], Ebuf[:, 0:1024], AF.Ln, bias=cp(C_ONE))
                    return e.activation(rr_bf[:], Ebuf[:, 0:1024], AF.Exp, scale=cp(C_NEG1))
                t_rr = P.add('S', rr2, waits=[t_xk], inc='s_rr')

                t_kp = P.add('T', mm_ffn(Q2, xk_fm, wkT_s),
                             waits=[t_xk, pv.get('vgB')] + ([t_wkT] if i == 0 else []),
                             inc='t_kp')
                t_kr = P.add('S', lambda e: e.activation(x2B[:], Q2[:], AF.Relu),
                             waits=[t_kp], inc='s_kr')
                pv['skr'] = t_kr
                t_ka = P.add('V', lambda e: e.tensor_tensor(ka_bf[:], x2B[:], x2B[:], OP.mult),
                             waits=[t_kr, pv.get('dkaT')], inc='v_ka')
                pv['vka'] = t_ka

                t_kaT = P.add('Y', dma_trn(ka_bf, kaf, 8),
                              waits=[t_ka, pv.get('tvo')], inc='d_kaT', inc_k=16)
                pv['dkaT'] = t_kaT

                t_vo = P.add('T', mm_ffn(Q3, kaf, wvT_s),
                             waits=[t_kaT, t_xn] + ([t_wvT] if i == 0 else []), inc='t_vo')
                pv['tvo'] = t_vo

                def v_f(e, b=b, xn=xncur, i=i):
                    t3 = Fbuf[:, (i % 2) * DV:(i % 2) * DV + 1024]
                    e.tensor_tensor(f_bf[:], Q3[:], rr_bf[:], OP.mult)
                    e.tensor_tensor(t3, f_bf[:], gb_rep[:, b * D:(b + 1) * D], OP.mult)
                    return e.tensor_tensor(xn, t3, xn, OP.add)
                t_f = P.add('V', v_f, waits=[t_vo, t_rr], inc='v_f')
                pv['vf'] = t_f

                pv[f'd_out{i}'] = P.add('G', lambda e, row=row, xn=xncur: e.dma_start(
                    out_d[row:row + 128, :], xn), waits=[t_f], inc='d_out', inc_k=16)

            for i in range(NTOT):
                emit_front(i)
                if i >= 1:
                    emit_back(i - 1)
            emit_back(NTOT - 1)

            P.add('G', lambda e: e.nop(), waits=[pv[f'd_out{NTOT - 1}']])
            P.run_block(nc)


    return nc


def _consts():
    ib = np.eye(128, dtype=ml_dtypes.bfloat16)
    iff = np.eye(128, dtype=np.float32)
    s = np.arange(128)
    ut = (s[:, None] <= s[None, :]).astype(np.float32)
    mt = -(s[:, None] > s[None, :]).astype(np.float32)
    cm = (s[:, None] <= s[None, :]).astype(np.uint8)
    ho = np.zeros((128, 3), np.float32)
    ho[:, 0] = 1.0
    ho[:64, 1] = 1.0
    ho[64:, 2] = 1.0
    return ib, iff, ut, mt, cm, ho


def kernel(L0_attn_x, L0_attn_Wq, L0_attn_Wk, L0_attn_Wv, L0_attn_Wg,
           L0_attn_Wgk1, L0_attn_Wgk2, L0_attn_bgk2, L0_attn_Wo,
           L0_attn_g_norm_weight, L0_ffn_W_r, L0_ffn_W_k, L0_ffn_W_v,
           L0_ffn_mu_r, L0_ffn_mu_k, L0_c, L0_wmod):
    x = np.ascontiguousarray(np.asarray(L0_attn_x, dtype=np.float32))
    ib, iff, ut, mt, cm, ho = _consts()

    common = dict(
        wq=np.ascontiguousarray(L0_attn_Wq, np.float32),
        wk=np.ascontiguousarray(L0_attn_Wk, np.float32),
        wv=np.ascontiguousarray(L0_attn_Wv, np.float32),
        wg=np.ascontiguousarray(L0_attn_Wg, np.float32),
        wgk1=np.ascontiguousarray(L0_attn_Wgk1, np.float32),
        wgk2=np.ascontiguousarray(L0_attn_Wgk2, np.float32),
        bgk2=np.ascontiguousarray(L0_attn_bgk2, np.float32).reshape(1, D),
        wo=np.ascontiguousarray(L0_attn_Wo, np.float32),
        gnw=np.ascontiguousarray(L0_attn_g_norm_weight, np.float32).reshape(1, DVH),
        wr=np.ascontiguousarray(L0_ffn_W_r, np.float32),
        wk2=np.ascontiguousarray(L0_ffn_W_k, np.float32),
        wv2=np.ascontiguousarray(L0_ffn_W_v, np.float32),
        mur=np.ascontiguousarray(L0_ffn_mu_r, np.float32).reshape(8, 128),
        muk=np.ascontiguousarray(L0_ffn_mu_k, np.float32).reshape(8, 128),
        wmod=np.ascontiguousarray(L0_wmod, np.float32),
        identb=ib, identf=iff, utri=ut, mtri=mt, cmask=cm, hones=ho,
    )
    cmod = np.ascontiguousarray(L0_c, np.float32)

    nc = build_nc(debug=DEBUG)
    in_maps = []
    for core in range(NCORE):
        m = dict(common)
        m["xin"] = np.ascontiguousarray(x[2 * core:2 * core + 2].reshape(TOK, D))
        m["cmod"] = np.ascontiguousarray(cmod[2 * core:2 * core + 2])
        in_maps.append(m)

    res = run_bass_kernel_spmd(nc, in_maps, core_ids=list(range(NCORE)))
    outs = [np.asarray(r["out"]).reshape(NSEQ, N, D) for r in res.results]
    return np.concatenate(outs, axis=0)




# version 10
# speedup vs baseline: 1.2557x; 1.0722x over previous
import contextlib

import numpy as np
import ml_dtypes

import concourse.bass as bass
import concourse.mybir as mybir
from concourse.bass_utils import run_bass_kernel_spmd

F32 = mybir.dt.float32
BF16 = mybir.dt.bfloat16
U8 = mybir.dt.uint8
AF = mybir.ActivationFunctionType
OP = mybir.AluOpType
AXX = mybir.AxisListType.X

NCORE = 8
NSEQ = 2
N = 2048
D = 1024
H = 4
DVH = 512
DV = H * DVH
C = 128
NCH = N // C
NTOT = NSEQ * NCH
TOK = NSEQ * N

LN16 = float(np.log(16.0))
EPS = 1e-5
CLAMP_E = 1e15

DEBUG = False


class Book:
    def __init__(self, nc, ctx):
        self.nc, self.ctx = nc, ctx
        self.sems = {}
        self.counts = {}

    @staticmethod
    def _key(name):
        if name.startswith('d_'):
            return name
        if name[0] == 't':
            return 'ET'
        if name[0] == 'v':
            return 'EV'
        return 'ES'

    def get(self, name):
        key = self._key(name)
        if key not in self.sems:
            self.sems[key] = self.ctx.enter_context(self.nc.semaphore(key))
            self.counts[key] = 0
        return self.sems[key]

    def tok(self, name, k=1):
        self.get(name)
        key = self._key(name)
        self.counts[key] += k
        return (key, self.counts[key], k)


class Prog:

    def __init__(self, book):
        self.book = book
        self.stages = []

    def add(self, eng, emit, waits=(), inc=None, inc_k=1, multi_dma=False):
        tok = self.book.tok(inc, inc_k) if inc is not None else None
        ws = [w for w in waits if w is not None]
        self.stages.append((eng, emit, ws, tok, multi_dma))
        return tok

    def emit_engine(self, eng_name, eng):
        book = self.book
        for e, emit, waits, tok, multi_dma in self.stages:
            if e != eng_name:
                continue
            for w in waits:
                eng.wait_ge(book.sems[w[0]], w[1])
            if multi_dma:
                emit(eng, book.sems[tok[0]])
            else:
                last = emit(eng)
                if tok is not None:
                    assert last is not None
                    last.then_inc(book.sems[tok[0]], tok[2])

    def run_block(self, nc):
        with nc.Block() as blk:
            @blk.tensor
            def _(t):
                self.emit_engine('T', t)

            @blk.vector
            def _(v):
                self.emit_engine('V', v)

            @blk.scalar
            def _(s):
                self.emit_engine('S', s)

            @blk.gpsimd
            def _(g):
                self.emit_engine('G', g)

            @blk.sync
            def _(y):
                self.emit_engine('Y', y)


def tr8(e, pt, src, identb):
    last = None
    for j in range(8):
        last = e.transpose(pt[:, 128 * j:128 * (j + 1)],
                           src[:, 128 * j:128 * (j + 1)], identb[:])
    return last


def build_nc(debug=False):
    nc = bass.Bass("TRN2", target_bir_lowering=False)

    xin = nc.dram_tensor("xin", [TOK, D], F32, kind="ExternalInput").ap()
    out_d = nc.dram_tensor("out", [TOK, D], F32, kind="ExternalOutput").ap()
    o_scr = nc.dram_tensor("o_scr", [TOK, DV], BF16,
                           kind="ExternalOutput" if debug else "Internal").ap()
    wkT_scr = nc.dram_tensor("wkT_scr", [128, 8 * D], BF16).ap()
    wvT_scr = nc.dram_tensor("wvT_scr", [128, 8 * D], BF16).ap()
    h_scr = nc.dram_tensor("h_scr", [TOK, D], BF16).ap()

    wq_d = nc.dram_tensor("wq", [D, D], F32, kind="ExternalInput").ap()
    wk_d = nc.dram_tensor("wk", [D, D], F32, kind="ExternalInput").ap()
    wv_d = nc.dram_tensor("wv", [D, DV], F32, kind="ExternalInput").ap()
    wg_d = nc.dram_tensor("wg", [D, DV], F32, kind="ExternalInput").ap()
    wgk1_d = nc.dram_tensor("wgk1", [D, 16], F32, kind="ExternalInput").ap()
    wgk2_d = nc.dram_tensor("wgk2", [16, D], F32, kind="ExternalInput").ap()
    bgk2_d = nc.dram_tensor("bgk2", [1, D], F32, kind="ExternalInput").ap()
    wo_d = nc.dram_tensor("wo", [DV, D], F32, kind="ExternalInput").ap()
    gnw_d = nc.dram_tensor("gnw", [1, DVH], F32, kind="ExternalInput").ap()
    wr_d = nc.dram_tensor("wr", [D, D], F32, kind="ExternalInput").ap()
    wk2_d = nc.dram_tensor("wk2", [D, D], F32, kind="ExternalInput").ap()
    wv2_d = nc.dram_tensor("wv2", [D, D], F32, kind="ExternalInput").ap()
    mur_d = nc.dram_tensor("mur", [8, 128], F32, kind="ExternalInput").ap()
    muk_d = nc.dram_tensor("muk", [8, 128], F32, kind="ExternalInput").ap()
    cmod_d = nc.dram_tensor("cmod", [NSEQ, D], F32, kind="ExternalInput").ap()
    wmod_d = nc.dram_tensor("wmod", [D, 6 * D], F32, kind="ExternalInput").ap()

    identb_d = nc.dram_tensor("identb", [128, 128], BF16, kind="ExternalInput").ap()
    identf_d = nc.dram_tensor("identf", [128, 128], F32, kind="ExternalInput").ap()
    utri_d = nc.dram_tensor("utri", [128, 128], F32, kind="ExternalInput").ap()
    mtri_d = nc.dram_tensor("mtri", [128, 128], F32, kind="ExternalInput").ap()
    cmask_d = nc.dram_tensor("cmask", [128, 128], U8, kind="ExternalInput").ap()
    hones_d = nc.dram_tensor("hones", [128, 3], F32, kind="ExternalInput").ap()

    dbg = {}
    if debug:
        dbg['h'] = nc.dram_tensor("dbg_h", [TOK, D], BF16, kind="ExternalOutput").ap()
        dbg['sp'] = nc.dram_tensor("dbg_sp", [TOK, D], F32, kind="ExternalOutput").ap()
        dbg['mod'] = nc.dram_tensor("dbg_mod", [NSEQ, 6 * D], F32, kind="ExternalOutput").ap()
        for nm, w in [('eA', D), ('eK', D), ('qh', D), ('kh', D), ('qhfm', D),
                      ('qinfm', D), ('kinfm', D), ('Pm', 512)]:
            dbg[nm] = nc.dram_tensor("dbg_" + nm, [128, w], BF16, kind="ExternalOutput").ap()
        dbg['colv'] = nc.dram_tensor("dbg_colv", [128, 48], F32, kind="ExternalOutput").ap()

    with contextlib.ExitStack() as CTX:
        E = CTX.enter_context
        book = Book(nc, CTX)

        wg_s = E(nc.sbuf_tensor("wg_s", [128, 8 * DV], BF16))
        wrT_s = E(nc.sbuf_tensor("wrT_s", [128, 8 * D], BF16))
        identb = E(nc.sbuf_tensor("identb_s", [128, 128], BF16))
        identf = E(nc.sbuf_tensor("identf_s", [128, 128], F32))
        utri = E(nc.sbuf_tensor("utri_s", [128, 128], F32))
        mtri = E(nc.sbuf_tensor("mtri_s", [128, 128], F32))
        cmask = E(nc.sbuf_tensor("cmask_s", [128, 128], U8))
        zeros = E(nc.sbuf_tensor("zeros", [128, 128], F32))
        onesrow = E(nc.sbuf_tensor("onesrow", [1, 128], F32))
        onescol = E(nc.sbuf_tensor("onescol", [128, 1], F32))
        hones = E(nc.sbuf_tensor("hones_s", [128, 3], F32))
        cpool = E(nc.sbuf_tensor("cpool", [128, 10], F32))
        (C_NEG1, C_NLN16, C_NI16, C_PI16, C_NHALF, C_I512, C_EPS, C_I1024,
         C_ONE, C_SPCL) = range(10)
        CPV = [-1.0, -LN16, -1.0 / 16, 1.0 / 16, -0.5, 1.0 / 512, EPS,
               1.0 / 1024, 1.0, 80.0]

        def cp(i):
            return cpool[:, i:i + 1]

        mod_fm = E(nc.sbuf_tensor("mod_fm", [128, 96], F32))

        def modcol(v, dj, b):
            i = (v * 8 + dj) * 2 + b
            return mod_fm[:, i:i + 1]

        murcol = E(nc.sbuf_tensor("murcol", [128, 8], F32))
        mukcol = E(nc.sbuf_tensor("mukcol", [128, 8], F32))
        omrcol = E(nc.sbuf_tensor("omrcol", [128, 8], F32))
        omkcol = E(nc.sbuf_tensor("omkcol", [128, 8], F32))
        gnw_rep = E(nc.sbuf_tensor("gnw_rep", [128, DV], BF16))
        ga_rep = E(nc.sbuf_tensor("ga_rep", [128, 2 * D], BF16))
        gb_rep = E(nc.sbuf_tensor("gb_rep", [128, 2 * D], BF16))
        S_bf = E(nc.sbuf_tensor("S_bf", [128, H * 2 * DVH], BF16))

        def S_tile(h, j):
            o = (2 * h + j) * DVH
            return S_bf[:, o:o + DVH]

        def wsrc(dr, F, nch=8):
            return bass.AP(dr.tensor, 0, [[F, 128], [128 * F, nch], [1, F]])

        def wdst(sb, F, nch=8):
            return bass.AP(sb, 0, [[nch * F, 128], [F, nch], [1, F]])

        bw = {}

        with contextlib.ExitStack() as SA:
            ES = SA.enter_context
            wq_s = ES(nc.sbuf_tensor("wq_s", [128, 8 * D], BF16))
            wk_s = ES(nc.sbuf_tensor("wk_s", [128, 8 * D], BF16))
            wv_s = ES(nc.sbuf_tensor("wv_s", [128, 8 * DV], BF16))
            wgk1_s = ES(nc.sbuf_tensor("wgk1_s", [128, 8 * 16], BF16))
            wgk2b = ES(nc.sbuf_tensor("wgk2b", [17, D], F32))
            z1ones = ES(nc.sbuf_tensor("z1ones", [17, 128], F32))

            with contextlib.ExitStack() as ST:
                EE = ST.enter_context
                wmod_bf = EE(nc.sbuf_tensor("wmod_bf", [128, 8 * D], BF16))
                cmod_s = EE(nc.sbuf_tensor("cmod_s", [NSEQ, D], F32))
                sc_bf = EE(nc.sbuf_tensor("sc_bf", [NSEQ, D], BF16))
                scfm = EE(nc.sbuf_tensor("scfm", [128, 16], BF16))
                modsb = EE(nc.sbuf_tensor("modsb", [NSEQ, 6 * D], F32))
                mtmp = EE(nc.sbuf_tensor("mtmp", [NSEQ, D], F32))
                mtmp2 = EE(nc.sbuf_tensor("mtmp2", [NSEQ, D], F32))
                murow = EE(nc.sbuf_tensor("murow", [8, 128], F32))
                mukrow = EE(nc.sbuf_tensor("mukrow", [8, 128], F32))
                gnwrow = EE(nc.sbuf_tensor("gnwrow", [1, DVH], F32))
                brow = EE(nc.sbuf_tensor("brow", [1, D], F32))

                pmod = EE(nc.psum_tensor("pmod", [NSEQ, 1024], F32))
                ptr0 = EE(nc.psum_tensor("ptr0", [128, 1024], BF16))
                ptf0 = EE(nc.psum_tensor("ptf0", [128, 512], F32))

                P = Prog(book)

                def dma(dst, src):
                    return lambda e: e.dma_start(dst, src)

                t_con = None
                for dst, src in [(identb[:], identb_d), (identf[:], identf_d),
                                 (utri[:], utri_d), (mtri[:], mtri_d),
                                 (cmask[:], cmask_d),
                                 (wgk2b[0:16, :], wgk2_d),
                                 (wgk2b[16:17, :], bgk2_d),
                                 (cmod_s[:], cmod_d),
                                 (murow[:], mur_d), (mukrow[:], muk_d),
                                 (gnwrow[:], gnw_d), (hones[:], hones_d)]:
                    t_con = P.add('Y', dma(dst, src), inc='d_st', inc_k=16)

                t_wq = P.add('G', dma(wdst(wq_s, D), wsrc(wq_d, D)), inc='d_w', inc_k=16)
                t_wk = P.add('G', dma(wdst(wk_s, D), wsrc(wk_d, D)), inc='d_w', inc_k=16)
                t_wv = P.add('G', dma(wdst(wv_s, DV), wsrc(wv_d, DV)), inc='d_w', inc_k=16)
                t_wgk1 = P.add('G', dma(wdst(wgk1_s, 16), wsrc(wgk1_d, 16)), inc='d_w', inc_k=16)
                t_wg = P.add('G', dma(wdst(wg_s, DV), wsrc(wg_d, DV)), inc='d_w', inc_k=16)
                bw['wg'] = t_wg

                def ms(e):
                    e.memset(zeros[:], 0.0)
                    e.memset(onesrow[:], 1.0)
                    e.memset(onescol[:], 1.0)
                    e.memset(z1ones[:], 1.0)
                    last = e.memset(S_bf[:], 0.0)
                    for i, val in enumerate(CPV):
                        last = e.memset(cpool[:, i:i + 1], float(val))
                    return last
                t_ms = P.add('V', ms, inc='v_st')

                def silu_c(e):
                    e.activation(mtmp[:], cmod_s[:], AF.Exp, scale=cpool[0:NSEQ, C_NEG1:C_NEG1 + 1])
                    e.activation(mtmp[:], mtmp[:], AF.Ln, bias=cpool[0:NSEQ, C_ONE:C_ONE + 1])
                    return e.activation(mtmp[:], mtmp[:], AF.Exp, scale=cpool[0:NSEQ, C_NEG1:C_NEG1 + 1])
                t_sig = P.add('S', silu_c, waits=[t_con, t_ms], inc='s_st')

                t_sc = P.add('V', lambda e: e.tensor_tensor(sc_bf[:], cmod_s[:], mtmp[:], OP.mult),
                             waits=[t_sig], inc='v_st')

                def tr_sc(e):
                    last = None
                    for j in range(8):
                        last = e.transpose(ptr0[:, 2 * j:2 * j + 2],
                                           sc_bf[:, 128 * j:128 * j + 128],
                                           identb[0:NSEQ, 0:NSEQ])
                    return last
                t_trsc = P.add('T', tr_sc, waits=[t_sc], inc='t_st')
                t_evsc = P.add('V', lambda e: e.tensor_copy(scfm[:], ptr0[:, 0:16]),
                               waits=[t_trsc], inc='v_st')

                t_prev_mm = t_evsc
                t_wmld = P.add('G', lambda e: e.dma_start(
                    wdst(wmod_bf, D),
                    bass.AP(wmod_d.tensor, 0, [[6 * D, 128], [128 * 6 * D, 8], [1, D]])),
                    inc='d_w', inc_k=16)
                t_modsb = None
                for v in range(6):
                    def mm_mod(e):
                        last = None
                        for half in range(2):
                            for dj in range(8):
                                last = e.matmul(
                                    pmod[:, 512 * half:512 * (half + 1)],
                                    scfm[:, 2 * dj:2 * dj + 2],
                                    wmod_bf[:, D * dj + 512 * half: D * dj + 512 * (half + 1)],
                                    start=(dj == 0), stop=(dj == 7))
                        return last
                    t_mm = P.add('T', mm_mod, waits=[t_prev_mm, t_wmld], inc='t_st')
                    if v < 5:
                        t_wmld = P.add('G', lambda e, v=v: e.dma_start(
                            wdst(wmod_bf, D),
                            bass.AP(wmod_d.tensor, (v + 1) * D,
                                    [[6 * D, 128], [128 * 6 * D, 8], [1, D]])),
                            waits=[t_mm], inc='d_w', inc_k=16)
                    t_prev_mm = P.add('V', lambda e, v=v: e.tensor_copy(
                        modsb[:, v * D:(v + 1) * D], pmod[:]),
                        waits=[t_mm], inc='v_st')
                t_modsb = t_prev_mm

                def mod_xf(e):
                    e.tensor_single_scalar(modsb[:, 1 * D:2 * D], modsb[:, 1 * D:2 * D], 1.0, OP.add)
                    e.tensor_single_scalar(modsb[:, 4 * D:5 * D], modsb[:, 4 * D:5 * D], 1.0, OP.add)
                    last = None
                    for v in (2, 5):
                        sl = modsb[:, v * D:(v + 1) * D]
                        e.tensor_single_scalar(mtmp[:], sl, 0.0, OP.is_ge)
                        e.tensor_scalar(mtmp[:], mtmp[:], 2.0, -1.0, OP.mult, OP.add)
                        e.tensor_single_scalar(mtmp2[:], sl, -1.0, OP.mult)
                        e.tensor_tensor(mtmp2[:], mtmp2[:], sl, OP.max)
                        e.tensor_single_scalar(mtmp2[:], mtmp2[:], 0.25, OP.max)
                        last = e.tensor_tensor(sl, mtmp[:], mtmp2[:], OP.mult)
                    return last
                t_modxf = P.add('V', mod_xf, waits=[t_modsb], inc='v_st')
                if debug:
                    P.add('Y', dma(dbg['mod'], modsb[:]), waits=[t_modxf], inc='d_dbg', inc_k=16)

                t_prev = t_modxf
                for v in range(6):
                    def tr_mod(e, v=v):
                        last = None
                        for dj in range(8):
                            last = e.transpose(ptf0[:, 2 * dj:2 * dj + 2],
                                               modsb[:, v * D + 128 * dj: v * D + 128 * dj + 128],
                                               identf[0:NSEQ, 0:NSEQ])
                        return last
                    tm = P.add('T', tr_mod, waits=[t_prev], inc='t_st')
                    t_prev = P.add('V', lambda e, v=v: e.tensor_copy(
                        mod_fm[:, v * 16:v * 16 + 16], ptf0[:, 0:16]),
                        waits=[tm], inc='v_st')
                t_modfm = t_prev

                def tr_mu(e):
                    e.transpose(ptf0[:, 32:40], murow[:], identf[0:8, 0:8])
                    return e.transpose(ptf0[:, 40:48], mukrow[:], identf[0:8, 0:8])
                t_trmu = P.add('T', tr_mu, waits=[t_modfm], inc='t_st')

                def ev_mu(e):
                    e.tensor_copy(murcol[:], ptf0[:, 32:40])
                    e.tensor_copy(mukcol[:], ptf0[:, 40:48])
                    e.drain()
                    e.tensor_scalar(omrcol[:], murcol[:], -1.0, 1.0, OP.mult, OP.add)
                    return e.tensor_scalar(omkcol[:], mukcol[:], -1.0, 1.0, OP.mult, OP.add)
                t_evmu = P.add('V', ev_mu, waits=[t_trmu], inc='v_st')

                t_bg = P.add('T', lambda e: e.matmul(ptf0[:, 0:512], onesrow[:], gnwrow[:],
                                                     start=True, stop=True),
                             waits=[t_evmu], inc='t_st')

                def ev_gnw(e):
                    last = None
                    for r in range(H):
                        last = e.tensor_copy(gnw_rep[:, r * 512:(r + 1) * 512], ptf0[:, 0:512])
                    return last
                t_prevb = P.add('V', ev_gnw, waits=[t_bg], inc='v_st')

                for b in range(NSEQ):
                    for rep, v in ((ga_rep, 2), (gb_rep, 5)):
                        t_row = P.add('Y', lambda e, b=b, v=v: e.dma_start(
                            brow[:], modsb[b:b + 1, v * D:(v + 1) * D]),
                            waits=[t_prevb], inc='d_st', inc_k=16)
                        for half in range(2):
                            tb = P.add('T', lambda e, half=half: e.matmul(
                                ptf0[:, 0:512], onesrow[:],
                                brow[:, 512 * half:512 * (half + 1)],
                                start=True, stop=True),
                                waits=[t_prevb, t_row], inc='t_st')
                            t_prevb = P.add('V', lambda e, b=b, rep=rep, half=half: e.tensor_copy(
                                rep[:, b * D + 512 * half: b * D + 512 * (half + 1)],
                                ptf0[:, 0:512]),
                                waits=[tb], inc='v_st')

                P.run_block(nc)

            with contextlib.ExitStack() as AS:
                EA = AS.enter_context
                xa2 = EA(nc.sbuf_tensor("xa2", [128, 2 * D], F32))
                y_bf = EA(nc.sbuf_tensor("y_bf", [128, D], BF16))
                hraw = EA(nc.sbuf_tensor("hraw", [128, D], BF16))
                h_fm = EA(nc.sbuf_tensor("h_fm", [128, D], BF16))
                q_bf = EA(nc.sbuf_tensor("q_bf", [128, D], BF16))
                k_bf = EA(nc.sbuf_tensor("k_bf", [128, D], BF16))
                v_bf = EA(nc.sbuf_tensor("v_bf", [128, DV], BF16))
                z1_f = EA(nc.sbuf_tensor("z1_f", [128, 16], F32))
                sp_f = EA(nc.sbuf_tensor("sp_f", [128, D], F32))
                eA_bf = EA(nc.sbuf_tensor("eA_bf", [128, D], BF16))
                eK_bf = EA(nc.sbuf_tensor("eK_bf", [128, D], BF16))
                qh_tm = EA(nc.sbuf_tensor("qh_tm", [128, D], BF16))
                kh_tm = EA(nc.sbuf_tensor("kh_tm", [128, D], BF16))
                qh_fm = EA(nc.sbuf_tensor("qh_fm", [128, D], BF16))
                qin_fm = EA(nc.sbuf_tensor("qin_fm", [128, D], BF16))
                kin_fm = EA(nc.sbuf_tensor("kin_fm", [128, D], BF16))
                Pm = EA(nc.sbuf_tensor("Pm", [128, 512], BF16))
                on_sb = EA(nc.sbuf_tensor("on_sb", [128, DV], BF16))
                wtr_bf = EA(nc.sbuf_tensor("wtr_bf", [128, 8 * D], BF16))
                ln_t = EA(nc.sbuf_tensor("ln_t", [128, 8], F32))
                x2a = EA(nc.sbuf_tensor("x2a", [128, D], BF16))
                EbA = EA(nc.sbuf_tensor("EbA", [128, D], BF16))
                colv = EA(nc.sbuf_tensor("colv", [128, 48], F32))
                ss4 = EA(nc.sbuf_tensor("ss4", [128, 8], F32))

                P1 = EA(nc.psum_tensor("P1_A", [128, 1024], F32))
                P2 = EA(nc.psum_tensor("P2_A", [128, 1024], F32))
                P3 = EA(nc.psum_tensor("P3_A", [128, 1024], F32))
                P4 = EA(nc.psum_tensor("P4_A", [128, 512], F32))
                P5 = EA(nc.psum_tensor("P5_A", [128, 512], F32))
                pS = P3

                P = Prog(book)
                pv = {}

                def dma_tr8(src, dst):
                    sap = bass.AP(src, 0, [[1024, 128], [128, 8], [1, 128]])
                    dap = bass.AP(dst, 0, [[1024, 128], [128, 8], [1, 128]])
                    return lambda e: e.dma_start(dap, sap, transpose=True)

                pv['d_x'] = P.add('G', lambda e: e.dma_start(xa2[:, 0:D], xin[0:128, :]),
                                  inc='d_x', inc_k=16)
                pv['d_x1'] = P.add('G', lambda e: e.dma_start(xa2[:, D:2 * D], xin[128:256, :]),
                                   inc='d_x', inc_k=16)

                def trs_group(dst_s, cj0):
                    def f(e, sem):
                        for cj in (cj0, cj0 + 1):
                            for dj in range(8):
                                e.dma_start(
                                    dst_s[:, 1024 * cj + 128 * dj: 1024 * cj + 128 * dj + 128],
                                    wtr_bf[:, 1024 * dj + 128 * cj: 1024 * dj + 128 * cj + 128],
                                    transpose=True).then_inc(sem, 16)
                        return None
                    return f
                bjob = []
                bjob.append(('G', dma(wdst(wtr_bf, D), wsrc(wk2_d, D)), 16))
                for cj0 in range(0, 8, 2):
                    bjob.append(('S', trs_group(wrT_s, cj0), 16 * 16))
                bjob.append(('S', lambda e: e.dma_start(wkT_scr, wrT_s[:]), 16))
                bjob.append(('G', dma(wdst(wtr_bf, D), wsrc(wv2_d, D)), 16))
                for cj0 in range(0, 8, 2):
                    bjob.append(('S', trs_group(wrT_s, cj0), 16 * 16))
                bjob.append(('S', lambda e: e.dma_start(wvT_scr, wrT_s[:]), 16))
                bjob.append(('G', dma(wdst(wtr_bf, D), wsrc(wr_d, D)), 16))
                for cj0 in range(0, 8, 2):
                    bjob.append(('S', trs_group(wrT_s, cj0), 16 * 16))

                bstate = {'prev': None, 'idx': 0}

                def inject_bjob():
                    if bstate['idx'] >= len(bjob):
                        return
                    eng, emit, k = bjob[bstate['idx']]
                    sem = 'd_wt' if eng in ('Y', 'S') else 'd_w'
                    bstate['prev'] = P.add(eng, emit, waits=[bstate['prev']],
                                           inc=sem, inc_k=k, multi_dma=(k > 16))
                    bstate['idx'] += 1

                def emit_frontA(i):
                    b = i // NCH
                    xa = xa2[:, (i % 2) * D:(i % 2 + 1) * D]
                    t_x2 = P.add('S', lambda e, xa=xa: e.activation(x2a[:], xa, AF.Square),
                                 waits=[pv['d_x'], pv.get('vm2r')], inc='s_x2')
                    t_m = P.add('V', lambda e, xa=xa: e.tensor_reduce(
                        ln_t[:, 0:1], xa, AXX, OP.add), waits=[pv['d_x']], inc='v_m')
                    t_m2 = P.add('V', lambda e: e.tensor_reduce(ln_t[:, 2:3], x2a[:],
                                                                AXX, OP.add),
                                 waits=[t_x2], inc='v_m2r')
                    pv['vm2r'] = t_m2

                    def s_stats(e):
                        e.activation(ln_t[:, 1:2], ln_t[:, 0:1], AF.Identity, scale=cp(C_I1024))
                        e.activation(ln_t[:, 4:5], ln_t[:, 2:3], AF.Identity,
                                     scale=cp(C_I1024))
                        e.drain()
                        return e.activation(ln_t[:, 3:4], ln_t[:, 1:2], AF.Square)
                    t_st_ = P.add('S', s_stats, waits=[t_m, t_m2], inc='s_stats')

                    t_var = P.add('V', lambda e: e.tensor_tensor(
                        ln_t[:, 5:6], ln_t[:, 4:5], ln_t[:, 3:4], OP.subtract),
                        waits=[t_st_], inc='v_var')

                    def rstd(e):
                        e.activation(ln_t[:, 6:7], ln_t[:, 5:6], AF.Ln, bias=cp(C_EPS))
                        e.drain()
                        return e.activation(ln_t[:, 6:7], ln_t[:, 6:7], AF.Exp, scale=cp(C_NHALF))
                    t_rstd = P.add('S', rstd, waits=[t_var], inc='s_rstd')

                    t_y = P.add('V', lambda e, xa=xa: e.tensor_scalar(
                        y_bf[:], xa, ln_t[:, 1:2], ln_t[:, 6:7], OP.subtract, OP.mult),
                        waits=[t_rstd, pv.get('d_ty')], inc='v_y')

                    if i + 2 < NTOT:
                        pv['xq'].append(P.add(
                            'G', lambda e, r=128 * (i + 2), p=(i % 2): e.dma_start(
                                xa2[:, p * D:(p + 1) * D], xin[r:r + 128, :]),
                            waits=[t_y], inc='d_x', inc_k=16))

                    t_ty = P.add('Y', dma_tr8(y_bf, hraw),
                                 waits=[t_y, pv.get('s_h')], inc='d_ty', inc_k=16)
                    pv['d_ty'] = t_ty

                    def hev(e, b=b):
                        last = None
                        for j in range(8):
                            last = e.activation(h_fm[:, 128 * j:128 * (j + 1)],
                                                hraw[:, 128 * j:128 * (j + 1)], AF.Identity,
                                                scale=modcol(1, j, b), bias=modcol(0, j, b))
                        return last
                    t_h = P.add('S', hev, waits=[t_ty, pv.get('tvb'), pv.get('d_h')],
                                inc='s_h')
                    pv['s_h'] = t_h
                    pv['d_h'] = P.add('G', lambda e, row=128 * i: e.dma_start(
                        h_scr[row:row + 128, :], h_fm[:]), waits=[t_h],
                        inc='d_h', inc_k=16)

                def emit_gk(i):
                    def mm_z1(e):
                        last = None
                        for dj in range(8):
                            last = e.matmul(P4[:, 0:16], h_fm[:, 128 * dj:128 * (dj + 1)],
                                            wgk1_s[:, 16 * dj:16 * (dj + 1)],
                                            start=(dj == 0), stop=(dj == 7))
                        return last
                    t_z1 = P.add('T', mm_z1,
                                 waits=[pv.get('scols'), pv['s_h']] +
                                       ([t_wq] if i == 0 else []),
                                 inc='t_z1')
                    t_z1e = P.add('V', lambda e: e.tensor_copy(z1_f[:], P4[:, 0:16]),
                                  waits=[t_z1], inc='v_z1')
                    t_z1t = P.add('T', lambda e: e.transpose(P4[0:16, 128:256], z1_f[:],
                                                             identf[:]),
                                  waits=[t_z1e], inc='t_z1t')
                    t_z1te = P.add('V', lambda e: e.tensor_copy(z1ones[0:16, :],
                                                                P4[0:16, 128:256]),
                                   waits=[t_z1t], inc='v_z1t')

                    def mm_gkp(e):
                        e.matmul(P4[:, 0:512], z1ones[:], wgk2b[:, 0:512], start=True, stop=True)
                        return e.matmul(P5[:, 0:512], z1ones[:], wgk2b[:, 512:1024],
                                        start=True, stop=True)
                    t_gkp = P.add('T', mm_gkp, waits=[t_z1te, pv.get('vP5')], inc='t_gkp')

                    def s_E(e):
                        e.activation(EbA[:, 0:512], P4[:, 0:512], AF.Exp, scale=cp(C_NEG1))
                        return e.activation(EbA[:, 512:1024], P5[:, 0:512], AF.Exp,
                                            scale=cp(C_NEG1))
                    t_E = P.add('S', s_E, waits=[t_gkp], inc='s_E')
                    pv['sE'] = t_E
                    t_sp0 = P.add('S', lambda e: e.activation(sp_f[:], EbA[:], AF.Ln,
                                                              bias=cp(C_ONE)),
                                  waits=[pv.get('tcols'), pv.get('dbg_sp')], inc='s_sp0')
                    pv['sp0'] = t_sp0
                    pv['vP5'] = pv['sE']

                pv['xq'] = [pv['d_x1']]
                emit_frontA(0)
                emit_gk(0)
                for i in range(NTOT):
                    b = i // NCH
                    c = i % NCH
                    row = 128 * i

                    if c == 0:
                        pv['sinit'] = P.add('V', lambda e: e.memset(S_bf[:], 0.0),
                                            inc='v_sinit')
                    t_h = pv['s_h']

                    def mm_proj(dst, wsb, stride, base):
                        def f(e):
                            last = None
                            for half in range(2):
                                for dj in range(8):
                                    last = e.matmul(
                                        dst[:, 512 * half:512 * (half + 1)],
                                        h_fm[:, 128 * dj:128 * (dj + 1)],
                                        wsb[:, stride * dj + base + 512 * half:
                                            stride * dj + base + 512 * (half + 1)],
                                        start=(dj == 0), stop=(dj == 7))
                            return last
                        return f

                    t_q = P.add('T', mm_proj(P1, wq_s, D, 0),
                                waits=[t_h, pv.get('von')] + ([t_wq] if i == 0 else []),
                                inc='t_q')
                    t_k = P.add('T', mm_proj(P2, wk_s, D, 0),
                                waits=([t_wk] if i == 0 else []), inc='t_k')
                    t_va = P.add('T', mm_proj(P3, wv_s, DV, 0),
                                 waits=[pv.get('stt3')] + ([t_wv] if i == 0 else []),
                                 inc='t_va')

                    t_qe = P.add('V', lambda e: e.tensor_copy(q_bf[:], P1[:]),
                                 waits=[t_q], inc='v_q')
                    t_ke = P.add('V', lambda e: e.tensor_copy(k_bf[:], P2[:]),
                                 waits=[t_k], inc='v_k')
                    t_vae = P.add('V', lambda e: e.tensor_copy(v_bf[:, 0:1024], P3[:]),
                                  waits=[t_va], inc='v_va')
                    t_vb = P.add('T', mm_proj(P1, wv_s, DV, 1024), waits=[t_qe], inc='t_vb')
                    pv['tvb'] = t_vb
                    t_vbe = P.add('V', lambda e: e.tensor_copy(v_bf[:, 1024:2048], P1[:]),
                                  waits=[t_vb], inc='v_vb')

                    def mm_cum(dst, tri):
                        def f(e):
                            e.matmul(dst[:, 0:512], tri[:], sp_f[:, 0:512], start=True, stop=True)
                            return e.matmul(dst[:, 512:1024], tri[:], sp_f[:, 512:1024],
                                            start=True, stop=True)
                        return f
                    pv['spcl'] = P.add('V', lambda e: e.tensor_single_scalar(
                        sp_f[:], sp_f[:], 80.0, OP.min), waits=[pv['sp0']], inc='v_spcl')
                    t_cs = P.add('T', mm_cum(P1, utri), waits=[pv['spcl'], t_vbe], inc='t_cs')
                    t_t1 = P.add('T', mm_cum(P2, mtri), waits=[t_ke], inc='t_t1')

                    def mm_cols(e):
                        last = None
                        for j in range(8):
                            last = e.matmul(P4[:, 256 + 3 * j:259 + 3 * j],
                                            sp_f[:, 128 * j:128 * (j + 1)], hones[:],
                                            start=True, stop=True)
                        return last
                    t_cols = P.add('T', mm_cols, waits=[pv['spcl'], pv.get('sE')], inc='t_cols')
                    pv['tcols'] = t_cols

                    t_eA = P.add('S', lambda e: e.activation(eA_bf[:], P1[:], AF.Exp,
                                                             scale=cp(C_NI16), bias=cp(C_NLN16)),
                                 waits=[t_cs], inc='s_eA')
                    t_eK = P.add('S', lambda e: e.activation(eK_bf[:], P2[:], AF.Exp,
                                                             scale=cp(C_PI16)),
                                 waits=[t_t1], inc='s_eK')

                    spC_ap = bass.AP(P4, 256, [[512, 128], [3, 8]])
                    spm_ap = bass.AP(P4, 257, [[512, 128], [3, 8]])
                    sph_ap = bass.AP(P4, 258, [[512, 128], [3, 8]])

                    def v_colpre(e):
                        e.tensor_single_scalar(colv[:, 32:40], spm_ap, 552.0, OP.min)
                        return e.tensor_single_scalar(colv[:, 8:16], sph_ap, 552.0, OP.min)
                    t_vcp = P.add('V', v_colpre, waits=[t_cols], inc='v_colp')

                    def s_cols(e):
                        e.activation(colv[:, 0:8], spC_ap, AF.Exp, scale=cp(C_NI16))
                        e.activation(colv[:, 16:24], colv[:, 32:40], AF.Exp, scale=cp(C_PI16))
                        return e.activation(colv[:, 24:32], colv[:, 8:16], AF.Exp,
                                            scale=cp(C_PI16))
                    t_vc = P.add('S', s_cols, waits=[t_vcp], inc='s_cols')
                    pv['scols'] = t_vc

                    if i + 1 < NTOT:
                        pv['d_x'] = pv['xq'].pop(0)
                        emit_frontA(i + 1)
                        emit_gk(i + 1)

                    t_qh = P.add('V', lambda e: e.tensor_tensor(qh_tm[:], q_bf[:], eA_bf[:],
                                                                OP.mult),
                                 waits=[t_eA], inc='v_qh')
                    t_kh = P.add('V', lambda e: e.tensor_tensor(kh_tm[:], k_bf[:], eK_bf[:],
                                                                OP.mult),
                                 waits=[t_eK], inc='v_kh')

                    t_tq = P.add('Y', dma_tr8(qh_tm, qh_fm),
                                 waits=[t_qh, pv.get('tio3')], inc='d_tq', inc_k=16)
                    pv['d_tq'] = t_tq

                    def ev_tq(e):
                        last = None
                        for j in range(8):
                            last = e.tensor_single_scalar(qin_fm[:, 128 * j:128 * (j + 1)],
                                                          qh_fm[:, 128 * j:128 * (j + 1)],
                                                          colv[:, 16 + j:17 + j], OP.mult)
                        return last
                    t_tqe = P.add('V', ev_tq, waits=[t_tq, t_vc, pv.get('tP')], inc='v_tqe')

                    t_tk = P.add('Y', dma_tr8(kh_tm, kin_fm),
                                 waits=[t_kh, pv.get('tP')], inc='d_tk', inc_k=16)
                    pv['d_tkp'] = t_tk

                    def ev_tk(e):
                        last = None
                        for j in range(8):
                            last = e.tensor_single_scalar(kin_fm[:, 128 * j:128 * (j + 1)],
                                                          kin_fm[:, 128 * j:128 * (j + 1)],
                                                          colv[:, 24 + j:25 + j], OP.mult)
                        return last
                    t_tke = P.add('V', ev_tk, waits=[t_tk, t_vc], inc='v_tke')

                    def mm_state(h):
                        def f(e):
                            last = None
                            for j in range(2):
                                last = e.matmul(pS[:, 512 * j:512 * (j + 1)],
                                                kh_tm[:, 256 * h + 128 * j:
                                                      256 * h + 128 * (j + 1)],
                                                v_bf[:, 512 * h:512 * (h + 1)],
                                                start=True, stop=True)
                            return last
                        return f

                    t_st = [None] * 4
                    t_st[0] = P.add('T', mm_state(0), waits=[t_kh, t_vae], inc='t_s0')

                    ps = [P1[:, 0:128], P1[:, 512:640], P2[:, 0:128], P2[:, 512:640]]
                    po = [P1[:, 0:512], P1[:, 512:1024], P2[:, 0:512], P2[:, 512:1024]]

                    def mm_P(e):
                        last = None
                        for h in range(4):
                            for j in range(2):
                                dblk = 2 * h + j
                                last = e.matmul(ps[h],
                                                kin_fm[:, 128 * dblk:128 * (dblk + 1)],
                                                qin_fm[:, 128 * dblk:128 * (dblk + 1)],
                                                start=(j == 0), stop=(j == 1))
                        return last
                    t_P = P.add('T', mm_P, waits=[t_tke, t_tqe, t_eA, t_eK], inc='t_P')
                    pv['tP'] = t_P

                    def v_sel(e):
                        last = None
                        for h in range(4):
                            last = e.select(Pm[:, 128 * h:128 * (h + 1)], cmask[:], ps[h],
                                            zeros[:])
                        return last
                    t_Pm = P.add('V', v_sel, waits=[t_P], inc='v_Pm')

                    t_io = [None] * 4
                    t_stt = [None] * 4
                    for h in range(4):
                        def mm_io(e, h=h):
                            e.matmul(po[h], Pm[:, 128 * h:128 * (h + 1)],
                                     v_bf[:, 512 * h:512 * (h + 1)], start=True, stop=False)
                            last = None
                            for j in range(2):
                                last = e.matmul(po[h],
                                                qh_fm[:, 128 * (2 * h + j):128 * (2 * h + j + 1)],
                                                S_tile(h, j), start=False, stop=(j == 1))
                            return last
                        io_w = []
                        if h == 0:
                            io_w = [t_Pm, pv.get('stt3'), t_tq]
                            if c == 0:
                                io_w.append(pv['sinit'])
                        t_io[h] = P.add('T', mm_io, waits=io_w, inc=f'tio{h}')

                        def v_stt(e, h=h):
                            last = None
                            for j in range(2):
                                last = e.scalar_tensor_tensor(
                                    S_tile(h, j), S_tile(h, j), colv[:, 2 * h + j:2 * h + j + 1],
                                    pS[:, 512 * j:512 * (j + 1)], OP.mult, OP.add)
                            return last
                        t_stt[h] = P.add('V', v_stt, waits=[t_st[h], t_io[h], t_vc],
                                         inc=f'vstt{h}')
                        if h < 3:
                            t_st[h + 1] = P.add('T', mm_state(h + 1),
                                                waits=[t_stt[h]], inc=f'ts{h + 1}')
                    pv['stt3'] = t_stt[3]
                    pv['tst3'] = t_st[3]
                    pv['tio3'] = t_io[3]

                    def s_rms(e):
                        for h in range(4):
                            e.activation(qh_tm[:, 0:512], po[h], AF.Square,
                                         accum_out=ss4[:, h:h + 1])
                        e.drain()
                        e.activation(ss4[:, 4:8], ss4[:, 0:4], AF.Ln, scale=cp(C_I512),
                                     bias=cp(C_EPS))
                        e.drain()
                        return e.activation(ss4[:, 4:8], ss4[:, 4:8], AF.Exp, scale=cp(C_NHALF))
                    t_rms = P.add('S', s_rms, waits=[t_io[3]], inc='s_rms')

                    def v_on(e):
                        last = None
                        for h in range(4):
                            last = e.tensor_single_scalar(on_sb[:, 512 * h:512 * (h + 1)],
                                                          po[h], ss4[:, 4 + h:5 + h], OP.mult)
                        return last
                    t_on = P.add('V', v_on, waits=[t_rms, pv.get('d_on')], inc='v_on')
                    pv['von'] = t_on

                    pv['d_on'] = P.add('G', lambda e, row=row: e.dma_start(
                        o_scr[row:row + 128, :], on_sb[:]), waits=[t_on],
                        inc='d_on', inc_k=16)

                    if debug:
                        pv['dbg_h'] = P.add('Y', lambda e, row=row: e.dma_start(
                            dbg['h'][row:row + 128, :], h_fm[:]), waits=[t_h],
                            inc='d_dbg', inc_k=16)
                        pv['dbg_sp'] = P.add('Y', lambda e, row=row: e.dma_start(
                            dbg['sp'][row:row + 128, :], sp_f[:]),
                            waits=[t_spcl], inc='d_dbg', inc_k=16)

                    inject_bjob()

                assert bstate['idx'] == len(bjob)
                bw['wrT'] = bstate['prev']
                P.add('G', lambda e: e.nop(), waits=[pv['d_on'], pv['d_h']])
                P.run_block(nc)


        with contextlib.ExitStack() as BS:
            EB = BS.enter_context
            wo_s = EB(nc.sbuf_tensor("wo_s", [128, 16 * D], BF16))
            wkT_s = EB(nc.sbuf_tensor("wkT_s", [128, 8 * D], BF16))
            wvT_s = EB(nc.sbuf_tensor("wvT_s", [128, 8 * D], BF16))

            xb2 = EB(nc.sbuf_tensor("xb2", [128, 2 * D], F32))
            onb = EB(nc.sbuf_tensor("onb", [128, 2 * DV], BF16))
            y_b = EB(nc.sbuf_tensor("y_b", [128, D], BF16))
            hraw2b = EB(nc.sbuf_tensor("hraw2b", [128, D], BF16))
            hB2 = EB(nc.sbuf_tensor("hB2", [128, 2 * D], BF16))
            h2_fm = EB(nc.sbuf_tensor("h2_fm", [128, D], BF16))
            h2s_fm = EB(nc.sbuf_tensor("h2s_fm", [128, D], BF16))
            carry = EB(nc.sbuf_tensor("carry", [128, 8], BF16))
            g_bf = EB(nc.sbuf_tensor("g_bf", [128, DV], BF16))
            Ebuf = EB(nc.sbuf_tensor("Ebuf", [128, DV], BF16))
            Fbuf = EB(nc.sbuf_tensor("Fbuf", [128, 2 * DV], BF16))
            og_fm = EB(nc.sbuf_tensor("og_fm", [128, DV], BF16))
            tr_bf = EB(nc.sbuf_tensor("tr_bf", [128, D], BF16))
            xnew = EB(nc.sbuf_tensor("xnew", [128, 2 * D], F32))
            xr_fm = EB(nc.sbuf_tensor("xr_fm", [128, D], BF16))
            xk_fm = EB(nc.sbuf_tensor("xk_fm", [128, D], BF16))
            t2r = EB(nc.sbuf_tensor("t2r", [128, D], BF16))
            t2k = EB(nc.sbuf_tensor("t2k", [128, D], BF16))
            ka_bf = EB(nc.sbuf_tensor("ka_bf", [128, D], BF16))
            kaf = EB(nc.sbuf_tensor("kaf", [128, D], BF16))
            rr_bf = t2r
            f_bf = ka_bf
            x2B = EB(nc.sbuf_tensor("x2B", [128, D], BF16))
            ln2B = EB(nc.sbuf_tensor("ln2B", [128, 8], F32))

            Q1 = EB(nc.psum_tensor("Q1_B", [128, 1024], F32))
            Q2 = EB(nc.psum_tensor("Q2_B", [128, 1024], F32))
            Q3 = EB(nc.psum_tensor("Q3_B", [128, 1024], F32))
            Q4 = EB(nc.psum_tensor("Q4_B", [128, 1024], F32))

            P = Prog(book)
            pv = {}

            def dma_trn(src, dst, n):
                w = 128 * n
                sap = bass.AP(src, 0, [[w, 128], [128, n], [1, 128]])
                dap = bass.AP(dst, 0, [[w, 128], [128, n], [1, 128]])
                return lambda e: e.dma_start(dap, sap, transpose=True)

            t_wo = P.add('G', lambda e: e.dma_start(wdst(wo_s, D, 16), wsrc(wo_d, D, 16)),
                         inc='d_w', inc_k=16)
            t_wkT = P.add('G', lambda e: e.dma_start(wkT_s[:], wkT_scr),
                          inc='d_w', inc_k=16)
            t_wvT = P.add('G', lambda e: e.dma_start(wvT_s[:], wvT_scr),
                          inc='d_w', inc_k=16)

            pv['d_x0'] = P.add('G', lambda e: e.dma_start(xb2[:, 0:D], xin[0:128, :]),
                               inc='d_x2', inc_k=16)
            pv['d_x1'] = P.add('G', lambda e: e.dma_start(xb2[:, D:2 * D], xin[128:256, :]),
                               inc='d_x2', inc_k=16)
            pv['d_o0'] = P.add('G', lambda e: e.dma_start(onb[:, 0:DV], o_scr[0:128, :]),
                               inc='d_on2', inc_k=16)
            pv['d_hB0'] = P.add('G', lambda e: e.dma_start(hB2[:, 0:D], h_scr[0:128, :]),
                                inc='d_hB', inc_k=16)

            def emit_front_mm(i):
                hcur = hB2[:, (i % 2) * D:(i % 2 + 1) * D]
                d_h_tok = pv[f'd_hB{i}'] if i else pv['d_hB0']
                if i + 1 < NTOT:
                    pv[f'd_hB{i + 1}'] = P.add(
                        'G', lambda e, r=128 * (i + 1), nb=(i + 1) % 2: e.dma_start(
                            hB2[:, nb * D:(nb + 1) * D], h_scr[r:r + 128, :]),
                        waits=[pv.get('tgB')], inc='d_hB', inc_k=16)

                def mm_g(dst, base):
                    def f(e):
                        last = None
                        for half in range(2):
                            for dj in range(8):
                                last = e.matmul(dst[:, 512 * half:512 * (half + 1)],
                                                hcur[:, 128 * dj:128 * (dj + 1)],
                                                wg_s[:, DV * dj + base + 512 * half:
                                                     DV * dj + base + 512 * (half + 1)],
                                                start=(dj == 0), stop=(dj == 7))
                        return last
                    return f
                t_gA = P.add('T', mm_g(Q4, 0),
                             waits=[d_h_tok, pv.get('vgA')] + ([bw['wg']] if i == 0 else []),
                             inc='t_gA')
                t_gB = P.add('T', mm_g(Q2, 1024),
                             waits=[pv.get('skr') if i >= 2 else pv.get('vgB')], inc='t_gB')
                pv['tgB'] = t_gB
                pv['pend_g'] = (t_gA, t_gB)

            def emit_gev():
                tga, tgb = pv.pop('pend_g')
                t_gBe = P.add('V', lambda e: e.tensor_copy(g_bf[:, 1024:2048], Q2[:]),
                              waits=[tgb], inc='v_gB')
                pv['vgB'] = t_gBe
                t_gAe = P.add('V', lambda e: e.tensor_copy(g_bf[:, 0:1024], Q4[:]),
                              waits=[tga], inc='v_gA')
                pv['vgA'] = t_gAe
                pv['gev'] = (t_gAe, t_gBe)

            def emit_front_post(i):
                t_gAe, t_gBe = pv['gev']

                def sg(e):
                    e.activation(Ebuf[:], g_bf[:], AF.Exp, scale=cp(C_NEG1))
                    e.activation(Ebuf[:], Ebuf[:], AF.Ln, bias=cp(C_ONE))
                    return e.activation(Ebuf[:], Ebuf[:], AF.Exp, scale=cp(C_NEG1))
                t_sg = P.add('S', sg, waits=[t_gAe, t_gBe], inc='s_sg')

                def fchain(e, i=i):
                    fb = Fbuf[:, (i % 2) * DV:(i % 2 + 1) * DV]
                    e.tensor_tensor(fb, g_bf[:], Ebuf[:], OP.mult)
                    return e.tensor_tensor(fb, fb, gnw_rep[:], OP.mult)
                t_F = P.add('V', fchain, waits=[t_sg, pv.get('vog2'), pv.get('vf')],
                            inc='v_F')
                pv[f'vF{i}'] = t_F

            def emit_back(i):
                b = i // NCH
                c = i % NCH
                row = 128 * i
                pb = i % 2
                xcur = xb2[:, pb * D:(pb + 1) * D]
                ocur = onb[:, pb * DV:(pb + 1) * DV]
                xncur = xnew[:, pb * D:(pb + 1) * D]
                d_o_tok = pv[f'd_o{i}'] if i else pv['d_o0']

                fslot = Fbuf[:, (i % 2) * DV:(i % 2 + 1) * DV]
                t_og = P.add('V', lambda e, oc=ocur, fs=fslot: e.tensor_tensor(
                    fs, oc, fs, OP.mult),
                    waits=[d_o_tok, pv[f'vF{i}'], pv.get('dogT')], inc='v_og')
                pv['vog2'] = t_og

                if i + 1 < NTOT:
                    nb = (i + 1) % 2
                    pv[f'd_o{i + 1}'] = P.add('G', lambda e, r=row + 128, nb=nb: e.dma_start(
                        onb[:, nb * DV:(nb + 1) * DV], o_scr[r:r + 128, :]),
                        waits=[t_og], inc='d_on2', inc_k=16)

                w16 = 2048
                sap16 = bass.AP(Fbuf, (i % 2) * DV, [[2 * DV, 128], [128, 16], [1, 128]])
                dap16 = bass.AP(og_fm, 0, [[DV, 128], [128, 16], [1, 128]])
                t_ogT = P.add('Y', lambda e, sap=sap16, dap=dap16: e.dma_start(
                    dap, sap, transpose=True),
                    waits=[t_og, pv.get('two')], inc='d_ogT', inc_k=16)
                pv['dogT'] = t_ogT

                def mm_wo(e):
                    last = None
                    for half in range(2):
                        for vj in range(16):
                            last = e.matmul(Q3[:, 512 * half:512 * (half + 1)],
                                            og_fm[:, 128 * vj:128 * (vj + 1)],
                                            wo_s[:, D * vj + 512 * half:
                                                 D * vj + 512 * (half + 1)],
                                            start=(vj == 0), stop=(vj == 15))
                    return last
                t_wom = P.add('T', mm_wo,
                              waits=[t_ogT, pv.get('vf')] + ([t_wo] if i == 0 else []),
                              inc='t_wo')
                pv['two'] = t_wom

                if i + 1 < NTOT:
                    emit_front_mm(i + 1)

                def v_xn(e, b=b, xc=xcur, xn=xncur):
                    e.tensor_tensor(tr_bf[:], Q3[:], ga_rep[:, b * D:(b + 1) * D], OP.mult)
                    return e.tensor_tensor(xn, tr_bf[:], xc, OP.add)
                t_xn = P.add('V', v_xn,
                             waits=[t_wom, pv.get(f'd_out{i - 2}'),
                                    pv[f'd_x{i}'] if i else pv['d_x0']], inc='v_xn')
                if i + 2 < NTOT:
                    pv[f'd_x{i + 2}'] = P.add('G', lambda e, r=128 * (i + 2), nb=i % 2: e.dma_start(
                        xb2[:, nb * D:(nb + 1) * D], xin[r:r + 128, :]),
                        waits=[t_xn], inc='d_x2', inc_k=16)

                t_x22 = P.add('S', lambda e, xn=xncur: e.activation(x2B[:], xn, AF.Square),
                              waits=[t_xn], inc='s_x2B2')
                t_m2 = P.add('V', lambda e, xn=xncur: e.tensor_reduce(
                    ln2B[:, 0:1], xn, AXX, OP.add), waits=[t_xn], inc='v_m2')
                t_m2b = P.add('V', lambda e: e.tensor_reduce(ln2B[:, 2:3], x2B[:], AXX, OP.add),
                              waits=[t_x22], inc='v_m2b')

                def s_stats2(e):
                    e.activation(ln2B[:, 1:2], ln2B[:, 0:1], AF.Identity, scale=cp(C_I1024))
                    e.activation(ln2B[:, 4:5], ln2B[:, 2:3], AF.Identity,
                                 scale=cp(C_I1024))
                    e.drain()
                    return e.activation(ln2B[:, 3:4], ln2B[:, 1:2], AF.Square)
                t_st2 = P.add('S', s_stats2, waits=[t_m2, t_m2b], inc='s_stats2')
                t_var2 = P.add('V', lambda e: e.tensor_tensor(
                    ln2B[:, 5:6], ln2B[:, 4:5], ln2B[:, 3:4], OP.subtract),
                    waits=[t_st2], inc='v_var2')

                def rstd2(e):
                    e.activation(ln2B[:, 6:7], ln2B[:, 5:6], AF.Ln, bias=cp(C_EPS))
                    e.drain()
                    return e.activation(ln2B[:, 6:7], ln2B[:, 6:7], AF.Exp, scale=cp(C_NHALF))
                t_rstd2 = P.add('S', rstd2, waits=[t_var2], inc='s_rstd2')

                t_y2b = P.add('V', lambda e, xn=xncur: e.tensor_scalar(
                    y_b[:], xn, ln2B[:, 1:2], ln2B[:, 6:7], OP.subtract, OP.mult),
                    waits=[t_rstd2, pv.get('d_ty2')], inc='v_y2b')

                if pv.get('pend_g'):
                    emit_gev()

                t_ty2 = P.add('Y', dma_trn(y_b, hraw2b, 8),
                              waits=[t_y2b, pv.get('s_h2Blast')], inc='d_ty2B', inc_k=16)
                pv['d_ty2'] = t_ty2

                def h2ev(e, b=b):
                    last = None
                    for j in range(8):
                        last = e.activation(h2_fm[:, 128 * j:128 * (j + 1)],
                                            hraw2b[:, 128 * j:128 * (j + 1)], AF.Identity,
                                            scale=modcol(4, j, b), bias=modcol(3, j, b))
                    return last
                t_h2 = P.add('S', h2ev, waits=[t_ty2], inc='s_h2B')
                pv['s_h2Blast'] = t_h2

                def shift(e, c=c):
                    if c == 0:
                        e.memset(carry[:], 0.0)
                    e.tensor_copy(bass.AP(h2s_fm, 0, [[D, 128], [128, 8]]),
                                  bass.AP(carry, 0, [[8, 128], [1, 8]]))
                    e.tensor_copy(bass.AP(h2s_fm, 1, [[D, 128], [128, 8], [1, 127]]),
                                  bass.AP(h2_fm, 0, [[D, 128], [128, 8], [1, 127]]))
                    return e.tensor_copy(carry[:],
                                         bass.AP(h2_fm, 127, [[D, 128], [128, 8]]))
                t_sh = P.add('V', shift, waits=[t_h2], inc='v_sh')

                def s_lerp_r(e):
                    last = None
                    for j in range(8):
                        last = e.activation(t2r[:, 128 * j:128 * (j + 1)],
                                            h2s_fm[:, 128 * j:128 * (j + 1)], AF.Identity,
                                            scale=omrcol[:, j:j + 1])
                    return last
                t_lpr = P.add('S', s_lerp_r, waits=[t_sh], inc='s_lerp')

                def v_xr(e):
                    last = None
                    for j in range(8):
                        last = e.scalar_tensor_tensor(
                            xr_fm[:, 128 * j:128 * (j + 1)], h2_fm[:, 128 * j:128 * (j + 1)],
                            murcol[:, j:j + 1], t2r[:, 128 * j:128 * (j + 1)], OP.mult, OP.add)
                    return last
                t_xr = P.add('V', v_xr, waits=[t_lpr], inc='v_xr')

                def s_lerp_k(e):
                    last = None
                    for j in range(8):
                        last = e.activation(t2k[:, 128 * j:128 * (j + 1)],
                                            h2s_fm[:, 128 * j:128 * (j + 1)], AF.Identity,
                                            scale=omkcol[:, j:j + 1])
                    return last
                t_lpk = P.add('S', s_lerp_k, waits=[t_sh], inc='s_lerpk')

                def v_xk(e):
                    last = None
                    for j in range(8):
                        last = e.scalar_tensor_tensor(
                            xk_fm[:, 128 * j:128 * (j + 1)], h2_fm[:, 128 * j:128 * (j + 1)],
                            mukcol[:, j:j + 1], t2k[:, 128 * j:128 * (j + 1)], OP.mult, OP.add)
                    return last
                t_xk = P.add('V', v_xk, waits=[t_lpk], inc='v_xk')

                def mm_ffn(dst, xsb, wsb):
                    def f(e):
                        last = None
                        for half in range(2):
                            for cj in range(8):
                                last = e.matmul(dst[:, 512 * half:512 * (half + 1)],
                                                xsb[:, 128 * cj:128 * (cj + 1)],
                                                wsb[:, D * cj + 512 * half:
                                                     D * cj + 512 * (half + 1)],
                                                start=(cj == 0), stop=(cj == 7))
                        return last
                    return f
                t_rp = P.add('T', mm_ffn(Q1, xr_fm, wrT_s),
                             waits=[t_xr, pv.get('sE2')] + ([bw['wrT'], t_wkT] if i == 0 else []),
                             inc='t_rp')

                t_E2 = P.add('S', lambda e: e.activation(Ebuf[:, 0:1024], Q1[:], AF.Exp,
                                                         scale=cp(C_NEG1)),
                             waits=[t_rp], inc='s_E2')
                pv['sE2'] = t_E2

                def rr2(e):
                    e.activation(Ebuf[:, 0:1024], Ebuf[:, 0:1024], AF.Ln, bias=cp(C_ONE))
                    return e.activation(rr_bf[:], Ebuf[:, 0:1024], AF.Exp, scale=cp(C_NEG1))
                t_rr = P.add('S', rr2, waits=[t_xk], inc='s_rr')

                t_kp = P.add('T', mm_ffn(Q2, xk_fm, wkT_s),
                             waits=[t_xk, pv.get('vgB')] + ([t_wkT] if i == 0 else []),
                             inc='t_kp')
                t_kr = P.add('S', lambda e: e.activation(x2B[:], Q2[:], AF.Relu),
                             waits=[t_kp], inc='s_kr')
                pv['skr'] = t_kr
                t_ka = P.add('V', lambda e: e.tensor_tensor(ka_bf[:], x2B[:], x2B[:], OP.mult),
                             waits=[t_kr, pv.get('dkaT')], inc='v_ka')
                pv['vka'] = t_ka

                t_kaT = P.add('Y', dma_trn(ka_bf, kaf, 8),
                              waits=[t_ka, pv.get('tvo')], inc='d_kaT', inc_k=16)
                pv['dkaT'] = t_kaT

                t_vo = P.add('T', mm_ffn(Q3, kaf, wvT_s),
                             waits=[t_kaT, t_xn] + ([t_wvT] if i == 0 else []), inc='t_vo')
                pv['tvo'] = t_vo

                def v_f(e, b=b, xn=xncur, i=i):
                    t3 = Fbuf[:, (i % 2) * DV:(i % 2) * DV + 1024]
                    e.tensor_tensor(f_bf[:], Q3[:], rr_bf[:], OP.mult)
                    e.tensor_tensor(t3, f_bf[:], gb_rep[:, b * D:(b + 1) * D], OP.mult)
                    return e.tensor_tensor(xn, t3, xn, OP.add)
                t_f = P.add('V', v_f, waits=[t_vo, t_rr], inc='v_f')
                pv['vf'] = t_f

                pv[f'd_out{i}'] = P.add('G', lambda e, row=row, xn=xncur: e.dma_start(
                    out_d[row:row + 128, :], xn), waits=[t_f], inc='d_out', inc_k=16)

            emit_front_mm(0)
            emit_gev()
            emit_front_post(0)
            for i in range(NTOT):
                emit_back(i)
                if i + 1 < NTOT:
                    emit_front_post(i + 1)

            P.add('G', lambda e: e.nop(), waits=[pv[f'd_out{NTOT - 1}']])
            P.run_block(nc)


    return nc


def _consts():
    ib = np.eye(128, dtype=ml_dtypes.bfloat16)
    iff = np.eye(128, dtype=np.float32)
    s = np.arange(128)
    ut = (s[:, None] <= s[None, :]).astype(np.float32)
    mt = -(s[:, None] > s[None, :]).astype(np.float32)
    cm = (s[:, None] <= s[None, :]).astype(np.uint8)
    ho = np.zeros((128, 3), np.float32)
    ho[:, 0] = 1.0
    ho[:64, 1] = 1.0
    ho[64:, 2] = 1.0
    return ib, iff, ut, mt, cm, ho


def kernel(L0_attn_x, L0_attn_Wq, L0_attn_Wk, L0_attn_Wv, L0_attn_Wg,
           L0_attn_Wgk1, L0_attn_Wgk2, L0_attn_bgk2, L0_attn_Wo,
           L0_attn_g_norm_weight, L0_ffn_W_r, L0_ffn_W_k, L0_ffn_W_v,
           L0_ffn_mu_r, L0_ffn_mu_k, L0_c, L0_wmod):
    x = np.ascontiguousarray(np.asarray(L0_attn_x, dtype=np.float32))
    ib, iff, ut, mt, cm, ho = _consts()

    common = dict(
        wq=np.ascontiguousarray(L0_attn_Wq, np.float32),
        wk=np.ascontiguousarray(L0_attn_Wk, np.float32),
        wv=np.ascontiguousarray(L0_attn_Wv, np.float32),
        wg=np.ascontiguousarray(L0_attn_Wg, np.float32),
        wgk1=np.ascontiguousarray(L0_attn_Wgk1, np.float32),
        wgk2=np.ascontiguousarray(L0_attn_Wgk2, np.float32),
        bgk2=np.ascontiguousarray(L0_attn_bgk2, np.float32).reshape(1, D),
        wo=np.ascontiguousarray(L0_attn_Wo, np.float32),
        gnw=np.ascontiguousarray(L0_attn_g_norm_weight, np.float32).reshape(1, DVH),
        wr=np.ascontiguousarray(L0_ffn_W_r, np.float32),
        wk2=np.ascontiguousarray(L0_ffn_W_k, np.float32),
        wv2=np.ascontiguousarray(L0_ffn_W_v, np.float32),
        mur=np.ascontiguousarray(L0_ffn_mu_r, np.float32).reshape(8, 128),
        muk=np.ascontiguousarray(L0_ffn_mu_k, np.float32).reshape(8, 128),
        wmod=np.ascontiguousarray(L0_wmod, np.float32),
        identb=ib, identf=iff, utri=ut, mtri=mt, cmask=cm, hones=ho,
    )
    cmod = np.ascontiguousarray(L0_c, np.float32)

    nc = build_nc(debug=DEBUG)
    in_maps = []
    for core in range(NCORE):
        m = dict(common)
        m["xin"] = np.ascontiguousarray(x[2 * core:2 * core + 2].reshape(TOK, D))
        m["cmod"] = np.ascontiguousarray(cmod[2 * core:2 * core + 2])
        in_maps.append(m)

    res = run_bass_kernel_spmd(nc, in_maps, core_ids=list(range(NCORE)))
    outs = [np.asarray(r["out"]).reshape(NSEQ, N, D) for r in res.results]
    return np.concatenate(outs, axis=0)




# version 11
# speedup vs baseline: 1.3531x; 1.0776x over previous
import contextlib

import numpy as np
import ml_dtypes

import concourse.bass as bass
import concourse.mybir as mybir
from concourse.bass_utils import run_bass_kernel_spmd

F32 = mybir.dt.float32
BF16 = mybir.dt.bfloat16
U8 = mybir.dt.uint8
AF = mybir.ActivationFunctionType
OP = mybir.AluOpType
AXX = mybir.AxisListType.X

NCORE = 8
NSEQ = 2
N = 2048
D = 1024
H = 4
DVH = 512
DV = H * DVH
C = 128
NCH = N // C
NTOT = NSEQ * NCH
TOK = NSEQ * N

LN16 = float(np.log(16.0))
EPS = 1e-5
CLAMP_E = 1e15

DEBUG = False


class Book:
    def __init__(self, nc, ctx):
        self.nc, self.ctx = nc, ctx
        self.sems = {}
        self.counts = {}

    @staticmethod
    def _key(name):
        if name.startswith('d_'):
            return name
        if name[0] == 't':
            return 'ET'
        if name[0] == 'v':
            return 'EV'
        return 'ES'

    def get(self, name):
        key = self._key(name)
        if key not in self.sems:
            self.sems[key] = self.ctx.enter_context(self.nc.semaphore(key))
            self.counts[key] = 0
        return self.sems[key]

    def tok(self, name, k=1):
        self.get(name)
        key = self._key(name)
        self.counts[key] += k
        return (key, self.counts[key], k)


class Prog:

    def __init__(self, book):
        self.book = book
        self.stages = []

    def add(self, eng, emit, waits=(), inc=None, inc_k=1, multi_dma=False):
        tok = self.book.tok(inc, inc_k) if inc is not None else None
        ws = [w for w in waits if w is not None]
        self.stages.append((eng, emit, ws, tok, multi_dma))
        return tok

    def emit_engine(self, eng_name, eng):
        book = self.book
        for e, emit, waits, tok, multi_dma in self.stages:
            if e != eng_name:
                continue
            for w in waits:
                eng.wait_ge(book.sems[w[0]], w[1])
            if multi_dma:
                emit(eng, book.sems[tok[0]])
            else:
                last = emit(eng)
                if tok is not None:
                    assert last is not None
                    last.then_inc(book.sems[tok[0]], tok[2])

    def run_block(self, nc):
        with nc.Block() as blk:
            @blk.tensor
            def _(t):
                self.emit_engine('T', t)

            @blk.vector
            def _(v):
                self.emit_engine('V', v)

            @blk.scalar
            def _(s):
                self.emit_engine('S', s)

            @blk.gpsimd
            def _(g):
                self.emit_engine('G', g)

            @blk.sync
            def _(y):
                self.emit_engine('Y', y)


def tr8(e, pt, src, identb):
    last = None
    for j in range(8):
        last = e.transpose(pt[:, 128 * j:128 * (j + 1)],
                           src[:, 128 * j:128 * (j + 1)], identb[:])
    return last


def build_nc(debug=False):
    nc = bass.Bass("TRN2", target_bir_lowering=False)

    xin = nc.dram_tensor("xin", [TOK, D], F32, kind="ExternalInput").ap()
    out_d = nc.dram_tensor("out", [TOK, D], F32, kind="ExternalOutput").ap()
    o_scr = nc.dram_tensor("o_scr", [TOK, DV], BF16,
                           kind="ExternalOutput" if debug else "Internal").ap()
    wkT_scr = nc.dram_tensor("wkT_scr", [128, 8 * D], BF16).ap()
    wvT_scr = nc.dram_tensor("wvT_scr", [128, 8 * D], BF16).ap()
    h_scr = nc.dram_tensor("h_scr", [TOK, D], BF16).ap()

    wq_d = nc.dram_tensor("wq", [D, D], F32, kind="ExternalInput").ap()
    wk_d = nc.dram_tensor("wk", [D, D], F32, kind="ExternalInput").ap()
    wv_d = nc.dram_tensor("wv", [D, DV], F32, kind="ExternalInput").ap()
    wg_d = nc.dram_tensor("wg", [D, DV], F32, kind="ExternalInput").ap()
    wgk1_d = nc.dram_tensor("wgk1", [D, 16], F32, kind="ExternalInput").ap()
    wgk2_d = nc.dram_tensor("wgk2", [16, D], F32, kind="ExternalInput").ap()
    bgk2_d = nc.dram_tensor("bgk2", [1, D], F32, kind="ExternalInput").ap()
    wo_d = nc.dram_tensor("wo", [DV, D], F32, kind="ExternalInput").ap()
    gnw_d = nc.dram_tensor("gnw", [1, DVH], F32, kind="ExternalInput").ap()
    wr_d = nc.dram_tensor("wr", [D, D], F32, kind="ExternalInput").ap()
    wk2_d = nc.dram_tensor("wk2", [D, D], F32, kind="ExternalInput").ap()
    wv2_d = nc.dram_tensor("wv2", [D, D], F32, kind="ExternalInput").ap()
    mur_d = nc.dram_tensor("mur", [8, 128], F32, kind="ExternalInput").ap()
    muk_d = nc.dram_tensor("muk", [8, 128], F32, kind="ExternalInput").ap()
    cmod_d = nc.dram_tensor("cmod", [NSEQ, D], F32, kind="ExternalInput").ap()
    wmod_d = nc.dram_tensor("wmod", [D, 6 * D], F32, kind="ExternalInput").ap()

    identb_d = nc.dram_tensor("identb", [128, 128], BF16, kind="ExternalInput").ap()
    identf_d = nc.dram_tensor("identf", [128, 128], F32, kind="ExternalInput").ap()
    utri_d = nc.dram_tensor("utri", [128, 128], F32, kind="ExternalInput").ap()
    mtri_d = nc.dram_tensor("mtri", [128, 128], F32, kind="ExternalInput").ap()
    cmask_d = nc.dram_tensor("cmask", [128, 128], U8, kind="ExternalInput").ap()
    hones_d = nc.dram_tensor("hones", [128, 3], F32, kind="ExternalInput").ap()

    dbg = {}
    if debug:
        dbg['h'] = nc.dram_tensor("dbg_h", [TOK, D], BF16, kind="ExternalOutput").ap()
        dbg['sp'] = nc.dram_tensor("dbg_sp", [TOK, D], F32, kind="ExternalOutput").ap()
        dbg['mod'] = nc.dram_tensor("dbg_mod", [NSEQ, 6 * D], F32, kind="ExternalOutput").ap()
        for nm, w in [('eA', D), ('eK', D), ('qh', D), ('kh', D), ('qhfm', D),
                      ('qinfm', D), ('kinfm', D), ('Pm', 512)]:
            dbg[nm] = nc.dram_tensor("dbg_" + nm, [128, w], BF16, kind="ExternalOutput").ap()
        dbg['colv'] = nc.dram_tensor("dbg_colv", [128, 48], F32, kind="ExternalOutput").ap()

    with contextlib.ExitStack() as CTX:
        E = CTX.enter_context
        book = Book(nc, CTX)

        wg_s = E(nc.sbuf_tensor("wg_s", [128, 8 * DV], BF16))
        wrT_s = E(nc.sbuf_tensor("wrT_s", [128, 8 * D], BF16))
        identb = E(nc.sbuf_tensor("identb_s", [128, 128], BF16))
        identf = E(nc.sbuf_tensor("identf_s", [128, 128], F32))
        utri = E(nc.sbuf_tensor("utri_s", [128, 128], F32))
        mtri = E(nc.sbuf_tensor("mtri_s", [128, 128], F32))
        cmask = E(nc.sbuf_tensor("cmask_s", [128, 128], U8))
        zeros = E(nc.sbuf_tensor("zeros", [128, 128], F32))
        onesrow = E(nc.sbuf_tensor("onesrow", [1, 128], F32))
        onescol = E(nc.sbuf_tensor("onescol", [128, 1], F32))
        hones = E(nc.sbuf_tensor("hones_s", [128, 3], F32))
        cpool = E(nc.sbuf_tensor("cpool", [128, 10], F32))
        (C_NEG1, C_NLN16, C_NI16, C_PI16, C_NHALF, C_I512, C_EPS, C_I1024,
         C_ONE, C_SPCL) = range(10)
        CPV = [-1.0, -LN16, -1.0 / 16, 1.0 / 16, -0.5, 1.0 / 512, EPS,
               1.0 / 1024, 1.0, 80.0]

        def cp(i):
            return cpool[:, i:i + 1]

        mod_fm = E(nc.sbuf_tensor("mod_fm", [128, 96], F32))

        def modcol(v, dj, b):
            i = (v * 8 + dj) * 2 + b
            return mod_fm[:, i:i + 1]

        murcol = E(nc.sbuf_tensor("murcol", [128, 8], F32))
        mukcol = E(nc.sbuf_tensor("mukcol", [128, 8], F32))
        omrcol = E(nc.sbuf_tensor("omrcol", [128, 8], F32))
        omkcol = E(nc.sbuf_tensor("omkcol", [128, 8], F32))
        gnw_rep = E(nc.sbuf_tensor("gnw_rep", [128, DV], BF16))
        ga_rep = E(nc.sbuf_tensor("ga_rep", [128, 2 * D], BF16))
        gb_rep = E(nc.sbuf_tensor("gb_rep", [128, 2 * D], BF16))
        S_bf = E(nc.sbuf_tensor("S_bf", [128, H * 2 * DVH], BF16))

        def S_tile(h, j):
            o = (2 * h + j) * DVH
            return S_bf[:, o:o + DVH]

        def wsrc(dr, F, nch=8):
            return bass.AP(dr.tensor, 0, [[F, 128], [128 * F, nch], [1, F]])

        def wdst(sb, F, nch=8):
            return bass.AP(sb, 0, [[nch * F, 128], [F, nch], [1, F]])

        bw = {}

        with contextlib.ExitStack() as SA:
            ES = SA.enter_context
            wq_s = ES(nc.sbuf_tensor("wq_s", [128, 8 * D], BF16))
            wk_s = ES(nc.sbuf_tensor("wk_s", [128, 8 * D], BF16))
            wv_s = ES(nc.sbuf_tensor("wv_s", [128, 8 * DV], BF16))
            wgk1_s = ES(nc.sbuf_tensor("wgk1_s", [128, 8 * 16], BF16))
            wgk2b = ES(nc.sbuf_tensor("wgk2b", [17, D], F32))
            z1ones = ES(nc.sbuf_tensor("z1ones", [17, 128], F32))

            with contextlib.ExitStack() as ST:
                EE = ST.enter_context
                wmod_bf = EE(nc.sbuf_tensor("wmod_bf", [128, 8 * D], BF16))
                cmod_s = EE(nc.sbuf_tensor("cmod_s", [NSEQ, D], F32))
                sc_bf = EE(nc.sbuf_tensor("sc_bf", [NSEQ, D], BF16))
                scfm = EE(nc.sbuf_tensor("scfm", [128, 16], BF16))
                modsb = EE(nc.sbuf_tensor("modsb", [NSEQ, 6 * D], F32))
                mtmp = EE(nc.sbuf_tensor("mtmp", [NSEQ, D], F32))
                mtmp2 = EE(nc.sbuf_tensor("mtmp2", [NSEQ, D], F32))
                murow = EE(nc.sbuf_tensor("murow", [8, 128], F32))
                mukrow = EE(nc.sbuf_tensor("mukrow", [8, 128], F32))
                gnwrow = EE(nc.sbuf_tensor("gnwrow", [1, DVH], F32))
                brow = EE(nc.sbuf_tensor("brow", [1, D], F32))

                pmod = EE(nc.psum_tensor("pmod", [NSEQ, 1024], F32))
                ptr0 = EE(nc.psum_tensor("ptr0", [128, 1024], BF16))
                ptf0 = EE(nc.psum_tensor("ptf0", [128, 512], F32))

                P = Prog(book)

                def dma(dst, src):
                    return lambda e: e.dma_start(dst, src)

                t_con = None
                for dst, src in [(identb[:], identb_d), (identf[:], identf_d),
                                 (utri[:], utri_d), (mtri[:], mtri_d),
                                 (cmask[:], cmask_d),
                                 (wgk2b[0:16, :], wgk2_d),
                                 (wgk2b[16:17, :], bgk2_d),
                                 (cmod_s[:], cmod_d),
                                 (murow[:], mur_d), (mukrow[:], muk_d),
                                 (gnwrow[:], gnw_d), (hones[:], hones_d)]:
                    t_con = P.add('Y', dma(dst, src), inc='d_st', inc_k=16)

                t_wq = P.add('G', dma(wdst(wq_s, D), wsrc(wq_d, D)), inc='d_w', inc_k=16)
                t_wk = P.add('G', dma(wdst(wk_s, D), wsrc(wk_d, D)), inc='d_w', inc_k=16)
                t_wv = P.add('G', dma(wdst(wv_s, DV), wsrc(wv_d, DV)), inc='d_w', inc_k=16)
                t_wgk1 = P.add('G', dma(wdst(wgk1_s, 16), wsrc(wgk1_d, 16)), inc='d_w', inc_k=16)
                t_wg = P.add('G', dma(wdst(wg_s, DV), wsrc(wg_d, DV)), inc='d_w', inc_k=16)
                bw['wg'] = t_wg

                def ms(e):
                    e.memset(zeros[:], 0.0)
                    e.memset(onesrow[:], 1.0)
                    e.memset(onescol[:], 1.0)
                    e.memset(z1ones[:], 1.0)
                    last = e.memset(S_bf[:], 0.0)
                    for i, val in enumerate(CPV):
                        last = e.memset(cpool[:, i:i + 1], float(val))
                    return last
                t_ms = P.add('V', ms, inc='v_st')

                def silu_c(e):
                    e.activation(mtmp[:], cmod_s[:], AF.Exp, scale=cpool[0:NSEQ, C_NEG1:C_NEG1 + 1])
                    e.activation(mtmp[:], mtmp[:], AF.Ln, bias=cpool[0:NSEQ, C_ONE:C_ONE + 1])
                    return e.activation(mtmp[:], mtmp[:], AF.Exp, scale=cpool[0:NSEQ, C_NEG1:C_NEG1 + 1])
                t_sig = P.add('S', silu_c, waits=[t_con, t_ms], inc='s_st')

                t_sc = P.add('V', lambda e: e.tensor_tensor(sc_bf[:], cmod_s[:], mtmp[:], OP.mult),
                             waits=[t_sig], inc='v_st')

                def tr_sc(e):
                    last = None
                    for j in range(8):
                        last = e.transpose(ptr0[:, 2 * j:2 * j + 2],
                                           sc_bf[:, 128 * j:128 * j + 128],
                                           identb[0:NSEQ, 0:NSEQ])
                    return last
                t_trsc = P.add('T', tr_sc, waits=[t_sc], inc='t_st')
                t_evsc = P.add('V', lambda e: e.tensor_copy(scfm[:], ptr0[:, 0:16]),
                               waits=[t_trsc], inc='v_st')

                t_prev_mm = t_evsc
                t_wmld = P.add('G', lambda e: e.dma_start(
                    wdst(wmod_bf, D),
                    bass.AP(wmod_d.tensor, 0, [[6 * D, 128], [128 * 6 * D, 8], [1, D]])),
                    inc='d_w', inc_k=16)
                t_modsb = None
                for v in range(6):
                    def mm_mod(e):
                        last = None
                        for half in range(2):
                            for dj in range(8):
                                last = e.matmul(
                                    pmod[:, 512 * half:512 * (half + 1)],
                                    scfm[:, 2 * dj:2 * dj + 2],
                                    wmod_bf[:, D * dj + 512 * half: D * dj + 512 * (half + 1)],
                                    start=(dj == 0), stop=(dj == 7))
                        return last
                    t_mm = P.add('T', mm_mod, waits=[t_prev_mm, t_wmld], inc='t_st')
                    if v < 5:
                        t_wmld = P.add('G', lambda e, v=v: e.dma_start(
                            wdst(wmod_bf, D),
                            bass.AP(wmod_d.tensor, (v + 1) * D,
                                    [[6 * D, 128], [128 * 6 * D, 8], [1, D]])),
                            waits=[t_mm], inc='d_w', inc_k=16)
                    t_prev_mm = P.add('V', lambda e, v=v: e.tensor_copy(
                        modsb[:, v * D:(v + 1) * D], pmod[:]),
                        waits=[t_mm], inc='v_st')
                t_modsb = t_prev_mm

                def mod_xf(e):
                    e.tensor_single_scalar(modsb[:, 1 * D:2 * D], modsb[:, 1 * D:2 * D], 1.0, OP.add)
                    e.tensor_single_scalar(modsb[:, 4 * D:5 * D], modsb[:, 4 * D:5 * D], 1.0, OP.add)
                    last = None
                    for v in (2, 5):
                        sl = modsb[:, v * D:(v + 1) * D]
                        e.tensor_single_scalar(mtmp[:], sl, 0.0, OP.is_ge)
                        e.tensor_scalar(mtmp[:], mtmp[:], 2.0, -1.0, OP.mult, OP.add)
                        e.tensor_single_scalar(mtmp2[:], sl, -1.0, OP.mult)
                        e.tensor_tensor(mtmp2[:], mtmp2[:], sl, OP.max)
                        e.tensor_single_scalar(mtmp2[:], mtmp2[:], 0.25, OP.max)
                        last = e.tensor_tensor(sl, mtmp[:], mtmp2[:], OP.mult)
                    return last
                t_modxf = P.add('V', mod_xf, waits=[t_modsb], inc='v_st')
                if debug:
                    P.add('Y', dma(dbg['mod'], modsb[:]), waits=[t_modxf], inc='d_dbg', inc_k=16)

                t_prev = t_modxf
                for v in range(6):
                    def tr_mod(e, v=v):
                        last = None
                        for dj in range(8):
                            last = e.transpose(ptf0[:, 2 * dj:2 * dj + 2],
                                               modsb[:, v * D + 128 * dj: v * D + 128 * dj + 128],
                                               identf[0:NSEQ, 0:NSEQ])
                        return last
                    tm = P.add('T', tr_mod, waits=[t_prev], inc='t_st')
                    t_prev = P.add('V', lambda e, v=v: e.tensor_copy(
                        mod_fm[:, v * 16:v * 16 + 16], ptf0[:, 0:16]),
                        waits=[tm], inc='v_st')
                t_modfm = t_prev

                def tr_mu(e):
                    e.transpose(ptf0[:, 32:40], murow[:], identf[0:8, 0:8])
                    return e.transpose(ptf0[:, 40:48], mukrow[:], identf[0:8, 0:8])
                t_trmu = P.add('T', tr_mu, waits=[t_modfm], inc='t_st')

                def ev_mu(e):
                    e.tensor_copy(murcol[:], ptf0[:, 32:40])
                    e.tensor_copy(mukcol[:], ptf0[:, 40:48])
                    e.drain()
                    e.tensor_scalar(omrcol[:], murcol[:], -1.0, 1.0, OP.mult, OP.add)
                    return e.tensor_scalar(omkcol[:], mukcol[:], -1.0, 1.0, OP.mult, OP.add)
                t_evmu = P.add('V', ev_mu, waits=[t_trmu], inc='v_st')

                t_bg = P.add('T', lambda e: e.matmul(ptf0[:, 0:512], onesrow[:], gnwrow[:],
                                                     start=True, stop=True),
                             waits=[t_evmu], inc='t_st')

                def ev_gnw(e):
                    last = None
                    for r in range(H):
                        last = e.tensor_copy(gnw_rep[:, r * 512:(r + 1) * 512], ptf0[:, 0:512])
                    return last
                t_prevb = P.add('V', ev_gnw, waits=[t_bg], inc='v_st')

                for b in range(NSEQ):
                    for rep, v in ((ga_rep, 2), (gb_rep, 5)):
                        t_row = P.add('Y', lambda e, b=b, v=v: e.dma_start(
                            brow[:], modsb[b:b + 1, v * D:(v + 1) * D]),
                            waits=[t_prevb], inc='d_st', inc_k=16)
                        for half in range(2):
                            tb = P.add('T', lambda e, half=half: e.matmul(
                                ptf0[:, 0:512], onesrow[:],
                                brow[:, 512 * half:512 * (half + 1)],
                                start=True, stop=True),
                                waits=[t_prevb, t_row], inc='t_st')
                            t_prevb = P.add('V', lambda e, b=b, rep=rep, half=half: e.tensor_copy(
                                rep[:, b * D + 512 * half: b * D + 512 * (half + 1)],
                                ptf0[:, 0:512]),
                                waits=[tb], inc='v_st')

                P.run_block(nc)

            with contextlib.ExitStack() as AS:
                EA = AS.enter_context
                xa2 = EA(nc.sbuf_tensor("xa2", [128, 2 * D], F32))
                y_bf = EA(nc.sbuf_tensor("y_bf", [128, D], BF16))
                hraw = EA(nc.sbuf_tensor("hraw", [128, D], BF16))
                h_fm = EA(nc.sbuf_tensor("h_fm", [128, D], BF16))
                q_bf = EA(nc.sbuf_tensor("q_bf", [128, D], BF16))
                k_bf = EA(nc.sbuf_tensor("k_bf", [128, D], BF16))
                v_bf = EA(nc.sbuf_tensor("v_bf", [128, DV], BF16))
                z1_f = EA(nc.sbuf_tensor("z1_f", [128, 16], F32))
                sp_f = EA(nc.sbuf_tensor("sp_f", [128, D], F32))
                eA_bf = EA(nc.sbuf_tensor("eA_bf", [128, D], BF16))
                eK_bf = EA(nc.sbuf_tensor("eK_bf", [128, D], BF16))
                qh_tm = EA(nc.sbuf_tensor("qh_tm", [128, D], BF16))
                kh_tm = EA(nc.sbuf_tensor("kh_tm", [128, D], BF16))
                qh_fm = EA(nc.sbuf_tensor("qh_fm", [128, D], BF16))
                qin_fm = EA(nc.sbuf_tensor("qin_fm", [128, D], BF16))
                kin_fm = EA(nc.sbuf_tensor("kin_fm", [128, D], BF16))
                Pm = EA(nc.sbuf_tensor("Pm", [128, 512], BF16))
                on_sb = EA(nc.sbuf_tensor("on_sb", [128, DV], BF16))
                wtr_bf = EA(nc.sbuf_tensor("wtr_bf", [128, 8 * D], BF16))
                ln_t = EA(nc.sbuf_tensor("ln_t", [128, 8], F32))
                x2a = EA(nc.sbuf_tensor("x2a", [128, D], BF16))
                EbA = EA(nc.sbuf_tensor("EbA", [128, D], BF16))
                colv = EA(nc.sbuf_tensor("colv", [128, 48], F32))
                ss4 = EA(nc.sbuf_tensor("ss4", [128, 8], F32))

                P1 = EA(nc.psum_tensor("P1_A", [128, 1024], F32))
                P2 = EA(nc.psum_tensor("P2_A", [128, 1024], F32))
                P3 = EA(nc.psum_tensor("P3_A", [128, 1024], F32))
                P4 = EA(nc.psum_tensor("P4_A", [128, 512], F32))
                P5 = EA(nc.psum_tensor("P5_A", [128, 512], F32))
                pS = P3

                P = Prog(book)
                pv = {}

                def dma_tr8(src, dst):
                    sap = bass.AP(src, 0, [[1024, 128], [128, 8], [1, 128]])
                    dap = bass.AP(dst, 0, [[1024, 128], [128, 8], [1, 128]])
                    return lambda e: e.dma_start(dap, sap, transpose=True)

                pv['d_x'] = P.add('G', lambda e: e.dma_start(xa2[:, 0:D], xin[0:128, :]),
                                  inc='d_x', inc_k=16)
                pv['d_x1'] = P.add('G', lambda e: e.dma_start(xa2[:, D:2 * D], xin[128:256, :]),
                                   inc='d_x', inc_k=16)

                def trs_group(dst_s, cj0):
                    def f(e, sem):
                        for cj in (cj0, cj0 + 1):
                            for dj in range(8):
                                e.dma_start(
                                    dst_s[:, 1024 * cj + 128 * dj: 1024 * cj + 128 * dj + 128],
                                    wtr_bf[:, 1024 * dj + 128 * cj: 1024 * dj + 128 * cj + 128],
                                    transpose=True).then_inc(sem, 16)
                        return None
                    return f
                bjob = []
                bjob.append(('G', dma(wdst(wtr_bf, D), wsrc(wk2_d, D)), 16))
                for cj0 in range(0, 8, 2):
                    bjob.append(('S', trs_group(wrT_s, cj0), 16 * 16))
                bjob.append(('S', lambda e: e.dma_start(wkT_scr, wrT_s[:]), 16))
                bjob.append(('G', dma(wdst(wtr_bf, D), wsrc(wv2_d, D)), 16))
                for cj0 in range(0, 8, 2):
                    bjob.append(('S', trs_group(wrT_s, cj0), 16 * 16))
                bjob.append(('S', lambda e: e.dma_start(wvT_scr, wrT_s[:]), 16))
                bjob.append(('G', dma(wdst(wtr_bf, D), wsrc(wr_d, D)), 16))
                for cj0 in range(0, 8, 2):
                    bjob.append(('S', trs_group(wrT_s, cj0), 16 * 16))

                bstate = {'prev': None, 'idx': 0}

                def inject_bjob():
                    if bstate['idx'] >= len(bjob):
                        return
                    eng, emit, k = bjob[bstate['idx']]
                    sem = 'd_wt' if eng in ('Y', 'S') else 'd_w'
                    bstate['prev'] = P.add(eng, emit, waits=[bstate['prev']],
                                           inc=sem, inc_k=k, multi_dma=(k > 16))
                    bstate['idx'] += 1

                def emit_frontA(i):
                    b = i // NCH
                    xa = xa2[:, (i % 2) * D:(i % 2 + 1) * D]
                    t_x2 = P.add('S', lambda e, xa=xa: e.activation(x2a[:], xa, AF.Square),
                                 waits=[pv['d_x'], pv.get('vm2r')], inc='s_x2')
                    t_m = P.add('V', lambda e, xa=xa: e.tensor_reduce(
                        ln_t[:, 0:1], xa, AXX, OP.add), waits=[pv['d_x']], inc='v_m')
                    t_m2 = P.add('V', lambda e: e.tensor_reduce(ln_t[:, 2:3], x2a[:],
                                                                AXX, OP.add),
                                 waits=[t_x2], inc='v_m2r')
                    pv['vm2r'] = t_m2

                    def s_stats(e):
                        e.activation(ln_t[:, 1:2], ln_t[:, 0:1], AF.Identity, scale=cp(C_I1024))
                        e.activation(ln_t[:, 4:5], ln_t[:, 2:3], AF.Identity,
                                     scale=cp(C_I1024))
                        e.drain()
                        return e.activation(ln_t[:, 3:4], ln_t[:, 1:2], AF.Square)
                    t_st_ = P.add('S', s_stats, waits=[t_m, t_m2], inc='s_stats')

                    t_var = P.add('V', lambda e: e.tensor_tensor(
                        ln_t[:, 5:6], ln_t[:, 4:5], ln_t[:, 3:4], OP.subtract),
                        waits=[t_st_], inc='v_var')

                    def rstd(e):
                        e.activation(ln_t[:, 6:7], ln_t[:, 5:6], AF.Ln, bias=cp(C_EPS))
                        e.drain()
                        return e.activation(ln_t[:, 6:7], ln_t[:, 6:7], AF.Exp, scale=cp(C_NHALF))
                    t_rstd = P.add('S', rstd, waits=[t_var], inc='s_rstd')

                    t_y = P.add('V', lambda e, xa=xa: e.tensor_scalar(
                        y_bf[:], xa, ln_t[:, 1:2], ln_t[:, 6:7], OP.subtract, OP.mult),
                        waits=[t_rstd, pv.get('d_ty')], inc='v_y')

                    if i + 2 < NTOT:
                        pv['xq'].append(P.add(
                            'G', lambda e, r=128 * (i + 2), p=(i % 2): e.dma_start(
                                xa2[:, p * D:(p + 1) * D], xin[r:r + 128, :]),
                            waits=[t_y], inc='d_x', inc_k=16))

                    t_ty = P.add('Y', dma_tr8(y_bf, hraw),
                                 waits=[t_y, pv.get('s_h')], inc='d_ty', inc_k=16)
                    pv['d_ty'] = t_ty

                    def hev(e, b=b):
                        last = None
                        for j in range(8):
                            last = e.activation(h_fm[:, 128 * j:128 * (j + 1)],
                                                hraw[:, 128 * j:128 * (j + 1)], AF.Identity,
                                                scale=modcol(1, j, b), bias=modcol(0, j, b))
                        return last
                    t_h = P.add('S', hev, waits=[t_ty, pv.get('tvb'), pv.get('d_h')],
                                inc='s_h')
                    pv['s_h'] = t_h
                    pv['d_h'] = P.add('G', lambda e, row=128 * i: e.dma_start(
                        h_scr[row:row + 128, :], h_fm[:]), waits=[t_h],
                        inc='d_h', inc_k=16)

                def emit_gk(i):
                    def mm_z1(e):
                        last = None
                        for dj in range(8):
                            last = e.matmul(P4[:, 0:16], h_fm[:, 128 * dj:128 * (dj + 1)],
                                            wgk1_s[:, 16 * dj:16 * (dj + 1)],
                                            start=(dj == 0), stop=(dj == 7))
                        return last
                    t_z1 = P.add('T', mm_z1,
                                 waits=[pv.get('scols'), pv['s_h']] +
                                       ([t_wq] if i == 0 else []),
                                 inc='t_z1')
                    t_z1e = P.add('V', lambda e: e.tensor_copy(z1_f[:], P4[:, 0:16]),
                                  waits=[t_z1], inc='v_z1')
                    t_z1t = P.add('T', lambda e: e.transpose(P4[0:16, 128:256], z1_f[:],
                                                             identf[:]),
                                  waits=[t_z1e], inc='t_z1t')
                    t_z1te = P.add('V', lambda e: e.tensor_copy(z1ones[0:16, :],
                                                                P4[0:16, 128:256]),
                                   waits=[t_z1t], inc='v_z1t')

                    def mm_gkp(e):
                        e.matmul(P4[:, 0:512], z1ones[:], wgk2b[:, 0:512], start=True, stop=True)
                        return e.matmul(P5[:, 0:512], z1ones[:], wgk2b[:, 512:1024],
                                        start=True, stop=True)
                    t_gkp = P.add('T', mm_gkp, waits=[t_z1te, pv.get('vP5')], inc='t_gkp')

                    def s_E(e):
                        e.activation(EbA[:, 0:512], P4[:, 0:512], AF.Exp, scale=cp(C_NEG1))
                        return e.activation(EbA[:, 512:1024], P5[:, 0:512], AF.Exp,
                                            scale=cp(C_NEG1))
                    t_E = P.add('S', s_E, waits=[t_gkp], inc='s_E')
                    pv['sE'] = t_E
                    t_sp0 = P.add('S', lambda e: e.activation(sp_f[:], EbA[:], AF.Ln,
                                                              bias=cp(C_ONE)),
                                  waits=[pv.get('tcols'), pv.get('dbg_sp')], inc='s_sp0')
                    pv['sp0'] = t_sp0
                    pv['vP5'] = pv['sE']

                pv['xq'] = [pv['d_x1']]
                emit_frontA(0)
                emit_gk(0)
                for i in range(NTOT):
                    b = i // NCH
                    c = i % NCH
                    row = 128 * i

                    if c == 0:
                        pv['sinit'] = P.add('V', lambda e: e.memset(S_bf[:], 0.0),
                                            inc='v_sinit')
                    t_h = pv['s_h']

                    def mm_proj(dst, wsb, stride, base):
                        def f(e):
                            last = None
                            for half in range(2):
                                for dj in range(8):
                                    last = e.matmul(
                                        dst[:, 512 * half:512 * (half + 1)],
                                        h_fm[:, 128 * dj:128 * (dj + 1)],
                                        wsb[:, stride * dj + base + 512 * half:
                                            stride * dj + base + 512 * (half + 1)],
                                        start=(dj == 0), stop=(dj == 7))
                            return last
                        return f

                    t_q = P.add('T', mm_proj(P1, wq_s, D, 0),
                                waits=[t_h, pv.get('von')] + ([t_wq] if i == 0 else []),
                                inc='t_q')
                    t_k = P.add('T', mm_proj(P2, wk_s, D, 0),
                                waits=([t_wk] if i == 0 else []), inc='t_k')
                    t_va = P.add('T', mm_proj(P3, wv_s, DV, 0),
                                 waits=[pv.get('stt3')] + ([t_wv] if i == 0 else []),
                                 inc='t_va')

                    t_qe = P.add('V', lambda e: e.tensor_copy(q_bf[:], P1[:]),
                                 waits=[t_q], inc='v_q')
                    t_ke = P.add('V', lambda e: e.tensor_copy(k_bf[:], P2[:]),
                                 waits=[t_k], inc='v_k')
                    t_vae = P.add('V', lambda e: e.tensor_copy(v_bf[:, 0:1024], P3[:]),
                                  waits=[t_va], inc='v_va')
                    t_vb = P.add('T', mm_proj(P1, wv_s, DV, 1024), waits=[t_qe], inc='t_vb')
                    pv['tvb'] = t_vb
                    t_vbe = P.add('V', lambda e: e.tensor_copy(v_bf[:, 1024:2048], P1[:]),
                                  waits=[t_vb], inc='v_vb')

                    def mm_cum(dst, tri):
                        def f(e):
                            e.matmul(dst[:, 0:512], tri[:], sp_f[:, 0:512], start=True, stop=True)
                            return e.matmul(dst[:, 512:1024], tri[:], sp_f[:, 512:1024],
                                            start=True, stop=True)
                        return f
                    pv['spcl'] = P.add('V', lambda e: e.tensor_single_scalar(
                        sp_f[:], sp_f[:], 80.0, OP.min), waits=[pv['sp0']], inc='v_spcl')
                    t_cs = P.add('T', mm_cum(P1, utri), waits=[pv['spcl'], t_vbe], inc='t_cs')
                    t_t1 = P.add('T', mm_cum(P2, mtri), waits=[t_ke], inc='t_t1')

                    def mm_cols(e):
                        last = None
                        for j in range(8):
                            last = e.matmul(P4[:, 256 + 3 * j:259 + 3 * j],
                                            sp_f[:, 128 * j:128 * (j + 1)], hones[:],
                                            start=True, stop=True)
                        return last
                    t_cols = P.add('T', mm_cols, waits=[pv['spcl'], pv.get('sE')], inc='t_cols')
                    pv['tcols'] = t_cols

                    t_eA = P.add('S', lambda e: e.activation(eA_bf[:], P1[:], AF.Exp,
                                                             scale=cp(C_NI16), bias=cp(C_NLN16)),
                                 waits=[t_cs], inc='s_eA')
                    t_eK = P.add('S', lambda e: e.activation(eK_bf[:], P2[:], AF.Exp,
                                                             scale=cp(C_PI16)),
                                 waits=[t_t1], inc='s_eK')

                    spC_ap = bass.AP(P4, 256, [[512, 128], [3, 8]])
                    spm_ap = bass.AP(P4, 257, [[512, 128], [3, 8]])
                    sph_ap = bass.AP(P4, 258, [[512, 128], [3, 8]])

                    def v_colpre(e):
                        e.tensor_single_scalar(colv[:, 32:40], spm_ap, 552.0, OP.min)
                        return e.tensor_single_scalar(colv[:, 8:16], sph_ap, 552.0, OP.min)
                    t_vcp = P.add('V', v_colpre, waits=[t_cols], inc='v_colp')

                    def s_cols(e):
                        e.activation(colv[:, 0:8], spC_ap, AF.Exp, scale=cp(C_NI16))
                        e.activation(colv[:, 16:24], colv[:, 32:40], AF.Exp, scale=cp(C_PI16))
                        return e.activation(colv[:, 24:32], colv[:, 8:16], AF.Exp,
                                            scale=cp(C_PI16))
                    t_vc = P.add('S', s_cols, waits=[t_vcp], inc='s_cols')
                    pv['scols'] = t_vc

                    if i + 1 < NTOT:
                        pv['d_x'] = pv['xq'].pop(0)
                        emit_frontA(i + 1)
                        emit_gk(i + 1)

                    t_qh = P.add('V', lambda e: e.tensor_tensor(qh_tm[:], q_bf[:], eA_bf[:],
                                                                OP.mult),
                                 waits=[t_eA], inc='v_qh')
                    t_kh = P.add('V', lambda e: e.tensor_tensor(kh_tm[:], k_bf[:], eK_bf[:],
                                                                OP.mult),
                                 waits=[t_eK], inc='v_kh')

                    t_tq = P.add('Y', dma_tr8(qh_tm, qh_fm),
                                 waits=[t_qh, pv.get('tio3')], inc='d_tq', inc_k=16)
                    pv['d_tq'] = t_tq

                    def ev_tq(e):
                        last = None
                        for j in range(8):
                            last = e.tensor_single_scalar(qin_fm[:, 128 * j:128 * (j + 1)],
                                                          qh_fm[:, 128 * j:128 * (j + 1)],
                                                          colv[:, 16 + j:17 + j], OP.mult)
                        return last
                    t_tqe = P.add('V', ev_tq, waits=[t_tq, t_vc, pv.get('tP')], inc='v_tqe')

                    t_tk = P.add('Y', dma_tr8(kh_tm, kin_fm),
                                 waits=[t_kh, pv.get('tP')], inc='d_tk', inc_k=16)
                    pv['d_tkp'] = t_tk

                    def ev_tk(e):
                        last = None
                        for j in range(8):
                            last = e.tensor_single_scalar(kin_fm[:, 128 * j:128 * (j + 1)],
                                                          kin_fm[:, 128 * j:128 * (j + 1)],
                                                          colv[:, 24 + j:25 + j], OP.mult)
                        return last
                    t_tke = P.add('V', ev_tk, waits=[t_tk, t_vc], inc='v_tke')

                    def mm_state(h):
                        def f(e):
                            last = None
                            for j in range(2):
                                last = e.matmul(pS[:, 512 * j:512 * (j + 1)],
                                                kh_tm[:, 256 * h + 128 * j:
                                                      256 * h + 128 * (j + 1)],
                                                v_bf[:, 512 * h:512 * (h + 1)],
                                                start=True, stop=True)
                            return last
                        return f

                    t_st = [None] * 4
                    t_st[0] = P.add('T', mm_state(0), waits=[t_kh, t_vae], inc='t_s0')

                    ps = [P1[:, 0:128], P1[:, 512:640], P2[:, 0:128], P2[:, 512:640]]
                    po = [P1[:, 0:512], P1[:, 512:1024], P2[:, 0:512], P2[:, 512:1024]]

                    def mm_P(e):
                        last = None
                        for h in range(4):
                            for j in range(2):
                                dblk = 2 * h + j
                                last = e.matmul(ps[h],
                                                kin_fm[:, 128 * dblk:128 * (dblk + 1)],
                                                qin_fm[:, 128 * dblk:128 * (dblk + 1)],
                                                start=(j == 0), stop=(j == 1))
                        return last
                    t_P = P.add('T', mm_P, waits=[t_tke, t_tqe, t_eA, t_eK], inc='t_P')
                    pv['tP'] = t_P

                    def v_sel(e):
                        last = None
                        for h in range(4):
                            last = e.select(Pm[:, 128 * h:128 * (h + 1)], cmask[:], ps[h],
                                            zeros[:])
                        return last
                    t_Pm = P.add('V', v_sel, waits=[t_P], inc='v_Pm')

                    t_io = [None] * 4
                    t_stt = [None] * 4
                    for h in range(4):
                        def mm_io(e, h=h):
                            e.matmul(po[h], Pm[:, 128 * h:128 * (h + 1)],
                                     v_bf[:, 512 * h:512 * (h + 1)], start=True, stop=False)
                            last = None
                            for j in range(2):
                                last = e.matmul(po[h],
                                                qh_fm[:, 128 * (2 * h + j):128 * (2 * h + j + 1)],
                                                S_tile(h, j), start=False, stop=(j == 1))
                            return last
                        io_w = []
                        if h == 0:
                            io_w = [t_Pm, pv.get('stt3'), t_tq]
                            if c == 0:
                                io_w.append(pv['sinit'])
                        t_io[h] = P.add('T', mm_io, waits=io_w, inc=f'tio{h}')

                        def v_stt(e, h=h):
                            last = None
                            for j in range(2):
                                last = e.scalar_tensor_tensor(
                                    S_tile(h, j), S_tile(h, j), colv[:, 2 * h + j:2 * h + j + 1],
                                    pS[:, 512 * j:512 * (j + 1)], OP.mult, OP.add)
                            return last
                        t_stt[h] = P.add('V', v_stt, waits=[t_st[h], t_io[h], t_vc],
                                         inc=f'vstt{h}')
                        if h < 3:
                            t_st[h + 1] = P.add('T', mm_state(h + 1),
                                                waits=[t_stt[h]], inc=f'ts{h + 1}')
                    pv['stt3'] = t_stt[3]
                    pv['tst3'] = t_st[3]
                    pv['tio3'] = t_io[3]

                    def s_rms(e):
                        for h in range(4):
                            e.activation(qh_tm[:, 0:512], po[h], AF.Square,
                                         accum_out=ss4[:, h:h + 1])
                        e.drain()
                        e.activation(ss4[:, 4:8], ss4[:, 0:4], AF.Ln, scale=cp(C_I512),
                                     bias=cp(C_EPS))
                        e.drain()
                        return e.activation(ss4[:, 4:8], ss4[:, 4:8], AF.Exp, scale=cp(C_NHALF))
                    t_rms = P.add('S', s_rms, waits=[t_io[3]], inc='s_rms')

                    def v_on(e):
                        last = None
                        for h in range(4):
                            last = e.tensor_single_scalar(on_sb[:, 512 * h:512 * (h + 1)],
                                                          po[h], ss4[:, 4 + h:5 + h], OP.mult)
                        return last
                    t_on = P.add('V', v_on, waits=[t_rms, pv.get('d_on')], inc='v_on')
                    pv['von'] = t_on

                    pv['d_on'] = P.add('G', lambda e, row=row: e.dma_start(
                        o_scr[row:row + 128, :], on_sb[:]), waits=[t_on],
                        inc='d_on', inc_k=16)

                    if debug:
                        pv['dbg_h'] = P.add('Y', lambda e, row=row: e.dma_start(
                            dbg['h'][row:row + 128, :], h_fm[:]), waits=[t_h],
                            inc='d_dbg', inc_k=16)
                        pv['dbg_sp'] = P.add('Y', lambda e, row=row: e.dma_start(
                            dbg['sp'][row:row + 128, :], sp_f[:]),
                            waits=[t_spcl], inc='d_dbg', inc_k=16)

                    inject_bjob()

                assert bstate['idx'] == len(bjob)
                bw['wrT'] = bstate['prev']
                P.add('G', lambda e: e.nop(), waits=[pv['d_on'], pv['d_h']])
                P.run_block(nc)


        with contextlib.ExitStack() as BS:
            EB = BS.enter_context
            wo_s = EB(nc.sbuf_tensor("wo_s", [128, 16 * D], BF16))
            wkT_s = EB(nc.sbuf_tensor("wkT_s", [128, 8 * D], BF16))
            wvT_s = EB(nc.sbuf_tensor("wvT_s", [128, 8 * D], BF16))

            xb2 = EB(nc.sbuf_tensor("xb2", [128, 2 * D], F32))
            onb = EB(nc.sbuf_tensor("onb", [128, 2 * DV], BF16))
            y_b = EB(nc.sbuf_tensor("y_b", [128, D], BF16))
            hraw2b = EB(nc.sbuf_tensor("hraw2b", [128, D], BF16))
            hB2 = EB(nc.sbuf_tensor("hB2", [128, 2 * D], BF16))
            h2d = EB(nc.sbuf_tensor("h2d", [128, 2 * D], BF16))
            h2s_fm = EB(nc.sbuf_tensor("h2s_fm", [128, D], BF16))
            carry = EB(nc.sbuf_tensor("carry", [128, 8], BF16))
            g_bf = EB(nc.sbuf_tensor("g_bf", [128, DV], BF16))
            Ebuf = EB(nc.sbuf_tensor("Ebuf", [128, DV], BF16))
            Fbuf = EB(nc.sbuf_tensor("Fbuf", [128, 2 * DV], BF16))
            og_fm = EB(nc.sbuf_tensor("og_fm", [128, DV], BF16))
            tr_bf = EB(nc.sbuf_tensor("tr_bf", [128, D], BF16))
            xnew = EB(nc.sbuf_tensor("xnew", [128, 2 * D], F32))
            xr_fm = EB(nc.sbuf_tensor("xr_fm", [128, D], BF16))
            xk_fm = EB(nc.sbuf_tensor("xk_fm", [128, D], BF16))
            t2r = EB(nc.sbuf_tensor("t2r", [128, D], BF16))
            ka_bf = EB(nc.sbuf_tensor("ka_bf", [128, D], BF16))
            kaf = EB(nc.sbuf_tensor("kaf", [128, D], BF16))
            rr_bf = t2r
            f_bf = ka_bf
            t2k = tr_bf
            x2B = EB(nc.sbuf_tensor("x2B", [128, D], BF16))
            ln2B = EB(nc.sbuf_tensor("ln2B", [128, 8], F32))

            Q1 = EB(nc.psum_tensor("Q1_B", [128, 1024], F32))
            Q2 = EB(nc.psum_tensor("Q2_B", [128, 1024], F32))
            Q3 = EB(nc.psum_tensor("Q3_B", [128, 1024], F32))
            Q4 = EB(nc.psum_tensor("Q4_B", [128, 1024], F32))

            P = Prog(book)
            pv = {}

            def dma_trn(src, dst, n):
                w = 128 * n
                sap = bass.AP(src, 0, [[w, 128], [128, n], [1, 128]])
                dap = bass.AP(dst, 0, [[w, 128], [128, n], [1, 128]])
                return lambda e: e.dma_start(dap, sap, transpose=True)

            t_wo = P.add('G', lambda e: e.dma_start(wdst(wo_s, D, 16), wsrc(wo_d, D, 16)),
                         inc='d_w', inc_k=16)
            t_wkT = P.add('G', lambda e: e.dma_start(wkT_s[:], wkT_scr),
                          inc='d_w', inc_k=16)
            t_wvT = P.add('G', lambda e: e.dma_start(wvT_s[:], wvT_scr),
                          inc='d_w', inc_k=16)

            pv['d_x0'] = P.add('G', lambda e: e.dma_start(xb2[:, 0:D], xin[0:128, :]),
                               inc='d_x2', inc_k=16)
            pv['d_x1'] = P.add('G', lambda e: e.dma_start(xb2[:, D:2 * D], xin[128:256, :]),
                               inc='d_x2', inc_k=16)
            pv['d_o0'] = P.add('G', lambda e: e.dma_start(onb[:, 0:DV], o_scr[0:128, :]),
                               inc='d_on2', inc_k=16)
            pv['d_hB0'] = P.add('G', lambda e: e.dma_start(hB2[:, 0:D], h_scr[0:128, :]),
                                inc='d_hB', inc_k=16)

            def emit_front_mm(i):
                hcur = hB2[:, (i % 2) * D:(i % 2 + 1) * D]
                d_h_tok = pv[f'd_hB{i}'] if i else pv['d_hB0']
                if i + 1 < NTOT:
                    pv[f'd_hB{i + 1}'] = P.add(
                        'G', lambda e, r=128 * (i + 1), nb=(i + 1) % 2: e.dma_start(
                            hB2[:, nb * D:(nb + 1) * D], h_scr[r:r + 128, :]),
                        waits=[pv.get('tgB')], inc='d_hB', inc_k=16)

                def mm_g(dst, base):
                    def f(e):
                        last = None
                        for half in range(2):
                            for dj in range(8):
                                last = e.matmul(dst[:, 512 * half:512 * (half + 1)],
                                                hcur[:, 128 * dj:128 * (dj + 1)],
                                                wg_s[:, DV * dj + base + 512 * half:
                                                     DV * dj + base + 512 * (half + 1)],
                                                start=(dj == 0), stop=(dj == 7))
                        return last
                    return f
                t_gA = P.add('T', mm_g(Q4, 0),
                             waits=[d_h_tok, pv.get('vgA')] + ([bw['wg']] if i == 0 else []),
                             inc='t_gA')
                t_gB = P.add('T', mm_g(Q2, 1024),
                             waits=[pv.get('skr') if i >= 2 else pv.get('vgB')], inc='t_gB')
                pv['tgB'] = t_gB
                pv['pend_g'] = (t_gA, t_gB)

            def emit_gev():
                tga, tgb = pv.pop('pend_g')
                t_gBe = P.add('V', lambda e: e.tensor_copy(g_bf[:, 1024:2048], Q2[:]),
                              waits=[tgb], inc='v_gB')
                pv['vgB'] = t_gBe
                t_gAe = P.add('V', lambda e: e.tensor_copy(g_bf[:, 0:1024], Q4[:]),
                              waits=[tga], inc='v_gA')
                pv['vgA'] = t_gAe
                pv['gev'] = (t_gAe, t_gBe)

            def emit_sg(i):
                t_gAe, t_gBe = pv['gev']

                def sg(e):
                    e.activation(Ebuf[:], g_bf[:], AF.Exp, scale=cp(C_NEG1))
                    e.activation(Ebuf[:], Ebuf[:], AF.Ln, bias=cp(C_ONE))
                    return e.activation(Ebuf[:], Ebuf[:], AF.Exp, scale=cp(C_NEG1))
                pv['sg'] = P.add('S', sg, waits=[t_gAe, t_gBe], inc='s_sg')

            def emit_F(i):
                def fchain(e, i=i):
                    fb = Fbuf[:, (i % 2) * DV:(i % 2 + 1) * DV]
                    e.tensor_tensor(fb, g_bf[:], Ebuf[:], OP.mult)
                    return e.tensor_tensor(fb, fb, gnw_rep[:], OP.mult)
                t_F = P.add('V', fchain, waits=[pv['sg'], pv.get('vog2'), pv.get('vf')],
                            inc='v_F')
                pv[f'vF{i}'] = t_F

            def mm_ffn(dst, xsb, wsb):
                def f(e):
                    last = None
                    for half in range(2):
                        for cj in range(8):
                            last = e.matmul(dst[:, 512 * half:512 * (half + 1)],
                                            xsb[:, 128 * cj:128 * (cj + 1)],
                                            wsb[:, D * cj + 512 * half:
                                                 D * cj + 512 * (half + 1)],
                                            start=(cj == 0), stop=(cj == 7))
                    return last
                return f

            def emit_P_og(i):
                row = 128 * i
                ocur = onb[:, (i % 2) * DV:(i % 2 + 1) * DV]
                d_o_tok = pv[f'd_o{i}'] if i else pv['d_o0']
                fslot = Fbuf[:, (i % 2) * DV:(i % 2 + 1) * DV]
                t_og = P.add('V', lambda e, oc=ocur, fs=fslot: e.tensor_tensor(
                    fs, oc, fs, OP.mult),
                    waits=[d_o_tok, pv[f'vF{i}'], pv.get('dogT')], inc='v_og')
                pv['vog2'] = t_og
                if i + 1 < NTOT:
                    nb = (i + 1) % 2
                    pv[f'd_o{i + 1}'] = P.add('G', lambda e, r=row + 128, nb=nb: e.dma_start(
                        onb[:, nb * DV:(nb + 1) * DV], o_scr[r:r + 128, :]),
                        waits=[t_og], inc='d_on2', inc_k=16)

                sap16 = bass.AP(Fbuf, (i % 2) * DV, [[2 * DV, 128], [128, 16], [1, 128]])
                dap16 = bass.AP(og_fm, 0, [[DV, 128], [128, 16], [1, 128]])
                t_ogT = P.add('Y', lambda e, sap=sap16, dap=dap16: e.dma_start(
                    dap, sap, transpose=True),
                    waits=[t_og, pv.get('two')], inc='d_ogT', inc_k=16)
                pv['dogT'] = t_ogT

                def mm_wo(e):
                    last = None
                    for half in range(2):
                        for vj in range(16):
                            last = e.matmul(Q3[:, 512 * half:512 * (half + 1)],
                                            og_fm[:, 128 * vj:128 * (vj + 1)],
                                            wo_s[:, D * vj + 512 * half:
                                                 D * vj + 512 * (half + 1)],
                                            start=(vj == 0), stop=(vj == 15))
                    return last
                t_wom = P.add('T', mm_wo,
                              waits=[t_ogT, pv.get('vxn')] + ([t_wo] if i == 0 else []),
                              inc='t_wo')
                pv['two'] = t_wom

            def emit_P_ln(i):
                b = i // NCH
                xcur = xb2[:, (i % 2) * D:(i % 2 + 1) * D]
                xncur = xnew[:, (i % 2) * D:(i % 2 + 1) * D]

                def v_xn(e, b=b, xc=xcur, xn=xncur):
                    e.tensor_tensor(tr_bf[:], Q3[:], ga_rep[:, b * D:(b + 1) * D], OP.mult)
                    return e.tensor_tensor(xn, tr_bf[:], xc, OP.add)
                t_xn = P.add('V', v_xn,
                             waits=[pv['two'], pv.get(f'd_out{i - 2}'),
                                    pv[f'd_x{i}'] if i else pv['d_x0']], inc='v_xn')
                pv['vxn'] = t_xn
                if i + 2 < NTOT:
                    pv[f'd_x{i + 2}'] = P.add(
                        'G', lambda e, r=128 * (i + 2), nb=i % 2: e.dma_start(
                            xb2[:, nb * D:(nb + 1) * D], xin[r:r + 128, :]),
                        waits=[t_xn], inc='d_x2', inc_k=16)

                t_x22 = P.add('S', lambda e, xn=xncur: e.activation(
                    x2B[:], xn, AF.Square, accum_out=ln2B[:, 2:3]),
                    waits=[t_xn, pv.get('vka')], inc='s_x2B2')
                t_m2 = P.add('V', lambda e, xn=xncur: e.tensor_reduce(
                    ln2B[:, 0:1], xn, AXX, OP.add), waits=[t_xn], inc='v_m2')

                def s_stats2(e):
                    e.drain()
                    e.activation(ln2B[:, 1:2], ln2B[:, 0:1], AF.Identity, scale=cp(C_I1024))
                    e.activation(ln2B[:, 4:5], ln2B[:, 2:3], AF.Identity,
                                 scale=cp(C_I1024))
                    e.drain()
                    return e.activation(ln2B[:, 3:4], ln2B[:, 1:2], AF.Square)
                t_st2 = P.add('S', s_stats2, waits=[t_m2], inc='s_stats2')
                t_var2 = P.add('V', lambda e: e.tensor_tensor(
                    ln2B[:, 5:6], ln2B[:, 4:5], ln2B[:, 3:4], OP.subtract),
                    waits=[t_st2], inc='v_var2')

                def rstd2(e):
                    e.activation(ln2B[:, 6:7], ln2B[:, 5:6], AF.Ln, bias=cp(C_EPS))
                    e.drain()
                    return e.activation(ln2B[:, 6:7], ln2B[:, 6:7], AF.Exp, scale=cp(C_NHALF))
                t_rstd2 = P.add('S', rstd2, waits=[t_var2], inc='s_rstd2')

                t_y2b = P.add('V', lambda e, xn=xncur: e.tensor_scalar(
                    y_b[:], xn, ln2B[:, 1:2], ln2B[:, 6:7], OP.subtract, OP.mult),
                    waits=[t_rstd2, pv.get('d_ty2')], inc='v_y2b')

                t_ty2 = P.add('Y', dma_trn(y_b, hraw2b, 8),
                              waits=[t_y2b, pv.get('s_h2Blast')], inc='d_ty2B', inc_k=16)
                pv['d_ty2'] = t_ty2

                sofs = (i % 2) * D

                def h2ev(e, b=b, sofs=sofs):
                    last = None
                    for j in range(8):
                        last = e.activation(h2d[:, sofs + 128 * j:sofs + 128 * (j + 1)],
                                            hraw2b[:, 128 * j:128 * (j + 1)], AF.Identity,
                                            scale=modcol(4, j, b), bias=modcol(3, j, b))
                    return last
                t_h2 = P.add('S', h2ev, waits=[t_ty2], inc='s_h2B')
                pv['s_h2Blast'] = t_h2
                pv[f'h2_{i}'] = t_h2

            def emit_Q1(i):
                c = i % NCH
                sofs = (i % 2) * D

                def shift(e, c=c, sofs=sofs):
                    if c == 0:
                        e.memset(carry[:], 0.0)
                    e.tensor_copy(bass.AP(h2s_fm, 0, [[D, 128], [128, 8]]),
                                  bass.AP(carry, 0, [[8, 128], [1, 8]]))
                    e.tensor_copy(bass.AP(h2s_fm, 1, [[D, 128], [128, 8], [1, 127]]),
                                  bass.AP(h2d, sofs, [[2 * D, 128], [128, 8], [1, 127]]))
                    return e.tensor_copy(carry[:],
                                         bass.AP(h2d, sofs + 127, [[2 * D, 128], [128, 8]]))
                t_sh = P.add('V', shift, waits=[pv[f'h2_{i}']], inc='v_sh')

                def s_lerp_r(e):
                    last = None
                    for j in range(8):
                        last = e.activation(t2r[:, 128 * j:128 * (j + 1)],
                                            h2s_fm[:, 128 * j:128 * (j + 1)], AF.Identity,
                                            scale=omrcol[:, j:j + 1])
                    return last
                t_lpr = P.add('S', s_lerp_r, waits=[t_sh, pv.get('vf')], inc='s_lerp')

                def s_lerp_k(e):
                    last = None
                    for j in range(8):
                        last = e.activation(t2k[:, 128 * j:128 * (j + 1)],
                                            h2s_fm[:, 128 * j:128 * (j + 1)], AF.Identity,
                                            scale=omkcol[:, j:j + 1])
                    return last
                t_lpk = P.add('S', s_lerp_k, waits=[t_sh], inc='s_lerpk')
                pv['q1'] = (t_sh, t_lpr, t_lpk)

            def emit_Q2(i):
                t_sh, t_lpr, t_lpk = pv['q1']
                sofs = (i % 2) * D

                def v_xr(e, sofs=sofs):
                    last = None
                    for j in range(8):
                        last = e.scalar_tensor_tensor(
                            xr_fm[:, 128 * j:128 * (j + 1)],
                            h2d[:, sofs + 128 * j:sofs + 128 * (j + 1)],
                            murcol[:, j:j + 1], t2r[:, 128 * j:128 * (j + 1)],
                            OP.mult, OP.add)
                    return last
                t_xr = P.add('V', v_xr, waits=[t_lpr], inc='v_xr')

                def v_xk(e, sofs=sofs):
                    last = None
                    for j in range(8):
                        last = e.scalar_tensor_tensor(
                            xk_fm[:, 128 * j:128 * (j + 1)],
                            h2d[:, sofs + 128 * j:sofs + 128 * (j + 1)],
                            mukcol[:, j:j + 1], t2k[:, 128 * j:128 * (j + 1)],
                            OP.mult, OP.add)
                    return last
                t_xk = P.add('V', v_xk, waits=[t_lpk], inc='v_xk')

                t_rp = P.add('T', mm_ffn(Q1, xr_fm, wrT_s),
                             waits=[t_xr, pv.get('vf')] +
                                   ([bw['wrT'], t_wkT] if i == 0 else []),
                             inc='t_rp')
                t_kp = P.add('T', mm_ffn(Q2, xk_fm, wkT_s),
                             waits=[t_xk, pv.get('vgB'), pv.get('skr')] +
                                   ([t_wkT] if i == 0 else []),
                             inc='t_kp')
                pv['q2'] = (t_xr, t_xk, t_rp, t_kp)

            def emit_Q3(i):
                b = i // NCH
                row = 128 * i
                xncur = xnew[:, (i % 2) * D:(i % 2 + 1) * D]
                t_xr, t_xk, t_rp, t_kp = pv['q2']

                t_E2 = P.add('S', lambda e: e.activation(Ebuf[:, 0:1024], Q1[:], AF.Exp,
                                                         scale=cp(C_NEG1)),
                             waits=[t_rp, pv.get(f'vF{i + 1}')], inc='s_E2')
                pv['sE2'] = t_E2

                def rr2(e):
                    e.activation(Ebuf[:, 0:1024], Ebuf[:, 0:1024], AF.Ln, bias=cp(C_ONE))
                    return e.activation(rr_bf[:], Ebuf[:, 0:1024], AF.Exp, scale=cp(C_NEG1))
                t_rr = P.add('S', rr2, waits=[t_xk], inc='s_rr')

                t_kr = P.add('S', lambda e: e.activation(x2B[:], Q2[:], AF.Relu),
                             waits=[t_kp], inc='s_kr')
                pv['skr'] = t_kr
                t_ka = P.add('V', lambda e: e.tensor_tensor(ka_bf[:], x2B[:], x2B[:], OP.mult),
                             waits=[t_kr, pv.get('dkaT')], inc='v_ka')
                pv['vka'] = t_ka
                t_kaT = P.add('Y', dma_trn(ka_bf, kaf, 8),
                              waits=[t_ka, pv.get('tvo')], inc='d_kaT', inc_k=16)
                pv['dkaT'] = t_kaT

                t_vo = P.add('T', mm_ffn(Q1, kaf, wvT_s),
                             waits=[t_kaT, t_E2] + ([t_wvT] if i == 0 else []), inc='t_vo')
                pv['tvo'] = t_vo

                def v_f(e, b=b, xn=xncur, i=i):
                    t3 = Fbuf[:, (i % 2) * DV:(i % 2) * DV + 1024]
                    e.tensor_tensor(f_bf[:], Q1[:], rr_bf[:], OP.mult)
                    e.tensor_tensor(t3, f_bf[:], gb_rep[:, b * D:(b + 1) * D], OP.mult)
                    return e.tensor_tensor(xn, t3, xn, OP.add)
                t_f = P.add('V', v_f, waits=[t_vo, t_rr], inc='v_f')
                pv['vf'] = t_f

                pv[f'd_out{i}'] = P.add('G', lambda e, row=row, xn=xncur: e.dma_start(
                    out_d[row:row + 128, :], xn), waits=[t_f], inc='d_out', inc_k=16)

            emit_front_mm(0)
            emit_gev()
            emit_sg(0)
            emit_F(0)
            emit_P_og(0)
            emit_P_ln(0)
            for i in range(NTOT):
                nx = i + 1 < NTOT
                if nx:
                    emit_front_mm(i + 1)
                emit_Q1(i)
                if nx:
                    emit_gev()
                    emit_sg(i + 1)
                emit_Q2(i)
                if nx:
                    emit_F(i + 1)
                    emit_P_og(i + 1)
                emit_Q3(i)
                if nx:
                    emit_P_ln(i + 1)

            P.add('G', lambda e: e.nop(), waits=[pv[f'd_out{NTOT - 1}']])
            P.run_block(nc)


    return nc


def _consts():
    ib = np.eye(128, dtype=ml_dtypes.bfloat16)
    iff = np.eye(128, dtype=np.float32)
    s = np.arange(128)
    ut = (s[:, None] <= s[None, :]).astype(np.float32)
    mt = -(s[:, None] > s[None, :]).astype(np.float32)
    cm = (s[:, None] <= s[None, :]).astype(np.uint8)
    ho = np.zeros((128, 3), np.float32)
    ho[:, 0] = 1.0
    ho[:64, 1] = 1.0
    ho[64:, 2] = 1.0
    return ib, iff, ut, mt, cm, ho


def kernel(L0_attn_x, L0_attn_Wq, L0_attn_Wk, L0_attn_Wv, L0_attn_Wg,
           L0_attn_Wgk1, L0_attn_Wgk2, L0_attn_bgk2, L0_attn_Wo,
           L0_attn_g_norm_weight, L0_ffn_W_r, L0_ffn_W_k, L0_ffn_W_v,
           L0_ffn_mu_r, L0_ffn_mu_k, L0_c, L0_wmod):
    x = np.ascontiguousarray(np.asarray(L0_attn_x, dtype=np.float32))
    ib, iff, ut, mt, cm, ho = _consts()

    common = dict(
        wq=np.ascontiguousarray(L0_attn_Wq, np.float32),
        wk=np.ascontiguousarray(L0_attn_Wk, np.float32),
        wv=np.ascontiguousarray(L0_attn_Wv, np.float32),
        wg=np.ascontiguousarray(L0_attn_Wg, np.float32),
        wgk1=np.ascontiguousarray(L0_attn_Wgk1, np.float32),
        wgk2=np.ascontiguousarray(L0_attn_Wgk2, np.float32),
        bgk2=np.ascontiguousarray(L0_attn_bgk2, np.float32).reshape(1, D),
        wo=np.ascontiguousarray(L0_attn_Wo, np.float32),
        gnw=np.ascontiguousarray(L0_attn_g_norm_weight, np.float32).reshape(1, DVH),
        wr=np.ascontiguousarray(L0_ffn_W_r, np.float32),
        wk2=np.ascontiguousarray(L0_ffn_W_k, np.float32),
        wv2=np.ascontiguousarray(L0_ffn_W_v, np.float32),
        mur=np.ascontiguousarray(L0_ffn_mu_r, np.float32).reshape(8, 128),
        muk=np.ascontiguousarray(L0_ffn_mu_k, np.float32).reshape(8, 128),
        wmod=np.ascontiguousarray(L0_wmod, np.float32),
        identb=ib, identf=iff, utri=ut, mtri=mt, cmask=cm, hones=ho,
    )
    cmod = np.ascontiguousarray(L0_c, np.float32)

    nc = build_nc(debug=DEBUG)
    in_maps = []
    for core in range(NCORE):
        m = dict(common)
        m["xin"] = np.ascontiguousarray(x[2 * core:2 * core + 2].reshape(TOK, D))
        m["cmod"] = np.ascontiguousarray(cmod[2 * core:2 * core + 2])
        in_maps.append(m)

    res = run_bass_kernel_spmd(nc, in_maps, core_ids=list(range(NCORE)))
    outs = [np.asarray(r["out"]).reshape(NSEQ, N, D) for r in res.results]
    return np.concatenate(outs, axis=0)


